# version 04
# baseline (speedup 1.0000x reference)
import contextlib
import numpy as np
import concourse.bass as bass
import concourse.mybir as mybir
from concourse.bass_utils import run_bass_kernel_spmd

F32 = mybir.dt.float32
BF16 = mybir.dt.bfloat16
AF = mybir.ActivationFunctionType
ALU = mybir.AluOpType
AX = mybir.AxisListType

B, T, D, DI, S, RNK, H, DH, FF = 4, 4096, 1024, 2048, 16, 64, 16, 64, 4096
TL = 2048
NCORE = 8
LN_EPS = 1e-5


class Sem:
    __slots__ = ("h", "n")

    def __init__(self, h):
        self.h = h
        self.n = 0


DEBUG_SELF_EDGES = False


class Plan:
    ENGINES = ("sync", "vector", "scalar", "tensor", "gpsimd")

    def __init__(self, nc, ctx):
        self.nc = nc
        self.ctx = ctx
        self.ops = {e: [] for e in self.ENGINES}
        self.esem = {}
        self._sems = {}
        self._pending = {}

    def sem(self, name):
        if name not in self._sems:
            self._sems[name] = Sem(self.ctx.enter_context(self.nc.semaphore(name)))
        return self._sems[name]

    def barrier(self):
        snap = [(s, s.n) for s in self._sems.values()]
        for e in self.ENGINES:
            self._pending[e] = snap

    def on(self, eng, fn):
        self.ops[eng].append(fn)

    def emit(self, eng, builder, waits=(), inc=None):
        pend = self._pending.pop(eng, None)
        w = [(s.h, v) for (s, v) in pend] if pend else []
        w += [(s.h, v) for (s, v) in waits]
        if DEBUG_SELF_EDGES:
            li = self.esem.get(eng)
            if li is not None:
                w.append((li[0].h, li[1]))
        if inc is not None:
            inc[0].n += inc[1]
            tgt = (inc[0].h, inc[1])
        else:
            tgt = None

        if DEBUG_SELF_EDGES and inc is not None:
            self.esem[eng] = (inc[0], inc[0].n)

        def fn(e, w=w, tgt=tgt, builder=builder):
            for (sh, v) in w:
                e.wait_ge(sh, v)
            inst = builder(e)
            if tgt is not None:
                assert inst is not None
                inst.then_inc(*tgt)

        self.on(eng, fn)

    def dma(self, eng, out, in_, sem, waits=()):
        self.emit(eng, lambda e, o=out, i=in_: e.dma_start(out=o, in_=i),
                  waits=waits, inc=(sem, 16))

    def barrier_waits(self, *sems):
        return [(s, s.n) for s in sems]

    def run(self):
        block = self.ctx.enter_context(self.nc.Block())
        for e in self.ENGINES:
            fns = self.ops[e]
            if fns:
                getattr(block, e)(lambda engine, _f=tuple(fns): [f(engine) for f in _f])


def bc0(ap2d, reps, inner=None):
    p = list(ap2d.ap[0])
    f = list(ap2d.ap[1])
    if inner is None:
        inner = f[1]
    return bass.AP(tensor=ap2d.tensor, offset=ap2d.offset,
                   ap=[p, [0, reps], [f[0], inner]])


def build_front():
    nc = bass.Bass("TRN2", target_bir_lowering=False, debug=False)
    xin = nc.dram_tensor("xin", [TL, D], F32, kind="ExternalInput").ap()
    xhalo = nc.dram_tensor("xhalo", [128, D], F32, kind="ExternalInput").ap()
    w_in = nc.dram_tensor("w_in", [D, 2 * DI], F32, kind="ExternalInput").ap()
    convw = nc.dram_tensor("convw", [128, (DI // 128) * 4], F32, kind="ExternalInput").ap()
    convb = nc.dram_tensor("convb", [128, DI // 128], F32, kind="ExternalInput").ap()
    wx = nc.dram_tensor("wx", [DI, 96], F32, kind="ExternalInput").ap()
    ident_in = nc.dram_tensor("ident", [128, 128], F32, kind="ExternalInput").ap()

    xconv_o = nc.dram_tensor("xconv_o", [DI, TL], BF16, kind="ExternalOutput").ap()
    zs_o = nc.dram_tensor("zs_o", [DI, TL], BF16, kind="ExternalOutput").ap()
    xdbl_o = nc.dram_tensor("xdbl_o", [96, TL], F32, kind="ExternalOutput").ap()

    NT = TL // 128
    ND = D // 128
    NC_ = DI // 128
    CH = 4
    CW = TL // CH
    UTW = TL + 128

    ctx = contextlib.ExitStack()
    with ctx:
        sb = lambda name, shape, dt: ctx.enter_context(nc.sbuf_tensor(name, shape, dt))
        ps = lambda name, shape, dt=F32: ctx.enter_context(nc.psum_tensor(name, shape, dt))

        identf = sb("identf", [128, 128], F32)
        identb = sb("identb", [128, 128], BF16)
        mcol = sb("mcol", [128, 17], F32)
        rcol = sb("rcol", [128, 17], F32)
        scol = sb("scol", [128, 17], F32)
        ub = sb("ub", [128, NT * D], BF16)
        uhb = sb("uhb", [128, D], BF16)
        uT = sb("uT", [128, ND * UTW], BF16)
        winb = sb("winb", [128, ND * DI], BF16)
        wxb = sb("wxb", [128, NC_ * 96], BF16)
        cwb = sb("cwb", [128, NC_ * 4], F32)
        cbb = sb("cbb", [128, NC_], F32)
        cbuf = sb("cbuf", [128, 8 + CW], F32)
        cva = sb("cva", [128, CW], F32)
        cvb = sb("cvb", [128, CW], F32)
        halo3 = sb("halo3", [128, NC_ * 4], F32)
        zst = sb("zst", [128, TL], BF16)
        xdbl_sb = sb("xdbl_sb", [96, TL], F32)
        g_xi = nc.sbuf_tensor("xi", [128, NT * D], F32, side="right")
        xi = g_xi.__enter__()
        g_xh = nc.sbuf_tensor("xh", [128, D], F32, side="right")
        xh = g_xh.__enter__()
        g_sqs = nc.sbuf_tensor("sqs", [128, 2 * D], F32, side="right")
        sqs = g_sqs.__enter__()

        ptb = [ctx.enter_context(nc.psum_tensor(f"ptb{i}", [128, 128], BF16)) for i in range(2)]
        pxz = [ps(f"pxz{i}", [128, CW]) for i in range(2)]
        pxh = ps("pxh", [128, 128])
        pxd = [ps(f"pxd{i}", [96, CW]) for i in range(2)]

        plan = Plan(nc, ctx)
        L = plan.sem("L")
        Lg = plan.sem("Lg")
        V = plan.sem("V")
        Cs = plan.sem("C")
        Ts = plan.sem("T")
        St = plan.sem("St")

        plan.dma("sync", xi[:].rearrange("p (n d) -> p n d", n=NT),
                 xin.rearrange("(n p) d -> p n d", p=128), L)
        plan.dma("sync", xh[:], xhalo, L)
        plan.dma("sync", identf[:], ident_in, L)
        plan.dma("gpsimd", winb[:].rearrange("p (k f) -> p k f", k=ND),
                 w_in[:, 0:DI].rearrange("(k p) f -> p k f", p=128), Lg)
        plan.dma("gpsimd", wxb[:].rearrange("p (k f) -> p k f", k=NC_),
                 wx.rearrange("(k p) f -> p k f", p=128), Lg)
        plan.dma("gpsimd", cwb[:], convw, Lg)
        plan.dma("gpsimd", cbb[:], convb, Lg)
        lds = L.n
        lgs = Lg.n
        for _eng in ("vector", "scalar", "tensor"):
            plan.emit(_eng, lambda e: None, waits=[(Lg, lgs)])

        plan.emit("vector", lambda e: e.tensor_copy(identb[:], identf[:]),
                  waits=[(L, lds)], inc=(V, 1))

        for tt in range(NT + 1):
            src = xi[:, tt * D:(tt + 1) * D] if tt < NT else xh[:]
            plan.emit("scalar",
                      lambda e, s=src, o=sqs[:, (tt % 2) * D:(tt % 2 + 1) * D]:
                      e.activation(o, s, AF.Square),
                      waits=[(L, lds), (V, V.n)], inc=(Cs, 1))
            plan.emit("vector",
                      lambda e, s=src, o=mcol[:, tt:tt + 1]:
                      e.tensor_reduce(o, s, AX.X, ALU.add),
                      waits=[(L, lds)], inc=(V, 1))
            plan.emit("vector",
                      lambda e, s=sqs[:, (tt % 2) * D:(tt % 2 + 1) * D], o=scol[:, tt:tt + 1]:
                      e.tensor_reduce(o, s, AX.X, ALU.add),
                      waits=[(Cs, Cs.n)], inc=(V, 1))
        plan.emit("vector", lambda e: e.tensor_scalar(mcol[:, 0:17], mcol[:, 0:17],
                                                      1.0 / D, None, ALU.mult),
                  waits=[(V, V.n)], inc=(V, 1))
        plan.emit("vector", lambda e: e.tensor_scalar(scol[:, 0:17], scol[:, 0:17],
                                                      1.0 / D, None, ALU.mult),
                  waits=[(V, V.n)], inc=(V, 1))
        plan.emit("vector", lambda e: e.scalar_tensor_tensor(
            rcol[:, 0:17], mcol[:, 0:17], -1.0, mcol[:, 0:17], ALU.mult, ALU.mult),
            waits=[(V, V.n)], inc=(V, 1))
        plan.emit("vector", lambda e: e.tensor_tensor(scol[:, 0:17], scol[:, 0:17],
                                                      rcol[:, 0:17], ALU.add),
                  waits=[(V, V.n)], inc=(V, 1))
        plan.emit("vector", lambda e: e.tensor_scalar(scol[:, 0:17], scol[:, 0:17],
                                                      LN_EPS, None, ALU.add),
                  waits=[(V, V.n)], inc=(V, 1))
        vstat = V.n
        plan.emit("scalar", lambda e: e.activation(rcol[:, 0:17], scol[:, 0:17],
                                                   AF.Sqrt),
                  waits=[(V, vstat)], inc=(Cs, 1))
        plan.emit("vector", lambda e: e.reciprocal(rcol[:, 0:17], rcol[:, 0:17]),
                  waits=[(Cs, Cs.n)], inc=(V, 1))
        for tt in range(NT + 1):
            src = xi[:, tt * D:(tt + 1) * D] if tt < NT else xh[:]
            dst = ub[:, tt * D:(tt + 1) * D] if tt < NT else uhb[:]
            plan.emit("vector",
                      lambda e, s=src, o=dst, t=tt: e.tensor_scalar(
                          o, s, mcol[:, t:t + 1], rcol[:, t:t + 1], ALU.subtract, ALU.mult),
                      inc=(V, 1))
        v_u = V.n

        n_tp = 0
        for dt in range(ND):
            for tt in range(NT + 1):
                src = (ub[:, tt * D + dt * 128: tt * D + dt * 128 + 128] if tt < NT
                       else uhb[:, dt * 128:dt * 128 + 128])
                dstcol = dt * UTW + (128 + tt * 128 if tt < NT else 0)
                pb = ptb[n_tp % 2]
                waits = [(V, v_u)]
                if n_tp >= 2:
                    waits.append((Cs, Cs.n))
                plan.emit("tensor", lambda e, s=src, p=pb: e.transpose(p[:], s, identb[:]),
                          waits=waits, inc=(Ts, 1))
                plan.emit("scalar", lambda e, p=pb, dc=dstcol: e.copy(uT[:, dc:dc + 128], p[:]),
                          waits=[(Ts, Ts.n)], inc=(Cs, 1))
                n_tp += 1
        v_uT_T, v_uT_C = Ts.n, Cs.n
        g_sqs.__exit__(None, None, None)
        g_xh.__exit__(None, None, None)
        g_xi.__exit__(None, None, None)
        xcv = sb("xcv", [128, NC_ * TL], BF16)

        for ct in range(NC_):
            for kt in range(ND):
                plan.emit("tensor", lambda e, ct=ct, kt=kt: e.matmul(
                    pxh[:], winb[:, kt * DI + ct * 128: kt * DI + ct * 128 + 128],
                    uT[:, kt * UTW: kt * UTW + 128],
                    start=(kt == 0), stop=(kt == ND - 1)),
                    waits=([(Cs, v_uT_C), (V, V.n)] if kt == 0 else []),
                    inc=((Ts, 1) if kt == ND - 1 else None))
            plan.emit("vector", lambda e, ct=ct: e.tensor_copy(
                halo3[:, ct * 4: ct * 4 + 3], pxh[:, 125:128]),
                waits=[(Ts, Ts.n)], inc=(V, 1))
        v_halo = V.n

        nxz = 0
        for ct in range(NC_):
            for chk in range(CH):
                pz = pxz[nxz % 2]
                for kt in range(ND):
                    waits = []
                    if kt == 0:
                        waits = [(V, V.n), (Cs, Cs.n)]
                    plan.emit("tensor", lambda e, ct=ct, kt=kt, chk=chk, pz=pz: e.matmul(
                        pz[:], winb[:, kt * DI + ct * 128: kt * DI + ct * 128 + 128],
                        uT[:, kt * UTW + 128 + chk * CW: kt * UTW + 128 + (chk + 1) * CW],
                        start=(kt == 0), stop=(kt == ND - 1)),
                        waits=waits, inc=((Ts, 1) if kt == ND - 1 else None))
                t_mm = Ts.n
                if chk == 0:
                    plan.emit("vector", lambda e, ct=ct: e.tensor_copy(
                        cbuf[:, 5:8], halo3[:, ct * 4:ct * 4 + 3]),
                        waits=[(V, v_halo)], inc=(V, 1))
                plan.emit("vector", lambda e, pz=pz: e.tensor_copy(cbuf[:, 8:8 + CW], pz[:]),
                          waits=[(Ts, t_mm), (Cs, Cs.n)], inc=(V, 1))
                plan.emit("vector", lambda e, ct=ct: e.tensor_scalar(
                    cva[:], cbuf[:, 5:5 + CW], cwb[:, ct * 4:ct * 4 + 1], None, ALU.mult),
                    inc=(V, 1))
                plan.emit("vector", lambda e, ct=ct: e.scalar_tensor_tensor(
                    cvb[:], cbuf[:, 6:6 + CW], cwb[:, ct * 4 + 1:ct * 4 + 2], cva[:],
                    ALU.mult, ALU.add), inc=(V, 1))
                plan.emit("vector", lambda e, ct=ct: e.scalar_tensor_tensor(
                    cva[:], cbuf[:, 7:7 + CW], cwb[:, ct * 4 + 2:ct * 4 + 3], cvb[:],
                    ALU.mult, ALU.add), inc=(V, 1))
                plan.emit("vector", lambda e, ct=ct: e.scalar_tensor_tensor(
                    cvb[:], cbuf[:, 8:8 + CW], cwb[:, ct * 4 + 3:ct * 4 + 4], cva[:],
                    ALU.mult, ALU.add), inc=(V, 1))
                plan.emit("vector", lambda e: e.tensor_copy(cbuf[:, 5:8],
                                                            cbuf[:, 5 + CW:8 + CW]),
                          inc=(V, 1))
                plan.emit("scalar", lambda e, ct=ct, chk=chk: e.activation(
                    xcv[:, ct * TL + chk * CW: ct * TL + (chk + 1) * CW], cvb[:],
                    AF.Silu, bias=cbb[:, ct:ct + 1]),
                    waits=[(V, V.n)], inc=(Cs, 1))
                nxz += 1
            plan.dma("gpsimd", xconv_o[ct * 128:(ct + 1) * 128, :],
                     xcv[:, ct * TL:(ct + 1) * TL], St, waits=[(Cs, Cs.n)])
        plan.dma("gpsimd", winb[:].rearrange("p (k f) -> p k f", k=ND),
                 w_in[:, DI:2 * DI].rearrange("(k p) f -> p k f", p=128), Lg,
                 waits=[(Ts, Ts.n)])
        l_z = Lg.n
        for ct in range(NC_):
            for chk in range(CH):
                pz = pxz[nxz % 2]
                for kt in range(ND):
                    waits = [] if kt else [(Cs, Cs.n), (V, V.n), (Lg, l_z)]
                    plan.emit("tensor", lambda e, ct=ct, kt=kt, chk=chk, pz=pz: e.matmul(
                        pz[:], winb[:, kt * DI + ct * 128: kt * DI + ct * 128 + 128],
                        uT[:, kt * UTW + 128 + chk * CW: kt * UTW + 128 + (chk + 1) * CW],
                        start=(kt == 0), stop=(kt == ND - 1)),
                        waits=waits, inc=((Ts, 1) if kt == ND - 1 else None))
                zw = [(Ts, Ts.n)] + ([(St, St.n)] if chk == 0 else [])
                plan.emit("scalar", lambda e, pz=pz, chk=chk: e.activation(
                    zst[:, chk * CW:(chk + 1) * CW], pz[:], AF.Silu),
                    waits=zw, inc=(Cs, 1))
                nxz += 1
            plan.dma("gpsimd", zs_o[ct * 128:(ct + 1) * 128, :],
                     zst[:], St, waits=[(Cs, Cs.n)])
        for chk in range(CH):
            pd = pxd[chk % 2]
            for kt in range(NC_):
                waits = []
                if kt == 0:
                    waits = [(Cs, Cs.n), (V, V.n)]
                plan.emit("tensor", lambda e, kt=kt, chk=chk, pd=pd: e.matmul(
                    pd[:], wxb[:, kt * 96:(kt + 1) * 96],
                    xcv[:, kt * TL + chk * CW: kt * TL + (chk + 1) * CW],
                    start=(kt == 0), stop=(kt == NC_ - 1)),
                    waits=waits, inc=((Ts, 1) if kt == NC_ - 1 else None))
            plan.emit("vector", lambda e, pd=pd, chk=chk: e.tensor_copy(
                xdbl_sb[:, chk * CW:(chk + 1) * CW], pd[:]),
                waits=[(Ts, Ts.n)], inc=(V, 1))
        plan.dma("gpsimd", xdbl_o, xdbl_sb[:], St, waits=[(V, V.n)])

        plan.emit("gpsimd", lambda e: e.engine_nop(), waits=[(St, St.n)])
        plan.run()
    return nc


def host_front_inputs(inp):
    x = np.ascontiguousarray(inp["L0_attn_x"])
    ident = np.eye(128, dtype=np.float32)
    convw = np.ascontiguousarray(
        inp["L0_attn_conv_w"].reshape(DI // 128, 128, 4).transpose(1, 0, 2).reshape(128, -1))
    convb_l = np.ascontiguousarray(
        inp["L0_attn_conv_b"].reshape(DI // 128, 128).T)
    maps = []
    for c in range(NCORE):
        b, m = c // 2, c % 2
        t0 = m * TL
        halo = np.zeros((128, D), np.float32)
        if m == 1:
            halo[125:128] = x[b, t0 - 3:t0]
        maps.append({
            "xin": np.ascontiguousarray(x[b, t0:t0 + TL]),
            "xhalo": halo,
            "w_in": inp["L0_attn_W_in"],
            "convw": convw,
            "convb": convb_l,
            "wx": inp["L0_attn_W_x"],
            "ident": ident,
        })
    return maps


def build_scan(nch_run=None):
    nc = bass.Bass("TRN2", target_bir_lowering=False, debug=False)
    CL = 1024
    NCT = CL // 128
    TC = 512
    NCH = T // TC
    if nch_run is None:
        nch_run = NCH

    xconv_c = nc.dram_tensor("xconv_c", [CL, T], BF16, kind="ExternalInput").ap()
    xdbl = nc.dram_tensor("xdbl", [64, T], F32, kind="ExternalInput").ap()
    xdbl_bc = nc.dram_tensor("xdbl_bc", [T // 512, 32 * 512], F32, kind="ExternalInput").ap()
    wdt = nc.dram_tensor("wdt", [64, CL], F32, kind="ExternalInput").ap()
    bdt = nc.dram_tensor("bdt", [128, NCT], F32, kind="ExternalInput").ap()
    alog = nc.dram_tensor("alog", [128, NCT * S], F32, kind="ExternalInput").ap()
    dpar = nc.dram_tensor("dpar", [128, NCT], F32, kind="ExternalInput").ap()
    ident_in = nc.dram_tensor("ident", [128, 128], F32, kind="ExternalInput").ap()
    ybar_o = nc.dram_tensor("ybar_o", [CL, T], BF16, kind="ExternalOutput").ap()

    ctx = contextlib.ExitStack()
    with ctx:
        sb = lambda name, shape, dt: ctx.enter_context(nc.sbuf_tensor(name, shape, dt))
        ps = lambda name, shape, dt=F32: ctx.enter_context(nc.psum_tensor(name, shape, dt))

        xcb = sb("xcb", [128, NCT * T], BF16)
        xdb = sb("xdb", [64, T], F32)
        bcrow = sb("bcrow", [1, 32 * TC], BF16)
        wdtb = sb("wdtb", [64, CL], F32)
        bdtb = sb("bdtb", [128, NCT], F32)
        acoef = sb("acoef", [128, NCT * S], F32)
        dparb = sb("dparb", [128, NCT], F32)
        identf = sb("identf", [128, 128], F32)
        identb = sb("identb", [128, 128], BF16)
        ones1b = sb("ones1b", [1, 128], BF16)
        dtc = sb("dtc", [128, NCT * TC], F32)
        vc = sb("vc", [128, NCT * TC], F32)
        brep = sb("brep", [128, S * TC], BF16)
        crep = sb("crep", [128, S * TC], BF16)
        bt = [sb(f"bt{i}", [128, TC], F32) for i in range(2)]
        dA = [sb(f"dA{i}", [128, TC], F32) for i in range(2)]
        hbuf = sb("hbuf", [128, TC], F32)
        hc = [sb(f"hc{i}", [128, TC], BF16) for i in range(2)]
        hcar = sb("hcar", [128, NCT * S], F32)
        ybst = [sb(f"ybst{i}", [128, TC], BF16) for i in range(2)]

        pdt = [ps(f"pdt{i}", [128, TC]) for i in range(2)]
        pbc = [ps(f"pbc{i}", [128, TC]) for i in range(2)]
        py = [ps(f"py{i}", [128, TC]) for i in range(2)]

        plan = Plan(nc, ctx)
        L = plan.sem("L")
        Lg = plan.sem("Lg")
        V = plan.sem("V")
        Cs = plan.sem("C")
        Ts = plan.sem("T")
        G = plan.sem("G")
        St = plan.sem("St")

        plan.dma("sync", xcb[:].rearrange("p (k t) -> p k t", k=NCT),
                 xconv_c.rearrange("(k p) t -> p k t", p=128), L)
        plan.dma("sync", xdb[:], xdbl, L)
        plan.dma("sync", identf[:], ident_in, L)
        plan.dma("sync", bdtb[:], bdt, L)
        plan.dma("sync", acoef[:], alog, L)
        plan.dma("sync", dparb[:], dpar, L)
        plan.dma("sync", wdtb[:], wdt, L)
        lds = L.n

        plan.emit("vector", lambda e: e.memset(ones1b[:], 1.0), inc=(V, 1))
        plan.emit("vector", lambda e: e.memset(hcar[:], 0.0), inc=(V, 1))
        plan.emit("vector", lambda e: e.tensor_copy(identb[:], identf[:]),
                  waits=[(L, lds)], inc=(V, 1))
        plan.emit("scalar", lambda e: e.activation(acoef[:], acoef[:], AF.Exp),
                  waits=[(L, lds)], inc=(Cs, 1))
        plan.emit("vector", lambda e: e.tensor_scalar(acoef[:], acoef[:], -1.0,
                                                      None, ALU.mult),
                  waits=[(Cs, Cs.n)], inc=(V, 1))

        it = 0
        ctg = 0
        vscan = {}
        c_dA = {}
        g_b = {}
        v_hc = {}
        t_y = {}
        v_fin = {}
        st_dma = {}
        for ch in range(nch_run):
            c0 = ch * TC
            for ct in range(NCT):
                pd = pdt[ct % 2]
                plan.emit("tensor", lambda e, ct=ct, pd=pd, c0=c0: e.matmul(
                    pd[:], wdtb[:, ct * 128:(ct + 1) * 128], xdb[0:64, c0:c0 + TC],
                    start=True, stop=True),
                    waits=[(L, lds), (Cs, Cs.n)], inc=(Ts, 1))
                plan.emit("scalar", lambda e, ct=ct, pd=pd: e.activation(
                    dtc[:, ct * TC:(ct + 1) * TC], pd[:], AF.Sigmoid,
                    bias=bdtb[:, ct:ct + 1]),
                    waits=[(Ts, Ts.n), (V, V.n)], inc=(Cs, 1))
                plan.emit("vector", lambda e, ct=ct: e.tensor_scalar(
                    dtc[:, ct * TC:(ct + 1) * TC], dtc[:, ct * TC:(ct + 1) * TC],
                    0.099, 0.001, ALU.mult, ALU.add),
                    waits=[(Cs, Cs.n), (G, G.n)], inc=(V, 1))
                plan.emit("vector", lambda e, ct=ct, c0=c0: e.tensor_tensor(
                    vc[:, ct * TC:(ct + 1) * TC], dtc[:, ct * TC:(ct + 1) * TC],
                    xcb[:, ct * T + c0: ct * T + c0 + TC], ALU.mult),
                    inc=(V, 1))
            v_dtv = V.n
            plan.dma("gpsimd", bcrow[:], xdbl_bc[ch:ch + 1, :], Lg,
                     waits=[(Ts, Ts.n)])
            l_bc = Lg.n
            for s in range(2 * S):
                pb = pbc[s % 2]
                plan.emit("tensor", lambda e, s=s, pb=pb: e.matmul(
                    pb[:], ones1b[:], bcrow[0:1, s * TC:(s + 1) * TC],
                    start=True, stop=True),
                    waits=[(V, V.n), (Cs, Cs.n), (G, G.n), (Lg, l_bc)], inc=(Ts, 1))
                if s < S:
                    plan.emit("vector", lambda e, s=s, pb=pb: e.tensor_copy(
                        brep[:, s * TC:(s + 1) * TC], pb[:]),
                        waits=[(Ts, Ts.n)], inc=(V, 1))
                else:
                    plan.emit("scalar", lambda e, s=s, pb=pb: e.copy(
                        crep[:, (s - S) * TC:(s - S + 1) * TC], pb[:]),
                        waits=[(Ts, Ts.n), (V, V.n)], inc=(Cs, 1))
            v_bc = V.n
            c_bc = Cs.n
            for ct in range(NCT):
                pyt = py[ct % 2]
                for s in range(S):
                    r = it % 2
                    wv = v_dtv
                    if it - 2 in vscan:
                        wv = max(wv, vscan[it - 2])
                    plan.emit("scalar", lambda e, ct=ct, s=s, r=r: e.activation(
                        dA[r][:], dtc[:, ct * TC:(ct + 1) * TC], AF.Exp,
                        scale=acoef[:, ct * S + s:ct * S + s + 1]),
                        waits=[(V, wv)], inc=(Cs, 1))
                    c_dA[it] = Cs.n
                    wv = max(v_dtv, v_bc)
                    if it - 2 in vscan:
                        wv = max(wv, vscan[it - 2])
                    plan.emit("gpsimd", lambda e, ct=ct, s=s, r=r: e.tensor_tensor(
                        bt[r][:], vc[:, ct * TC:(ct + 1) * TC],
                        brep[:, s * TC:(s + 1) * TC], ALU.mult),
                        waits=[(V, wv)], inc=(G, 1))
                    g_b[it] = G.n
                    plan.emit("vector", lambda e, ct=ct, s=s, r=r: e.tensor_tensor_scan(
                        hbuf[:], dA[r][:], bt[r][:],
                        hcar[:, ct * S + s:ct * S + s + 1], ALU.mult, ALU.add),
                        waits=[(Cs, c_dA[it]), (G, g_b[it])], inc=(V, 1))
                    vscan[it] = V.n
                    wts = t_y.get(it - 2, 0)
                    plan.emit("vector", lambda e, s=s, r=r: e.tensor_tensor(
                        hc[r][:], hbuf[:], crep[:, s * TC:(s + 1) * TC], ALU.mult),
                        waits=[(Ts, wts)], inc=(V, 1))
                    v_hc[it] = V.n
                    plan.emit("scalar", lambda e, ct=ct, s=s: e.copy(
                        hcar[:, ct * S + s:ct * S + s + 1], hbuf[:, TC - 1:TC]),
                        waits=[(V, v_hc[it])], inc=(Cs, 1))
                    wv = v_hc[it]
                    if s == 0 and ctg - 2 in v_fin:
                        wv = max(wv, v_fin[ctg - 2])
                    plan.emit("tensor", lambda e, s=s, r=r, pyt=pyt: e.matmul(
                        pyt[:], identb[:], hc[r][:],
                        start=(s == 0), stop=(s == S - 1)),
                        waits=[(V, wv)], inc=(Ts, 1))
                    t_y[it] = Ts.n
                    it += 1
                wst = st_dma.get(ctg - 2, 0)
                plan.emit("vector", lambda e, ct=ct, c0=c0, pyt=pyt, rg=ctg % 2:
                          e.scalar_tensor_tensor(
                              ybst[rg][:], xcb[:, ct * T + c0: ct * T + c0 + TC],
                              dparb[:, ct:ct + 1], pyt[:], ALU.mult, ALU.add),
                          waits=[(Ts, t_y[it - 1]), (St, wst)], inc=(V, 1))
                v_fin[ctg] = V.n
                plan.dma("gpsimd", ybar_o[ct * 128:(ct + 1) * 128, c0:c0 + TC],
                         ybst[ctg % 2][:], St, waits=[(V, v_fin[ctg])])
                st_dma[ctg] = St.n
                ctg += 1
        plan.emit("gpsimd", lambda e: e.engine_nop(), waits=[(St, St.n)])
        plan.run()
    return nc


def host_scan_inputs(inp, front_results):
    ident = np.eye(128, dtype=np.float32)
    NCT = 8
    maps = []
    for c in range(NCORE):
        b, m = c // 2, c % 2
        ch0 = m * 1024
        xc = np.concatenate([front_results[2 * b]["xconv_o"],
                             front_results[2 * b + 1]["xconv_o"]], axis=1)
        xd = np.concatenate([front_results[2 * b]["xdbl_o"],
                             front_results[2 * b + 1]["xdbl_o"]], axis=1)
        lay = lambda v: np.ascontiguousarray(
            v.reshape(NCT, 128, -1).transpose(1, 0, 2).reshape(128, -1))
        maps.append({
            "xconv_c": np.ascontiguousarray(xc[ch0:ch0 + 1024]),
            "xdbl": np.ascontiguousarray(xd[0:64]),
            "xdbl_bc": np.ascontiguousarray(
                xd[64:96].T.reshape(T // 512, 512, 32).transpose(0, 2, 1).reshape(T // 512, -1)),
            "wdt": np.ascontiguousarray(inp["L0_attn_W_dt"][:, ch0:ch0 + 1024]),
            "bdt": lay(inp["L0_attn_b_dt"][ch0:ch0 + 1024]),
            "alog": lay(inp["L0_attn_A_log"][ch0:ch0 + 1024]),
            "dpar": lay(inp["L0_attn_D_param"][ch0:ch0 + 1024]),
            "ident": ident,
        })
    return maps


def build_back(upto="x2", F=None):
    nc = bass.Bass("TRN2", target_bir_lowering=False, debug=False) if F is None else F["nc"]
    NT = TL // 128
    ND = D // 128
    NC_ = DI // 128
    NF = FF // 128
    TCH = 512
    NCH = TL // TCH
    NTT = TCH // 128

    if F is None:
        x_in = nc.dram_tensor("x_in", [TL, D], F32, kind="ExternalInput").ap()
        ybp_in = nc.dram_tensor("ybp_in", [DI, TL], BF16, kind="ExternalInput").ap()
        zs_in = nc.dram_tensor("zs_in", [DI, TL], BF16, kind="ExternalInput").ap()
        wout_in = nc.dram_tensor("wout_in", [DI, D], F32, kind="ExternalInput").ap()
        w1_in = nc.dram_tensor("w1_in", [D, FF], F32, kind="ExternalInput").ap()
        b1_in = nc.dram_tensor("b1_in", [128, NF], F32, kind="ExternalInput").ap()
        w2_in = nc.dram_tensor("w2_in", [FF, D], F32, kind="ExternalInput").ap()
        b2_in = nc.dram_tensor("b2_in", [1, D], F32, kind="ExternalInput").ap()
        gam_in = nc.dram_tensor("gam_in", [1, D], F32, kind="ExternalInput").ap()
        bet_in = nc.dram_tensor("bet_in", [1, D], F32, kind="ExternalInput").ap()
        ident_in = nc.dram_tensor("ident", [128, 128], F32, kind="ExternalInput").ap()
        x2_o = nc.dram_tensor("x2_o", [TL, D], F32, kind="ExternalOutput").ap()
    else:
        io = F["io"]
        x_in, ybp_in, zs_in, wout_in = io["x_in"], io["ybp_in"], io["zs_in"], io["wout_in"]
        w1_in, b1_in, w2_in, b2_in = io["w1_in"], io["b1_in"], io["w2_in"], io["b2_in"]
        gam_in, bet_in, ident_in, x2_o = io["gam_in"], io["bet_in"], io["ident"], io["x2_o"]
    h_o = (nc.dram_tensor("h_o", [FF, 512], BF16, kind="ExternalOutput").ap()
           if upto == "h" else None)
    pfx = "" if F is None else F["pfx"]

    ctx = contextlib.ExitStack()
    with ctx:
        sb = lambda name, shape, dt: ctx.enter_context(nc.sbuf_tensor(pfx + name, shape, dt))
        ps = lambda name, shape, dt=F32: ctx.enter_context(nc.psum_tensor(pfx + name, shape, dt))

        xtd = sb("xtd", [128, NT * D], F32)
        identb = sb("identb", [128, 128], BF16)
        ones1 = sb("ones1", [1, 128], F32)
        b2rep = sb("b2rep", [128, D], F32)
        gamrep = sb("gamrep", [128, D], F32)
        betrep = sb("betrep", [128, D], F32)
        sqs = sb("sqs", [128, D], F32)
        mcol = sb("mcol", [128, NT], F32)
        rcol = sb("rcol", [128, NT], F32)
        scol = sb("scol", [128, NT], F32)
        dscr = sb("dscr", [128, D], F32)
        b1b = sb("b1b", [128, NF], F32)
        rowt = sb("rowt", [1, D], F32)
        g_yb = nc.sbuf_tensor(pfx + "ybar", [128, NC_ * TL], BF16, side="right")
        ybar = g_yb.__enter__()
        g_wo = nc.sbuf_tensor(pfx + "woutb", [128, NC_ * D], BF16, side="right")
        woutb = g_wo.__enter__()
        g_zb = nc.sbuf_tensor(pfx + "zbuf", [128, TL], BF16, side="right")
        zbuf = g_zb.__enter__()
        g_mx = nc.sbuf_tensor(pfx + "mixc", [128, 4 * D], F32, side="right")
        mixc = g_mx.__enter__()
        g_idf = nc.sbuf_tensor(pfx + "identf", [128, 128], F32, side="right")
        identf = g_idf.__enter__()

        pmm = [ps(f"pmm{i}", [128, 512]) for i in range(4)]
        pff = [ps(f"pff{i}", [128, 512]) for i in range(2)]
        ptb = [ctx.enter_context(nc.psum_tensor(pfx + f"ptb{i}", [128, 128], BF16)) for i in range(2)]

        plan = Plan(nc, ctx) if F is None else F["plan"]
        L = plan.sem("L")
        Lg = plan.sem("Lg")
        V = plan.sem("V")
        Cs = plan.sem("C")
        Ts = plan.sem("T")
        St = plan.sem("St")

        plan.dma("sync", xtd[:].rearrange("p (n d) -> p n d", n=NT),
                 x_in.rearrange("(n p) d -> p n d", p=128), L)
        plan.dma("sync", identf[:], ident_in, L)
        plan.dma("sync", ybar[:].rearrange("p (k t) -> p k t", k=NC_),
                 ybp_in.rearrange("(k p) t -> p k t", p=128), L)
        plan.dma("gpsimd", woutb[:].rearrange("p (k d) -> p k d", k=NC_),
                 wout_in.rearrange("(k p) d -> p k d", p=128), Lg)
        lg0 = Lg.n
        for _eng in ("vector", "scalar", "tensor"):
            plan.emit(_eng, lambda e: None, waits=[(Lg, lg0)])
        plan.dma("sync", b1b[:], b1_in, L)
        lds0 = L.n

        plan.emit("vector", lambda e: e.memset(ones1[:], 1.0), inc=(V, 1))
        plan.emit("vector", lambda e: e.tensor_copy(identb[:], identf[:]),
                  waits=[(L, lds0)], inc=(V, 1))
        for ri, (row_in, dst) in enumerate(((b2_in, b2rep), (gam_in, gamrep),
                                            (bet_in, betrep))):
            plan.dma("sync", rowt[:], row_in, L, waits=[(Ts, Ts.n)])
            lr = L.n
            for dc in range(2):
                plan.emit("tensor", lambda e, dc=dc: e.matmul(
                    pff[0][:], ones1[:], rowt[0:1, dc * 512:(dc + 1) * 512],
                    start=True, stop=True),
                    waits=[(L, lr), (V, V.n)], inc=(Ts, 1))
                plan.emit("vector", lambda e, dst=dst, dc=dc: e.tensor_copy(
                    dst[:, dc * 512:(dc + 1) * 512], pff[0][:]),
                    waits=[(Ts, Ts.n)], inc=(V, 1))

        for ct in range(NC_):
            plan.dma("sync", zbuf[:], zs_in[ct * 128:(ct + 1) * 128, :], L,
                     waits=[(V, V.n)])
            lz = L.n
            plan.emit("vector", lambda e, ct=ct: e.tensor_tensor(
                ybar[:, ct * TL:(ct + 1) * TL], ybar[:, ct * TL:(ct + 1) * TL],
                zbuf[:], ALU.mult),
                waits=[(L, lz)], inc=(V, 1))

        def ln_tiles(get_src, n_tiles, base_v, out_cb):
            for tt in range(n_tiles):
                src = get_src(tt)
                plan.emit("scalar", lambda e, s=src: e.activation(sqs[:], s, AF.Square),
                          waits=[(V, max(base_v, V.n))], inc=(Cs, 1))
                plan.emit("vector", lambda e, s=src, o=mcol[:, tt:tt + 1]:
                          e.tensor_reduce(o, s, AX.X, ALU.add),
                          waits=[(V, base_v)], inc=(V, 1))
                plan.emit("vector", lambda e, o=scol[:, tt:tt + 1]:
                          e.tensor_reduce(o, sqs[:], AX.X, ALU.add),
                          waits=[(Cs, Cs.n)], inc=(V, 1))
            nt = n_tiles
            plan.emit("vector", lambda e, nt=nt: e.tensor_scalar(
                mcol[:, 0:nt], mcol[:, 0:nt], 1.0 / D, None, ALU.mult),
                waits=[(V, V.n)], inc=(V, 1))
            plan.emit("vector", lambda e, nt=nt: e.tensor_scalar(
                scol[:, 0:nt], scol[:, 0:nt], 1.0 / D, None, ALU.mult),
                waits=[(V, V.n)], inc=(V, 1))
            plan.emit("vector", lambda e, nt=nt: e.scalar_tensor_tensor(
                rcol[:, 0:nt], mcol[:, 0:nt], -1.0, mcol[:, 0:nt], ALU.mult, ALU.mult),
                waits=[(V, V.n)], inc=(V, 1))
            plan.emit("vector", lambda e, nt=nt: e.tensor_tensor(
                scol[:, 0:nt], scol[:, 0:nt], rcol[:, 0:nt], ALU.add),
                waits=[(V, V.n)], inc=(V, 1))
            plan.emit("vector", lambda e, nt=nt: e.tensor_scalar(
                scol[:, 0:nt], scol[:, 0:nt], LN_EPS, None, ALU.add),
                waits=[(V, V.n)], inc=(V, 1))
            vv = V.n
            plan.emit("scalar", lambda e, nt=nt: e.activation(
                rcol[:, 0:nt], scol[:, 0:nt], AF.Sqrt), waits=[(V, vv)], inc=(Cs, 1))
            plan.emit("vector", lambda e, nt=nt: e.reciprocal(
                rcol[:, 0:nt], rcol[:, 0:nt]), waits=[(Cs, Cs.n)], inc=(V, 1))
            for tt in range(n_tiles):
                out_cb(tt, mcol[:, tt:tt + 1], rcol[:, tt:tt + 1])

        for ch in range(NCH):
            tt0 = ch * NTT
            for tl in range(NTT):
                for dc in range(2):
                    pm = pmm[tl % 4]
                    for kt in range(NC_):
                        w = [(V, V.n), (Cs, Cs.n)] if kt == 0 else []
                        plan.emit("tensor", lambda e, tl=tl, dc=dc, kt=kt, pm=pm, tt0=tt0: e.matmul(
                            pm[:], ybar[:, kt * TL + (tt0 + tl) * 128: kt * TL + (tt0 + tl) * 128 + 128],
                            woutb[:, kt * D + dc * 512: kt * D + (dc + 1) * 512],
                            start=(kt == 0), stop=(kt == NC_ - 1)),
                            waits=w, inc=((Ts, 1) if kt == NC_ - 1 else None))
                    plan.emit("vector", lambda e, tl=tl, dc=dc, pm=pm: e.tensor_copy(
                        mixc[:, tl * D + dc * 512: tl * D + (dc + 1) * 512], pm[:]),
                        waits=[(Ts, Ts.n), (Cs, Cs.n), (St, St.n)], inc=(V, 1))
            v_mix = V.n
            if upto == "mix":
                for tt in range(NTT):
                    plan.dma("gpsimd", x2_o[(tt0 + tt) * 128:(tt0 + tt + 1) * 128, :],
                             mixc[:, tt * D:(tt + 1) * D], St, waits=[(V, v_mix)])
                continue

            def upd_x1(tt, mc, rc, tt0=tt0):
                plan.emit("vector", lambda e, tt=tt, mc=mc, rc=rc: e.tensor_scalar(
                    dscr[:], mixc[:, tt * D:(tt + 1) * D], mc, rc,
                    ALU.subtract, ALU.mult), waits=[(V, V.n)], inc=(V, 1))
                plan.emit("vector", lambda e, tt=tt, tt0=tt0: e.tensor_tensor(
                    xtd[:, (tt0 + tt) * D:(tt0 + tt + 1) * D], dscr[:],
                    xtd[:, (tt0 + tt) * D:(tt0 + tt + 1) * D], ALU.add), inc=(V, 1))
            ln_tiles(lambda tt: mixc[:, tt * D:(tt + 1) * D], NTT, v_mix, upd_x1)

        if upto == "mix":
            plan.emit("gpsimd", lambda e: e.engine_nop(), waits=[(St, St.n)])
            plan.run()
            return nc
        if upto == "x1":
            for tt in range(NT):
                plan.dma("gpsimd", x2_o[tt * 128:(tt + 1) * 128, :],
                         xtd[:, tt * D:(tt + 1) * D], St, waits=[(V, V.n)])
            plan.emit("gpsimd", lambda e: e.engine_nop(), waits=[(St, St.n)])
            plan.run()
            return nc
        g_idf.__exit__(None, None, None)
        g_mx.__exit__(None, None, None)
        g_zb.__exit__(None, None, None)
        g_wo.__exit__(None, None, None)
        g_yb.__exit__(None, None, None)
        KG = 8
        u2c = sb("u2c", [128, NTT * D], BF16)
        u2T = sb("u2T", [128, ND * TCH], BF16)
        hbuf2 = sb("hbuf2", [128, NF * TCH], BF16)
        v2c = sb("v2c", [128, NTT * D], F32)
        w1s = [sb(f"w1s{i}", [128, ND * KG * 128], BF16) for i in range(2)]
        w2s = [sb(f"w2s{i}", [128, 4 * 512], BF16) for i in range(2)]

        ts_pa, v_pa, c_pa = Ts.n, V.n, Cs.n
        w1_last = {0: None, 1: None}
        w2_last = {0: None, 1: None}
        for ch in range(NCH):
            tt0 = ch * NTT
            v_x1 = V.n
            def mk_u2(tt, mc, rc, tt0=tt0):
                plan.emit("vector", lambda e, tt=tt, mc=mc, rc=rc: e.tensor_scalar(
                    u2c[:, tt * D:(tt + 1) * D], xtd[:, (tt0 + tt) * D:(tt0 + tt + 1) * D],
                    mc, rc, ALU.subtract, ALU.mult),
                    waits=[(Ts, Ts.n)], inc=(V, 1))
            ln_tiles(lambda tt, tt0=tt0: xtd[:, (tt0 + tt) * D:(tt0 + tt + 1) * D],
                     NTT, v_x1, mk_u2)
            v_u2 = V.n
            ntp = 0
            for tt in range(NTT):
                for dt in range(ND):
                    pb = ptb[ntp % 2]
                    w = [(V, v_u2)]
                    if ntp >= 2:
                        w.append((Cs, Cs.n))
                    plan.emit("tensor", lambda e, tt=tt, dt=dt, pb=pb: e.transpose(
                        pb[:], u2c[:, tt * D + dt * 128: tt * D + dt * 128 + 128],
                        identb[:]), waits=w, inc=(Ts, 1))
                    plan.emit("scalar", lambda e, tt=tt, dt=dt, pb=pb: e.copy(
                        u2T[:, dt * TCH + tt * 128: dt * TCH + (tt + 1) * 128], pb[:]),
                        waits=[(Ts, Ts.n)], inc=(Cs, 1))
                    ntp += 1
            c_u2T = Cs.n
            for fg in range(NF // KG):
                bi = fg % 2
                wbuf = w1s[bi]
                if w1_last[bi] is None:
                    wt = [(Ts, ts_pa), (V, v_pa), (Cs, c_pa)]
                else:
                    wt = [(Ts, w1_last[bi])]
                plan.dma("gpsimd", wbuf[:].rearrange("p (k f) -> p k f", k=ND),
                         w1_in[:, fg * KG * 128:(fg + 1) * KG * 128]
                         .rearrange("(k p) f -> p k f", p=128), Lg, waits=wt)
                lw = Lg.n
                for fi in range(KG):
                    ft = fg * KG + fi
                    pf = pff[ft % 2]
                    for kt in range(ND):
                        w = ([(Lg, lw), (Cs, Cs.n)] if kt == 0 else [])
                        plan.emit("tensor", lambda e, fi=fi, kt=kt, pf=pf, wbuf=wbuf: e.matmul(
                            pf[:], wbuf[:, kt * KG * 128 + fi * 128: kt * KG * 128 + fi * 128 + 128],
                            u2T[:, kt * TCH:(kt + 1) * TCH],
                            start=(kt == 0), stop=(kt == ND - 1)),
                            waits=w, inc=((Ts, 1) if kt == ND - 1 else None))
                    w1_last[bi] = Ts.n
                    plan.emit("scalar", lambda e, ft=ft, pf=pf: e.activation(
                        hbuf2[:, ft * TCH:(ft + 1) * TCH], pf[:], AF.Relu,
                        bias=b1b[:, ft:ft + 1]),
                        waits=[(Ts, Ts.n)], inc=(Cs, 1))
            c_h = Cs.n
            if upto == "h":
                if ch == 0:
                    for ft in range(NF):
                        plan.dma("gpsimd", h_o[ft * 128:(ft + 1) * 128, :],
                                 hbuf2[:, ft * TCH:(ft + 1) * TCH], St,
                                 waits=[(Cs, c_h)])
                continue
            for dc in range(2):
                for kg in range(NF // 4):
                    bi = kg % 2
                    wbuf = w2s[bi]
                    if w2_last[bi] is None:
                        wt = [(Ts, ts_pa), (V, v_pa), (Cs, c_pa)]
                    else:
                        wt = [(Ts, w2_last[bi])]
                    plan.dma("gpsimd", wbuf[:].rearrange("p (k d) -> p k d", k=4),
                             w2_in[kg * 512:(kg + 1) * 512, dc * 512:(dc + 1) * 512]
                             .rearrange("(k p) d -> p k d", p=128), Lg, waits=wt)
                    lw = Lg.n
                    for ki in range(4):
                        kt = kg * 4 + ki
                        for tl in range(NTT):
                            w = [(Lg, lw)] if tl == 0 else []
                            if kt == 0 and tl == 0:
                                w.append((V, V.n))
                                w.append((Cs, c_h))
                            plan.emit("tensor", lambda e, ki=ki, kt=kt, tl=tl, dc=dc,
                                      wbuf=wbuf: e.matmul(
                                pmm[tl][:, 0:512], hbuf2[:, kt * TCH + tl * 128: kt * TCH + tl * 128 + 128],
                                wbuf[:, ki * 512:(ki + 1) * 512],
                                start=(kt == 0), stop=(kt == NF - 1)),
                                waits=w,
                                inc=((Ts, 1) if (kt == NF - 1 or
                                                 (ki == 3 and tl == NTT - 1)) else None))
                    w2_last[bi] = Ts.n
                t_e = Ts.n
                for tl in range(NTT):
                    plan.emit("vector", lambda e, tl=tl, dc=dc: e.tensor_tensor(
                        v2c[:, tl * D + dc * 512: tl * D + (dc + 1) * 512], pmm[tl][:, 0:512],
                        b2rep[:, dc * 512:(dc + 1) * 512], ALU.add),
                        waits=[(Ts, t_e), (Cs, Cs.n), (St, St.n)], inc=(V, 1))
                    plan.emit("vector", lambda e, tl=tl, dc=dc: e.tensor_tensor(
                        v2c[:, tl * D + dc * 512: tl * D + (dc + 1) * 512],
                        v2c[:, tl * D + dc * 512: tl * D + (dc + 1) * 512],
                        u2c[:, tl * D + dc * 512: tl * D + (dc + 1) * 512], ALU.add),
                        inc=(V, 1))
            v_v2 = V.n
            if upto == "v2":
                for tt in range(NTT):
                    plan.dma("gpsimd", x2_o[(tt0 + tt) * 128:(tt0 + tt + 1) * 128, :],
                             v2c[:, tt * D:(tt + 1) * D], St, waits=[(V, v_v2)])
                continue
            def mk_w(tt, mc, rc):
                plan.emit("vector", lambda e, tt=tt, mc=mc, rc=rc: e.tensor_scalar(
                    v2c[:, tt * D:(tt + 1) * D], v2c[:, tt * D:(tt + 1) * D],
                    mc, rc, ALU.subtract, ALU.mult), inc=(V, 1))
                plan.emit("vector", lambda e, tt=tt: e.tensor_tensor(
                    v2c[:, tt * D:(tt + 1) * D], v2c[:, tt * D:(tt + 1) * D],
                    gamrep[:], ALU.mult), inc=(V, 1))
                plan.emit("vector", lambda e, tt=tt: e.tensor_tensor(
                    v2c[:, tt * D:(tt + 1) * D], v2c[:, tt * D:(tt + 1) * D],
                    betrep[:], ALU.add), inc=(V, 1))
            ln_tiles(lambda tt: v2c[:, tt * D:(tt + 1) * D], NTT, v_v2, mk_w)
            v_w = V.n
            def upd_x2(tt, mc, rc, tt0=tt0):
                plan.emit("vector", lambda e, tt=tt, mc=mc, rc=rc: e.tensor_scalar(
                    dscr[:], v2c[:, tt * D:(tt + 1) * D], mc, rc,
                    ALU.subtract, ALU.mult), waits=[(V, V.n)], inc=(V, 1))
                plan.emit("vector", lambda e, tt=tt, tt0=tt0: e.tensor_tensor(
                    xtd[:, (tt0 + tt) * D:(tt0 + tt + 1) * D], dscr[:],
                    xtd[:, (tt0 + tt) * D:(tt0 + tt + 1) * D], ALU.add), inc=(V, 1))
            ln_tiles(lambda tt: v2c[:, tt * D:(tt + 1) * D], NTT, v_w, upd_x2)
            for tt in range(NTT):
                plan.dma("gpsimd", x2_o[(tt0 + tt) * 128:(tt0 + tt + 1) * 128, :],
                         xtd[:, (tt0 + tt) * D:(tt0 + tt + 1) * D], St,
                         waits=[(V, V.n)])
        plan.emit("gpsimd", lambda e: e.engine_nop(), waits=[(St, St.n)])
        if F is None:
            plan.run()
    return nc


def host_back_inputs(inp, front_results, scan_results):
    ident = np.eye(128, dtype=np.float32)
    x = inp["L0_attn_x"]
    maps = []
    for c in range(NCORE):
        b, m = c // 2, c % 2
        t0 = m * TL
        ybp = np.concatenate([scan_results[2 * b][:, t0:t0 + TL],
                              scan_results[2 * b + 1][:, t0:t0 + TL]], axis=0)
        b1l = np.ascontiguousarray(
            inp["L0_ffn_b1"][0].reshape(FF // 128, 128).T)
        maps.append({
            "x_in": np.ascontiguousarray(x[b, t0:t0 + TL]),
            "ybp_in": np.ascontiguousarray(ybp),
            "zs_in": front_results[c]["zs_o"],
            "wout_in": inp["L0_attn_W_out"],
            "w1_in": inp["L0_ffn_w1"][0],
            "b1_in": b1l,
            "w2_in": inp["L0_ffn_w2"][0],
            "b2_in": inp["L0_ffn_b2"][0:1, :].reshape(1, D) if inp["L0_ffn_b2"].ndim == 2
                     else inp["L0_ffn_b2"][0].reshape(1, D),
            "gam_in": inp["L0_ffn_gamma"].reshape(1, D),
            "bet_in": inp["L0_ffn_beta"].reshape(1, D),
            "ident": ident,
        })
    return maps




def build_attn(upto='full', F=None):
    nc = bass.Bass("TRN2", target_bir_lowering=False, debug=False) if F is None else F["nc"]
    NT = TL // 128
    ND = D // 128
    TC = 256
    NCH = TL // TC
    NTT = TC // 128

    if F is None:
        x2_in = nc.dram_tensor("x2_in", [TL, D], F32, kind="ExternalInput").ap()
        wq_in = nc.dram_tensor("wq_in", [D, D], F32, kind="ExternalInput").ap()
        bq_in = nc.dram_tensor("bq_in", [128, ND], F32, kind="ExternalInput").ap()
        wd_in = nc.dram_tensor("wd_in", [D, 256], F32, kind="ExternalInput").ap()
        bd_in = nc.dram_tensor("bd_in", [128, 2], F32, kind="ExternalInput").ap()
        wu_in = nc.dram_tensor("wu_in", [256, DI], F32, kind="ExternalInput").ap()
        bu_in = nc.dram_tensor("bu_in", [128, DI // 128], F32, kind="ExternalInput").ap()
        wo_in = nc.dram_tensor("wo_in", [D, D], F32, kind="ExternalInput").ap()
        bo_in = nc.dram_tensor("bo_in", [1, D], F32, kind="ExternalInput").ap()
        cosr_in = nc.dram_tensor("cosr_in", [128, TL], F32, kind="ExternalInput").ap()
        sinr_in = nc.dram_tensor("sinr_in", [128, TL], F32, kind="ExternalInput").ap()
        selk_in = nc.dram_tensor("selk_in", [128, 128], F32, kind="ExternalInput").ap()
        selv_in = nc.dram_tensor("selv_in", [128, 128], F32, kind="ExternalInput").ap()
        red_in = nc.dram_tensor("red_in", [128, 8 * 16], F32, kind="ExternalInput").ap()
        am_in = nc.dram_tensor("am_in", [16, 8 * 128], F32, kind="ExternalInput").ap()
        ident_in = nc.dram_tensor("ident", [128, 128], F32, kind="ExternalInput").ap()
        x3_o = nc.dram_tensor("x3_o", [TL, D], F32, kind="ExternalOutput").ap()
    else:
        io = F["io"]
        x2_in, wq_in, bq_in, wd_in = io["x2_in"], io["wq_in"], io["bq_in"], io["wd_in"]
        bd_in, wu_in, bu_in, wo_in = io["bd_in"], io["wu_in"], io["bu_in"], io["wo_in"]
        bo_in, cosr_in, sinr_in = io["bo_in"], io["cosr_in"], io["sinr_in"]
        selk_in, selv_in, red_in, am_in = io["selk_in"], io["selv_in"], io["red_in"], io["am_in"]
        ident_in, x3_o = io["ident"], io["x3_o"]
    pfx = "" if F is None else F["pfx"]

    ctx = contextlib.ExitStack()
    with ctx:
        sb = lambda name, shape, dt: ctx.enter_context(nc.sbuf_tensor(pfx + name, shape, dt))
        ps = lambda name, shape, dt=F32: ctx.enter_context(nc.psum_tensor(pfx + name, shape, dt))

        identb = sb("identb", [128, 128], BF16)
        identf = sb("identf", [128, 128], F32)
        ones1 = sb("ones1", [1, 128], F32)
        sqs = sb("sqs", [128, D], F32)
        mcol = sb("mcol", [128, 16], F32)
        rcol = sb("rcol", [128, 16], F32)
        scol = sb("scol", [128, 16], F32)
        dscr = sb("dscr", [128, D], F32)
        xts = [sb(f"xts{i}", [128, D], F32) for i in range(2)]
        u3t = sb("u3t", [128, D], BF16)
        u3T = sb("u3T", [128, ND * TL], BF16)
        wqb = sb("wqb", [128, ND * D], BF16)
        wdb = sb("wdb", [128, ND * 256], BF16)
        wub = sb("wub", [128, 2 * DI], BF16)
        wob = sb("wob", [128, ND * D], BF16)
        bqb = sb("bqb", [128, ND], F32)
        bdb = sb("bdb", [128, 2], F32)
        bub = sb("bub", [128, DI // 128], F32)
        borep = sb("borep", [128, D], F32)
        cosr = sb("cosr", [128, TC], F32)
        sinr = sb("sinr", [128, TC], F32)
        selkb = sb("selkb", [128, 128], BF16)
        selvb = sb("selvb", [128, 128], BF16)
        redb = sb("redb", [128, 8 * 16], BF16)
        amb = sb("amb", [16, 8 * 128], BF16)
        rowt = sb("rowt", [1, D], F32)
        qb = sb("qb", [128, ND * TC], BF16)
        qsw = sb("qsw", [128, 16 * TC], BF16)
        kvb = sb("kvb", [128, 16 * TC], BF16)
        latb = sb("latb", [128, 2 * TC], BF16)
        krep = sb("krep", [128, 16 * TC], BF16)
        vrep = sb("vrep", [128, 16 * TC], BF16)
        pgb = sb("pgb", [128, ND * TC], BF16)
        attb = sb("attb", [16, 16 * TC], F32)
        rmaxb = sb("rmaxb", [16, TC], F32)
        rsumb = sb("rsumb", [16, TC], F32)
        probsb = sb("probsb", [16, 16 * TC], BF16)
        arep = [sb(f"arep{i}", [128, TC], BF16) for i in range(2)]
        mbuf = [sb(f"mbuf{i}", [128, TC], BF16) for i in range(2)]
        aoutT = sb("aoutT", [128, ND * TC], BF16)
        aoc = sb("aoc", [128, NTT * D], F32)
        t1 = sb("t1", [128, 16 * TC], BF16)
        t2 = sb("t2", [128, 16 * TC], BF16)

        pq = [ps(f"pq{i}", [128, 512]) for i in range(2)]
        pat = [ctx.enter_context(nc.psum_tensor(pfx + "pat0", [16, TC], F32))] * 2
        prep = [ps(f"prep{i}", [128, TC]) for i in range(2)]
        pou = [ps("pou0", [128, TC])] * 2
        ptb = [ctx.enter_context(nc.psum_tensor(pfx + f"ptb{i}", [128, 128], BF16))
               for i in range(2)]

        plan = Plan(nc, ctx) if F is None else F["plan"]
        L = plan.sem("L")
        Lg = plan.sem("Lg")
        V = plan.sem("V")
        Cs = plan.sem("C")
        Ts = plan.sem("T")
        G = plan.sem("G")
        St = plan.sem("St")

        plan.dma("gpsimd", wqb[:].rearrange("p (k d) -> p k d", k=ND),
                 wq_in.rearrange("(k p) d -> p k d", p=128), Lg)
        plan.dma("gpsimd", wdb[:].rearrange("p (k d) -> p k d", k=ND),
                 wd_in.rearrange("(k p) d -> p k d", p=128), Lg)
        plan.dma("gpsimd", wub[:].rearrange("p (k d) -> p k d", k=2),
                 wu_in.rearrange("(k p) d -> p k d", p=128), Lg)
        plan.dma("gpsimd", wob[:].rearrange("p (k d) -> p k d", k=ND),
                 wo_in.rearrange("(k p) d -> p k d", p=128), Lg)
        plan.dma("sync", bqb[:], bq_in, L)
        plan.dma("sync", bdb[:], bd_in, L)
        plan.dma("sync", bub[:], bu_in, L)

        plan.dma("gpsimd", selkb[:], selk_in, Lg)
        plan.dma("gpsimd", selvb[:], selv_in, Lg)
        plan.dma("gpsimd", redb[:], red_in, Lg)
        plan.dma("gpsimd", amb[:], am_in, Lg)
        plan.dma("sync", rowt[:], bo_in, L)
        plan.dma("sync", identf[:], ident_in, L)
        lds = L.n
        lga = Lg.n
        for _eng in ("vector", "scalar", "tensor"):
            plan.emit(_eng, lambda e: None, waits=[(Lg, lga)])

        plan.emit("vector", lambda e: e.memset(ones1[:], 1.0), inc=(V, 1))
        plan.emit("vector", lambda e: e.tensor_copy(identb[:], identf[:]),
                  waits=[(L, lds)], inc=(V, 1))
        for dc in range(2):
            plan.emit("tensor", lambda e, dc=dc: e.matmul(
                pq[0][:], ones1[:], rowt[0:1, dc * 512:(dc + 1) * 512],
                start=True, stop=True),
                waits=[(L, lds), (V, V.n)], inc=(Ts, 1))
            plan.emit("vector", lambda e, dc=dc: e.tensor_copy(
                borep[:, dc * 512:(dc + 1) * 512], pq[0][:]),
                waits=[(Ts, Ts.n)], inc=(V, 1))

        def ln_tiles(get_src, n_tiles, base_v, out_cb):
            for tt in range(n_tiles):
                src = get_src(tt)
                plan.emit("scalar", lambda e, s=src: e.activation(sqs[:], s, AF.Square),
                          waits=[(V, max(base_v, V.n))], inc=(Cs, 1))
                plan.emit("vector", lambda e, s=src, o=mcol[:, tt:tt + 1]:
                          e.tensor_reduce(o, s, AX.X, ALU.add),
                          waits=[(V, base_v)], inc=(V, 1))
                plan.emit("vector", lambda e, o=scol[:, tt:tt + 1]:
                          e.tensor_reduce(o, sqs[:], AX.X, ALU.add),
                          waits=[(Cs, Cs.n)], inc=(V, 1))
            nt = n_tiles
            plan.emit("vector", lambda e, nt=nt: e.tensor_scalar(
                mcol[:, 0:nt], mcol[:, 0:nt], 1.0 / D, None, ALU.mult),
                waits=[(V, V.n)], inc=(V, 1))
            plan.emit("vector", lambda e, nt=nt: e.tensor_scalar(
                scol[:, 0:nt], scol[:, 0:nt], 1.0 / D, None, ALU.mult),
                waits=[(V, V.n)], inc=(V, 1))
            plan.emit("vector", lambda e, nt=nt: e.scalar_tensor_tensor(
                rcol[:, 0:nt], mcol[:, 0:nt], -1.0, mcol[:, 0:nt], ALU.mult, ALU.mult),
                waits=[(V, V.n)], inc=(V, 1))
            plan.emit("vector", lambda e, nt=nt: e.tensor_tensor(
                scol[:, 0:nt], scol[:, 0:nt], rcol[:, 0:nt], ALU.add),
                waits=[(V, V.n)], inc=(V, 1))
            plan.emit("vector", lambda e, nt=nt: e.tensor_scalar(
                scol[:, 0:nt], scol[:, 0:nt], LN_EPS, None, ALU.add),
                waits=[(V, V.n)], inc=(V, 1))
            vv = V.n
            plan.emit("scalar", lambda e, nt=nt: e.activation(
                rcol[:, 0:nt], scol[:, 0:nt], AF.Sqrt), waits=[(V, vv)], inc=(Cs, 1))
            plan.emit("vector", lambda e, nt=nt: e.reciprocal(
                rcol[:, 0:nt], rcol[:, 0:nt]), waits=[(Cs, Cs.n)], inc=(V, 1))
            for tt in range(n_tiles):
                out_cb(tt, mcol[:, tt:tt + 1], rcol[:, tt:tt + 1])

        tp_use = {}
        for tt in range(NT):
            xb = xts[tt % 2]
            wt = [(Ts, tp_use[tt - 2])] if (tt - 2) in tp_use else []
            plan.dma("sync", xb[:], x2_in[tt * 128:(tt + 1) * 128, :], L, waits=wt)
            lx = L.n

            plan.emit("scalar", lambda e, xb=xb: e.activation(sqs[:], xb[:], AF.Square),
                      waits=[(L, lx), (V, V.n)], inc=(Cs, 1))
            plan.emit("vector", lambda e, xb=xb: e.tensor_reduce(
                mcol[:, 0:1], xb[:], AX.X, ALU.add), waits=[(L, lx)], inc=(V, 1))
            plan.emit("vector", lambda e: e.tensor_reduce(
                scol[:, 0:1], sqs[:], AX.X, ALU.add), waits=[(Cs, Cs.n)], inc=(V, 1))
            plan.emit("vector", lambda e: e.tensor_scalar(
                mcol[:, 0:1], mcol[:, 0:1], 1.0 / D, None, ALU.mult),
                waits=[(V, V.n)], inc=(V, 1))
            plan.emit("vector", lambda e: e.tensor_scalar(
                scol[:, 0:1], scol[:, 0:1], 1.0 / D, None, ALU.mult),
                waits=[(V, V.n)], inc=(V, 1))
            plan.emit("vector", lambda e: e.scalar_tensor_tensor(
                rcol[:, 0:1], mcol[:, 0:1], -1.0, mcol[:, 0:1], ALU.mult, ALU.mult),
                waits=[(V, V.n)], inc=(V, 1))
            plan.emit("vector", lambda e: e.tensor_tensor(
                scol[:, 0:1], scol[:, 0:1], rcol[:, 0:1], ALU.add),
                waits=[(V, V.n)], inc=(V, 1))
            plan.emit("vector", lambda e: e.tensor_scalar(
                scol[:, 0:1], scol[:, 0:1], LN_EPS, None, ALU.add),
                waits=[(V, V.n)], inc=(V, 1))
            vv = V.n
            plan.emit("scalar", lambda e: e.activation(
                rcol[:, 0:1], scol[:, 0:1], AF.Sqrt), waits=[(V, vv)], inc=(Cs, 1))
            plan.emit("vector", lambda e: e.reciprocal(
                rcol[:, 0:1], rcol[:, 0:1]), waits=[(Cs, Cs.n)], inc=(V, 1))
            plan.emit("vector", lambda e, xb=xb: e.tensor_scalar(
                u3t[:], xb[:], mcol[:, 0:1], rcol[:, 0:1], ALU.subtract, ALU.mult),
                waits=[(Ts, Ts.n)], inc=(V, 1))
            v_u3 = V.n
            for dt in range(ND):
                pbu = (tt * ND + dt) % 2
                w = [(V, v_u3)]
                if tt * ND + dt >= 2:
                    w.append((Cs, Cs.n))
                plan.emit("tensor", lambda e, dt=dt, pbu=pbu: e.transpose(
                    ptb[pbu][:], u3t[:, dt * 128:(dt + 1) * 128], identb[:]),
                    waits=w, inc=(Ts, 1))
                plan.emit("scalar", lambda e, tt=tt, dt=dt, pbu=pbu: e.copy(
                    u3T[:, dt * TL + tt * 128: dt * TL + (tt + 1) * 128], ptb[pbu][:]),
                    waits=[(Ts, Ts.n)], inc=(Cs, 1))
            tp_use[tt] = Ts.n
        c_u3T = Cs.n
        if upto == "u3":
            for dt in range(ND):
                plan.dma("gpsimd", x3_o[dt * 256:dt * 256 + 128, 0:D],
                         u3T[:, dt * TL:dt * TL + D], St, waits=[(Cs, c_u3T)])
            plan.emit("gpsimd", lambda e: e.engine_nop(), waits=[(St, St.n)])
            plan.run()
            return nc

        nch_run = 1 if upto.startswith("one_") else (2 if upto.startswith("two_") else NCH)
        upto = upto.removeprefix("one_").removeprefix("two_")
        for ch in range(nch_run):
            c0 = ch * TC
            plan.dma("sync", cosr[:], cosr_in[:, c0:c0 + TC], L, waits=[(V, V.n)])
            plan.dma("sync", sinr[:], sinr_in[:, c0:c0 + TC], L, waits=[(V, V.n)])
            l_rope = L.n
            for qt in range(ND):
                pqt = pq[qt % 2]
                for kt in range(ND):
                    w = [(Cs, Cs.n), (V, V.n), (G, G.n)] if kt == 0 else []
                    plan.emit("tensor", lambda e, qt=qt, kt=kt, c0=c0, pqt=pqt: e.matmul(
                        pqt[:, 0:TC], wqb[:, kt * D + qt * 128: kt * D + qt * 128 + 128],
                        u3T[:, kt * TL + c0: kt * TL + c0 + TC],
                        start=(kt == 0), stop=(kt == ND - 1)),
                        waits=w, inc=((Ts, 1) if kt == ND - 1 else None))
                plan.emit("scalar", lambda e, qt=qt, pqt=pqt: e.activation(
                    qb[:, qt * TC:(qt + 1) * TC], pqt[:, 0:TC], AF.Identity,
                    bias=bqb[:, qt:qt + 1]),
                    waits=[(Ts, Ts.n)], inc=(Cs, 1))
            c_q = Cs.n
            if upto == "q":
                if ch == 0:
                    plan.dma("gpsimd", x3_o[0:128, 0:D], qb[:, 0:D], St,
                             waits=[(Cs, c_q)])
                continue
            for lt in range(2):
                plt = pq[lt % 2]
                for kt in range(ND):
                    w = [(Cs, Cs.n)] if kt == 0 else []
                    plan.emit("tensor", lambda e, lt=lt, kt=kt, c0=c0, plt=plt: e.matmul(
                        plt[:, 0:TC], wdb[:, kt * 256 + lt * 128: kt * 256 + lt * 128 + 128],
                        u3T[:, kt * TL + c0: kt * TL + c0 + TC],
                        start=(kt == 0), stop=(kt == ND - 1)),
                        waits=w, inc=((Ts, 1) if kt == ND - 1 else None))
                plan.emit("scalar", lambda e, lt=lt, plt=plt: e.activation(
                    latb[:, lt * TC:(lt + 1) * TC], plt[:, 0:TC], AF.Identity,
                    bias=bdb[:, lt:lt + 1]),
                    waits=[(Ts, Ts.n)], inc=(Cs, 1))
            c_lat = Cs.n
            for ht in range(16):
                pkv = pq[ht % 2]
                for kt in range(2):
                    w = [(Cs, Cs.n)] if kt == 0 else []
                    plan.emit("tensor", lambda e, ht=ht, kt=kt, pkv=pkv: e.matmul(
                        pkv[:, 0:TC], wub[:, kt * DI + ht * 128: kt * DI + ht * 128 + 128],
                        latb[:, kt * TC:(kt + 1) * TC],
                        start=(kt == 0), stop=(kt == 1)),
                        waits=w, inc=((Ts, 1) if kt == 1 else None))
                plan.emit("scalar", lambda e, ht=ht, pkv=pkv: e.activation(
                    kvb[:, ht * TC:(ht + 1) * TC], pkv[:, 0:TC], AF.Identity,
                    bias=bub[:, ht:ht + 1]),
                    waits=[(Ts, Ts.n)], inc=(Cs, 1))
            c_kv = Cs.n
            if upto == "qkv":
                if ch == 0:
                    plan.dma("gpsimd", x3_o[0:128, 0:D], qb[:, 0:D], St,
                             waits=[(Cs, c_kv)])
                    plan.dma("gpsimd", x3_o[128:256, 0:D],
                             kvb[:, 0:D], St, waits=[(Cs, c_kv)])
                continue
            for blk in range(4):
                r0 = blk * 32
                s0 = r0 + 32 if blk % 2 == 0 else r0 - 32
                plan.emit("gpsimd", lambda e, r0=r0, s0=s0: e.dma_start(
                    out=qsw[r0:r0 + 32, 0:ND * TC], in_=qb[s0:s0 + 32, :]),
                    waits=[(Cs, c_kv), (V, V.n)], inc=(G, 16))
            g_qsw = G.n
            plan.emit("vector", lambda e: e.tensor_tensor(
                t1[:, 0:ND * TC].rearrange("p (n t) -> p n t", n=ND),
                qb[:].rearrange("p (n t) -> p n t", n=ND),
                bc0(cosr[:], ND), ALU.mult),
                waits=[(G, g_qsw), (L, l_rope)], inc=(V, 1))
            plan.emit("vector", lambda e: e.tensor_tensor(
                t2[:, 0:ND * TC].rearrange("p (n t) -> p n t", n=ND),
                qsw[:, 0:ND * TC].rearrange("p (n t) -> p n t", n=ND),
                bc0(sinr[:], ND), ALU.mult), inc=(V, 1))
            plan.emit("vector", lambda e: e.tensor_tensor(
                qb[:], t1[:, 0:ND * TC], t2[:, 0:ND * TC], ALU.add), inc=(V, 1))
            v_qrope = V.n
            for ht in range(16):
                plan.emit("gpsimd", lambda e, ht=ht: e.dma_start(
                    out=qsw[0:32, ht * TC:(ht + 1) * TC],
                    in_=kvb[32:64, ht * TC:(ht + 1) * TC]),
                    waits=[(Cs, c_kv), (V, v_qrope)], inc=(G, 16))
                plan.emit("gpsimd", lambda e, ht=ht: e.dma_start(
                    out=qsw[32:64, ht * TC:(ht + 1) * TC],
                    in_=kvb[0:32, ht * TC:(ht + 1) * TC]),
                    inc=(G, 16))
            g_ksw = G.n
            plan.emit("vector", lambda e: e.tensor_tensor(
                t1[0:64, :].rearrange("p (n t) -> p n t", n=16),
                kvb[0:64, :].rearrange("p (n t) -> p n t", n=16),
                bc0(cosr[0:64, :], 16), ALU.mult),
                waits=[(G, g_ksw)], inc=(V, 1))
            plan.emit("vector", lambda e: e.tensor_tensor(
                t2[0:64, :].rearrange("p (n t) -> p n t", n=16),
                qsw[0:64, :].rearrange("p (n t) -> p n t", n=16),
                bc0(sinr[0:64, :], 16), ALU.mult), inc=(V, 1))
            plan.emit("vector", lambda e: e.tensor_tensor(
                kvb[0:64, :].rearrange("p (n t) -> p n t", n=16),
                t1[0:64, :].rearrange("p (n t) -> p n t", n=16),
                t2[0:64, :].rearrange("p (n t) -> p n t", n=16), ALU.add),
                inc=(V, 1))
            v_rope = V.n
            if upto == "rope":
                if ch == 0:
                    plan.dma("gpsimd", x3_o[0:128, 0:D], qb[:, 0:D], St,
                             waits=[(V, v_rope)])
                continue
            for g in range(16):
                pr = prep[g % 2]
                plan.emit("tensor", lambda e, g=g, pr=pr: e.matmul(
                    pr[:], selkb[:], kvb[:, g * TC:(g + 1) * TC], start=True, stop=True),
                    waits=[(V, v_rope), (Cs, Cs.n)], inc=(Ts, 1))
                plan.emit("scalar", lambda e, g=g, pr=pr: e.copy(
                    krep[:, g * TC:(g + 1) * TC], pr[:]),
                    waits=[(Ts, Ts.n)], inc=(Cs, 1))
                plan.emit("tensor", lambda e, g=g, pr=pr: e.matmul(
                    pr[:], selvb[:], kvb[:, g * TC:(g + 1) * TC], start=True, stop=True),
                    waits=[(Cs, Cs.n)], inc=(Ts, 1))
                plan.emit("scalar", lambda e, g=g, pr=pr: e.copy(
                    vrep[:, g * TC:(g + 1) * TC], pr[:]),
                    waits=[(Ts, Ts.n)], inc=(Cs, 1))
            c_rep = Cs.n
            for g in range(16):
                plan.emit("vector", lambda e, g=g: e.tensor_tensor(
                    pgb[:].rearrange("p (n t) -> p n t", n=ND),
                    qb[:].rearrange("p (n t) -> p n t", n=ND),
                    bc0(krep[:, g * TC:(g + 1) * TC], ND), ALU.mult),
                    waits=[(Cs, c_rep), (Ts, Ts.n)], inc=(V, 1))
                v_pg = V.n
                pa = pat[g % 2]
                for qt in range(ND):
                    w = [(V, v_pg)] if qt == 0 else []
                    plan.emit("tensor", lambda e, g=g, qt=qt, pa=pa: e.matmul(
                        pa[:], redb[:, qt * 16:(qt + 1) * 16],
                        pgb[:, qt * TC:(qt + 1) * TC],
                        start=(qt == 0), stop=(qt == ND - 1)),
                        waits=w, inc=((Ts, 1) if qt == ND - 1 else None))
                plan.emit("vector", lambda e, g=g, pa=pa: e.tensor_copy(
                    attb[:, g * TC:(g + 1) * TC], pa[:]),
                    waits=[(Ts, Ts.n)], inc=(V, 1))
            v_att = V.n
            if upto == "att":
                if ch == 0:
                    plan.dma("gpsimd", x3_o[0:16, 0:D], attb[:, 0:D], St,
                             waits=[(V, v_att)])
                continue
            plan.emit("vector", lambda e: e.tensor_copy(rmaxb[:], attb[:, 0:TC]),
                      waits=[(V, v_att)], inc=(V, 1))
            for g in range(1, 16):
                plan.emit("vector", lambda e, g=g: e.tensor_tensor(
                    rmaxb[:], rmaxb[:], attb[:, g * TC:(g + 1) * TC], ALU.max),
                    inc=(V, 1))
            for g in range(16):
                plan.emit("vector", lambda e, g=g: e.tensor_tensor(
                    attb[:, g * TC:(g + 1) * TC], attb[:, g * TC:(g + 1) * TC],
                    rmaxb[:], ALU.subtract), inc=(V, 1))
            v_sub = V.n
            plan.emit("scalar", lambda e: e.activation(attb[:], attb[:], AF.Exp,
                                                       scale=0.125),
                      waits=[(V, v_sub)], inc=(Cs, 1))
            c_exp = Cs.n
            plan.emit("vector", lambda e: e.tensor_copy(rsumb[:], attb[:, 0:TC]),
                      waits=[(Cs, c_exp)], inc=(V, 1))
            for g in range(1, 16):
                plan.emit("vector", lambda e, g=g: e.tensor_tensor(
                    rsumb[:], rsumb[:], attb[:, g * TC:(g + 1) * TC], ALU.add),
                    inc=(V, 1))
            plan.emit("vector", lambda e: e.reciprocal(rsumb[:], rsumb[:]), inc=(V, 1))
            for g in range(16):
                plan.emit("vector", lambda e, g=g: e.tensor_tensor(
                    probsb[:, g * TC:(g + 1) * TC], attb[:, g * TC:(g + 1) * TC],
                    rsumb[:], ALU.mult), inc=(V, 1))
            v_probs = V.n
            if upto == "probs":
                if ch == 0:
                    plan.dma("gpsimd", x3_o[0:16, 0:D],
                             probsb[:, 0:D], St, waits=[(V, v_probs)])
                continue
            for qt in range(ND):
                po = pou[qt % 2]
                for g in range(16):
                    ra = arep[g % 2]
                    rm = mbuf[g % 2]
                    pr = prep[g % 2]
                    plan.emit("tensor", lambda e, g=g, qt=qt, pr=pr: e.matmul(
                        pr[:], amb[:, qt * 128:(qt + 1) * 128],
                        probsb[:, g * TC:(g + 1) * TC], start=True, stop=True),
                        waits=[(V, max(v_probs, V.n)), (Cs, Cs.n)], inc=(Ts, 1))
                    plan.emit("scalar", lambda e, pr=pr, ra=ra: e.copy(ra[:], pr[:]),
                              waits=[(Ts, Ts.n)], inc=(Cs, 1))
                    plan.emit("vector", lambda e, g=g, ra=ra, rm=rm: e.tensor_tensor(
                        rm[:], ra[:], vrep[:, g * TC:(g + 1) * TC], ALU.mult),
                        waits=[(Cs, Cs.n), (Ts, Ts.n)], inc=(V, 1))
                    plan.emit("tensor", lambda e, rm=rm, po=po, g=g: e.matmul(
                        po[:], identb[:], rm[:], start=(g == 0), stop=(g == 15)),
                        waits=[(V, V.n)], inc=(Ts, 1))
                plan.emit("scalar", lambda e, qt=qt, po=po: e.copy(
                    aoutT[:, qt * TC:(qt + 1) * TC], po[:]),
                    waits=[(Ts, Ts.n)], inc=(Cs, 1))
            c_aout = Cs.n
            if upto == "aout":
                if ch == 0:
                    plan.dma("gpsimd", x3_o[0:128, 0:D], aoutT[:, 0:D], St,
                             waits=[(Cs, c_aout)])
                continue
            for tl in range(NTT):
                for dc in range(2):
                    pm = pq[(tl * 2 + dc) % 2]
                    for kt in range(ND):
                        w = [(Cs, c_aout), (V, V.n)] if kt == 0 else []
                        plan.emit("tensor", lambda e, tl=tl, dc=dc, kt=kt, pm=pm: e.matmul(
                            pm[:], aoutT[:, kt * TC + tl * 128: kt * TC + tl * 128 + 128],
                            wob[:, kt * D + dc * 512: kt * D + (dc + 1) * 512],
                            start=(kt == 0), stop=(kt == ND - 1)),
                            waits=w, inc=((Ts, 1) if kt == ND - 1 else None))
                    plan.emit("vector", lambda e, tl=tl, dc=dc, pm=pm: e.tensor_tensor(
                        aoc[:, tl * D + dc * 512: tl * D + (dc + 1) * 512], pm[:],
                        borep[:, dc * 512:(dc + 1) * 512], ALU.add),
                        waits=[(Ts, Ts.n), (Cs, Cs.n)], inc=(V, 1))
            v_ao = V.n
            tt0 = ch * NTT
            lxc = {}
            for tt in range(NTT):
                xb = xts[tt % 2]
                plan.dma("sync", xb[:], x2_in[(tt0 + tt) * 128:(tt0 + tt + 1) * 128, :],
                         L, waits=[(V, V.n), (St, St.n), (Cs, Cs.n)])
                lxc[tt] = L.n

            def upd_x3(tt, mc, rc, tt0=tt0, lxc=lxc):
                xb = xts[tt % 2]
                plan.emit("vector", lambda e, tt=tt, mc=mc, rc=rc: e.tensor_scalar(
                    dscr[:], aoc[:, tt * D:(tt + 1) * D], mc, rc,
                    ALU.subtract, ALU.mult), waits=[(V, V.n)], inc=(V, 1))
                plan.emit("vector", lambda e, tt=tt, xb=xb: e.tensor_tensor(
                    xb[:], dscr[:], xb[:], ALU.add),
                    waits=[(L, lxc[tt])], inc=(V, 1))
                plan.dma("gpsimd", x3_o[(tt0 + tt) * 128:(tt0 + tt + 1) * 128, :],
                         xb[:], St, waits=[(V, V.n)])
            ln_tiles(lambda tt: aoc[:, tt * D:(tt + 1) * D], NTT, v_ao, upd_x3)
        plan.emit("gpsimd", lambda e: e.engine_nop(), waits=[(St, St.n)])
        if F is None:
            plan.run()
    return nc


def host_attn_inputs(inp, x2_list):
    ident = np.eye(128, dtype=np.float32)
    NDg = D // 128
    lay = lambda v, n: np.ascontiguousarray(v.reshape(n, 128).T)
    inv = 1.0 / 10000.0 ** (np.arange(0, DH, 2, dtype=np.float32) / DH)
    selk = np.zeros((128, 128), np.float32)
    selv = np.zeros((128, 128), np.float32)
    for mm in range(128):
        selk[mm % 64, mm] = 1.0
        selv[64 + mm % 64, mm] = 1.0
    red = np.zeros((128, 8 * 16), np.float32)
    for qt in range(8):
        for j in range(128):
            red[j, qt * 16 + 2 * qt + j // 64] = 1.0
    am = np.zeros((16, 8 * 128), np.float32)
    for qt in range(8):
        for j in range(128):
            am[2 * qt + j // 64, qt * 128 + j] = 1.0
    maps = []
    for c in range(NCORE):
        b, m = c // 2, c % 2
        t0 = m * TL
        t = np.arange(t0, t0 + TL, dtype=np.float32)
        fr = t[None, :] * inv[:, None]
        cos = np.cos(fr).astype(np.float32)
        sin = np.sin(fr).astype(np.float32)
        cosr = np.ascontiguousarray(np.tile(cos, (4, 1)))
        sinr = np.ascontiguousarray(np.concatenate([-sin, sin, -sin, sin], 0))
        maps.append({
            "x2_in": np.ascontiguousarray(x2_list[c]),
            "wq_in": inp["L1_attn_Wq"],
            "bq_in": lay(inp["L1_attn_bq"], NDg),
            "wd_in": inp["L1_attn_Wdown"],
            "bd_in": lay(inp["L1_attn_bdown"], 2),
            "wu_in": inp["L1_attn_Wup"],
            "bu_in": lay(inp["L1_attn_bup"], DI // 128),
            "wo_in": inp["L1_attn_Wout"],
            "bo_in": inp["L1_attn_bout"].reshape(1, D),
            "cosr_in": cosr, "sinr_in": sinr,
            "selk_in": selk, "selv_in": selv, "red_in": red, "am_in": am,
            "ident": ident,
        })
    return maps


def build_ffn(upto="full", BARRIER=False, F=None):
    nc = bass.Bass("TRN2", target_bir_lowering=False, debug=False) if F is None else F["nc"]
    NT = TL // 128
    ND = D // 128
    NF = FF // 128
    TCH = 512
    NCH = TL // TCH
    NTT = TCH // 128

    if F is None:
        x3_in = nc.dram_tensor("x3_in", [TL, D], F32, kind="ExternalInput").ap()
        w1T_in = nc.dram_tensor("w1T_in", [D, FF], F32, kind="ExternalInput").ap()
        w2T_in = nc.dram_tensor("w2T_in", [D, FF], F32, kind="ExternalInput").ap()
        h1T_in = nc.dram_tensor("h1T_in", [FF, FF], F32, kind="ExternalInput").ap()
        h2T_in = nc.dram_tensor("h2T_in", [FF, FF], F32, kind="ExternalInput").ap()
        w3T_in = nc.dram_tensor("w3T_in", [FF, D], F32, kind="ExternalInput").ap()
        s1_in = nc.dram_tensor("s1_in", [128, NF], F32, kind="ExternalInput").ap()
        s2_in = nc.dram_tensor("s2_in", [128, NF], F32, kind="ExternalInput").ap()
        h1s_in = nc.dram_tensor("h1s_in", [128, NF], F32, kind="ExternalInput").ap()
        h2s_in = nc.dram_tensor("h2s_in", [128, NF], F32, kind="ExternalInput").ap()
        w3s_in = nc.dram_tensor("w3s_in", [1, D], F32, kind="ExternalInput").ap()
        ident_in = nc.dram_tensor("ident", [128, 128], F32, kind="ExternalInput").ap()
        x4_o = nc.dram_tensor("x4_o", [TL, D], F32, kind="ExternalOutput").ap()
    else:
        io = F["io"]
        x3_in, w1T_in, w2T_in, h1T_in = io["x3_in"], io["w1T_in"], io["w2T_in"], io["h1T_in"]
        h2T_in, w3T_in, s1_in, s2_in = io["h2T_in"], io["w3T_in"], io["s1_in"], io["s2_in"]
        h1s_in, h2s_in, w3s_in = io["h1s_in"], io["h2s_in"], io["w3s_in"]
        ident_in, x4_o = io["ident"], io["x4_o"]
    pfx = "" if F is None else F["pfx"]

    ctx = contextlib.ExitStack()
    with ctx:
        sb = lambda name, shape, dt: ctx.enter_context(nc.sbuf_tensor(pfx + name, shape, dt))
        ps = lambda name, shape, dt=F32: ctx.enter_context(nc.psum_tensor(pfx + name, shape, dt))

        identb = sb("identb", [128, 128], BF16)
        identf = sb("identf", [128, 128], F32)
        ones1 = sb("ones1", [1, 128], F32)
        sqs = sb("sqs", [128, D], F32)
        mcol = sb("mcol", [128, 4], F32)
        rcol = sb("rcol", [128, 4], F32)
        scol = sb("scol", [128, 4], F32)
        dscr = sb("dscr", [128, D], F32)
        xts = [sb(f"xts{i}", [128, D], F32) for i in range(2)]
        u4t = sb("u4t", [128, D], BF16)
        u4T = sb("u4T", [128, ND * TCH], BF16)
        abuf = sb("abuf", [128, NF * TCH], BF16)
        hbuf = sb("hbuf", [128, NF * TCH], BF16)
        wga = [sb(f"wga{i}", [128, 8 * 1024], BF16) for i in range(2)]
        sa = sb("sa", [128, TCH], F32)
        sbb = sb("sbb", [128, TCH], F32)
        s1c = sb("s1c", [128, NF], F32)
        s2c = sb("s2c", [128, NF], F32)
        h1c = sb("h1c", [128, NF], F32)
        h2c = sb("h2c", [128, NF], F32)
        w3rep = sb("w3rep", [128, D], F32)
        rowt = sb("rowt", [1, D], F32)
        ffc = sb("ffc", [128, NTT * D], F32)

        pff = [ps(f"pff{i}", [128, 512]) for i in range(2)]
        pmm = [ps(f"pmm{i}", [128, 512]) for i in range(4)]
        ptb = [ctx.enter_context(nc.psum_tensor(pfx + f"ptb{i}", [128, 128], BF16))
               for i in range(2)]

        plan = Plan(nc, ctx) if F is None else F["plan"]
        L = plan.sem("L")
        Lg = plan.sem("Lg")
        V = plan.sem("V")
        Cs = plan.sem("C")
        Ts = plan.sem("T")
        St = plan.sem("St")

        plan.dma("sync", s1c[:], s1_in, L)
        plan.dma("sync", s2c[:], s2_in, L)
        plan.dma("sync", h1c[:], h1s_in, L)
        plan.dma("sync", h2c[:], h2s_in, L)
        plan.dma("sync", rowt[:], w3s_in, L)
        plan.dma("sync", identf[:], ident_in, L)
        lds = L.n

        plan.emit("vector", lambda e: e.memset(ones1[:], 1.0), inc=(V, 1))
        plan.emit("vector", lambda e: e.tensor_copy(identb[:], identf[:]),
                  waits=[(L, lds)], inc=(V, 1))
        for dc in range(2):
            plan.emit("tensor", lambda e, dc=dc: e.matmul(
                pff[0][:], ones1[:], rowt[0:1, dc * 512:(dc + 1) * 512],
                start=True, stop=True),
                waits=[(L, lds), (V, V.n)], inc=(Ts, 1))
            plan.emit("vector", lambda e, dc=dc: e.tensor_copy(
                w3rep[:, dc * 512:(dc + 1) * 512], pff[0][:]),
                waits=[(Ts, Ts.n)], inc=(V, 1))

        wlast = {0: None, 1: None}
        nch_run = 1 if upto.startswith("one_") else (2 if upto.startswith("two_") else NCH)
        upto = upto.removeprefix("one_").removeprefix("two_")

        def stream(dram_ap, cols, bi, extra=()):
            if wlast[bi] is None:
                wt = list(extra)
            else:
                wt = [(Ts, wlast[bi])] + list(extra)
            kk = dram_ap.shape[0] // 128
            plan.dma("gpsimd", wga[bi][:, 0:cols].rearrange("p (k f) -> p k f", k=kk),
                     dram_ap.rearrange("(k p) f -> p k f", p=128), Lg, waits=wt)
            return Lg.n

        for ch in range(nch_run):
            tt0 = ch * NTT
            if ch > 0 and BARRIER:
                for eng in ("vector", "scalar", "tensor", "gpsimd", "sync"):
                    plan.emit(eng, lambda e: None,
                              waits=[(L, L.n), (V, V.n), (Cs, Cs.n), (Ts, Ts.n), (St, St.n)])
            tpu = {}
            for tt in range(NTT):
                xb = xts[tt % 2]
                wt = [(Ts, tpu[tt - 2])] if (tt - 2) in tpu else [(Ts, Ts.n)]
                wt = wt + [(St, St.n), (V, V.n), (Cs, Cs.n)]
                plan.dma("sync", xb[:], x3_in[(tt0 + tt) * 128:(tt0 + tt + 1) * 128, :],
                         L, waits=wt)
                lx = L.n
                plan.emit("scalar", lambda e, xb=xb: e.activation(sqs[:], xb[:], AF.Square),
                          waits=[(L, lx), (V, V.n)], inc=(Cs, 1))
                plan.emit("vector", lambda e, xb=xb: e.tensor_reduce(
                    mcol[:, 0:1], xb[:], AX.X, ALU.add), waits=[(L, lx)], inc=(V, 1))
                plan.emit("vector", lambda e: e.tensor_reduce(
                    scol[:, 0:1], sqs[:], AX.X, ALU.add), waits=[(Cs, Cs.n)], inc=(V, 1))
                plan.emit("vector", lambda e: e.tensor_scalar(
                    mcol[:, 0:1], mcol[:, 0:1], 1.0 / D, None, ALU.mult), inc=(V, 1))
                plan.emit("vector", lambda e: e.tensor_scalar(
                    scol[:, 0:1], scol[:, 0:1], 1.0 / D, None, ALU.mult), inc=(V, 1))
                plan.emit("vector", lambda e: e.scalar_tensor_tensor(
                    rcol[:, 0:1], mcol[:, 0:1], -1.0, mcol[:, 0:1], ALU.mult, ALU.mult),
                    inc=(V, 1))
                plan.emit("vector", lambda e: e.tensor_tensor(
                    scol[:, 0:1], scol[:, 0:1], rcol[:, 0:1], ALU.add), inc=(V, 1))
                plan.emit("vector", lambda e: e.tensor_scalar(
                    scol[:, 0:1], scol[:, 0:1], LN_EPS, None, ALU.add), inc=(V, 1))
                vv = V.n
                plan.emit("scalar", lambda e: e.activation(
                    rcol[:, 0:1], scol[:, 0:1], AF.Sqrt), waits=[(V, vv)], inc=(Cs, 1))
                plan.emit("vector", lambda e: e.reciprocal(
                    rcol[:, 0:1], rcol[:, 0:1]), waits=[(Cs, Cs.n)], inc=(V, 1))
                plan.emit("vector", lambda e, xb=xb: e.tensor_scalar(
                    u4t[:], xb[:], mcol[:, 0:1], rcol[:, 0:1], ALU.subtract, ALU.mult),
                    waits=[(Ts, Ts.n)], inc=(V, 1))
                v_u4 = V.n
                for dt in range(ND):
                    pbu = (tt * ND + dt) % 2
                    w = [(V, v_u4)]
                    if tt * ND + dt >= 2:
                        w.append((Cs, Cs.n))
                    plan.emit("tensor", lambda e, dt=dt, pbu=pbu: e.transpose(
                        ptb[pbu][:], u4t[:, dt * 128:(dt + 1) * 128], identb[:]),
                        waits=w, inc=(Ts, 1))
                    plan.emit("scalar", lambda e, tt=tt, dt=dt, pbu=pbu: e.copy(
                        u4T[:, dt * TCH + tt * 128: dt * TCH + (tt + 1) * 128],
                        ptb[pbu][:]), waits=[(Ts, Ts.n)], inc=(Cs, 1))
                tpu[tt] = Ts.n
            c_u4T = Cs.n
            for fg in range(NF // 8):
                lw1 = stream(w1T_in[:, fg * 1024:(fg + 1) * 1024], 8 * 1024, 0,
                             extra=([(V, V.n), (Cs, Cs.n)] if wlast[0] is None else []))
                lw2 = stream(w2T_in[:, fg * 1024:(fg + 1) * 1024], 8 * 1024, 1,
                             extra=([(V, V.n), (Cs, Cs.n)] if wlast[1] is None else []))
                for fi in range(8):
                    ft = fg * 8 + fi
                    for kt in range(ND):
                        w = [(Lg, lw1), (Cs, Cs.n), (V, V.n)] if kt == 0 else []
                        plan.emit("tensor", lambda e, fi=fi, kt=kt: e.matmul(
                            pff[0][:], wga[0][:, kt * 1024 + fi * 128: kt * 1024 + fi * 128 + 128],
                            u4T[:, kt * TCH:(kt + 1) * TCH],
                            start=(kt == 0), stop=(kt == ND - 1)),
                            waits=w, inc=((Ts, 1) if kt == ND - 1 else None))
                    for kt in range(ND):
                        w = [(Lg, lw2), (Cs, Cs.n)] if kt == 0 else []
                        plan.emit("tensor", lambda e, fi=fi, kt=kt: e.matmul(
                            pff[1][:], wga[1][:, kt * 1024 + fi * 128: kt * 1024 + fi * 128 + 128],
                            u4T[:, kt * TCH:(kt + 1) * TCH],
                            start=(kt == 0), stop=(kt == ND - 1)),
                            waits=w, inc=((Ts, 1) if kt == ND - 1 else None))
                    wlast[0] = wlast[1] = Ts.n
                    plan.emit("scalar", lambda e, ft=ft: e.activation(
                        sa[:], pff[0][:], AF.Silu, scale=s1c[:, ft:ft + 1]),
                        waits=[(Ts, Ts.n), (V, V.n)], inc=(Cs, 1))
                    plan.emit("scalar", lambda e, ft=ft: e.activation(
                        sbb[:], pff[1][:], AF.Identity, scale=s2c[:, ft:ft + 1]),
                        inc=(Cs, 1))
                    plan.emit("vector", lambda e, ft=ft: e.tensor_tensor(
                        abuf[:, ft * TCH:(ft + 1) * TCH], sa[:], sbb[:], ALU.mult),
                        waits=[(Cs, Cs.n), (Ts, Ts.n)], inc=(V, 1))
            v_a = V.n
            if upto == "a":
                if ch == nch_run - 1:
                    plan.dma("gpsimd", x4_o[0:128, 0:D], abuf[:, 0:D], St,
                             waits=[(V, v_a)])
                continue
            stages = ((abuf, hbuf, h1c, h1T_in), (hbuf, abuf, h2c, h2T_in))
            if upto == "h1":
                stages = stages[:1]
            for stage, (src_t, dst_t, scl, win) in enumerate(stages):
                for fg in range(NF // 2):
                    lw = stream(win[:, fg * 256:(fg + 1) * 256], 32 * 256, fg % 2,
                                extra=[(V, V.n)] if stage == 0 and fg < 2 else [])
                    for fi in range(2):
                        ft = fg * 2 + fi
                        pf = pff[fi % 2]
                        for kt in range(NF):
                            w = [(Lg, lw), (Cs, Cs.n)] if kt == 0 else []
                            plan.emit("tensor", lambda e, fi=fi, kt=kt, pf=pf,
                                      bi=fg % 2, src_t=src_t: e.matmul(
                                pf[:], wga[bi][:, kt * 256 + fi * 128: kt * 256 + fi * 128 + 128],
                                src_t[:, kt * TCH:(kt + 1) * TCH],
                                start=(kt == 0), stop=(kt == NF - 1)),
                                waits=w, inc=((Ts, 1) if (kt == NF - 1) else None))
                        plan.emit("scalar", lambda e, ft=ft, pf=pf, scl=scl,
                                  dst_t=dst_t: e.activation(
                            dst_t[:, ft * TCH:(ft + 1) * TCH], pf[:], AF.Gelu,
                            scale=scl[:, ft:ft + 1]),
                            waits=[(Ts, Ts.n)], inc=(Cs, 1))
                    wlast[fg % 2] = Ts.n
            c_h2 = Cs.n
            if upto in ("h1", "h2"):
                if ch == nch_run - 1:
                    srcb = hbuf if upto == "h1" else abuf
                    plan.dma("gpsimd", x4_o[0:128, 0:D], srcb[:, 0:D], St,
                             waits=[(Cs, c_h2)])
                    if upto == "h2":
                        plan.dma("gpsimd", x4_o[128:256, 0:D], hbuf[:, 0:D], St,
                                 waits=[(Cs, c_h2)])
                continue
            for dc in range(2):
                for kg in range(NF // 8):
                    bi = kg % 2
                    wt = [(Ts, wlast[bi])]
                    kk = 8
                    plan.dma("gpsimd",
                             wga[bi][:, 0:8 * 512].rearrange("p (k f) -> p k f", k=8),
                             w3T_in[kg * 1024:(kg + 1) * 1024, dc * 512:(dc + 1) * 512]
                             .rearrange("(k p) f -> p k f", p=128), Lg, waits=wt)
                    lw = Lg.n
                    for ki in range(8):
                        kt = kg * 8 + ki
                        for tl in range(NTT):
                            w = [(Lg, lw)] if tl == 0 else []
                            if kt == 0 and tl == 0:
                                w.append((V, V.n))
                                w.append((Cs, c_h2))
                            plan.emit("tensor", lambda e, ki=ki, kt=kt, tl=tl, bi=bi:
                                      e.matmul(
                                pmm[tl][:], abuf[:, kt * TCH + tl * 128: kt * TCH + tl * 128 + 128],
                                wga[bi][:, ki * 512:(ki + 1) * 512],
                                start=(kt == 0), stop=(kt == NF - 1)),
                                waits=w,
                                inc=((Ts, 1) if (kt == NF - 1 or
                                                 (ki == 7 and tl == NTT - 1)) else None))
                    wlast[bi] = Ts.n
                t_e = Ts.n
                for tl in range(NTT):
                    plan.emit("vector", lambda e, tl=tl, dc=dc: e.tensor_tensor(
                        ffc[:, tl * D + dc * 512: tl * D + (dc + 1) * 512], pmm[tl][:],
                        w3rep[:, dc * 512:(dc + 1) * 512], ALU.mult),
                        waits=[(Ts, t_e), (Cs, Cs.n), (St, St.n)], inc=(V, 1))
            v_ff = V.n
            if upto == "stats":
                if ch == nch_run - 1:
                    plan.dma("gpsimd", x4_o[0:128, 0:D], ffc[:, 0:D], St,
                             waits=[(V, v_ff)])
                    plan.dma("gpsimd", x4_o[128:256, 0:D], ffc[:, D:2 * D], St,
                             waits=[(V, v_ff)])
                continue
            if upto == "ff":
                if ch == nch_run - 1:
                    plan.dma("gpsimd", x4_o[0:128, 0:D], ffc[:, 0:D], St,
                             waits=[(V, v_ff)])
                continue
            def fin(tt, mc, rc, tt0=tt0):
                xb = xts[tt % 2]
                plan.dma("sync", xb[:], x3_in[(tt0 + tt) * 128:(tt0 + tt + 1) * 128, :],
                         L, waits=[(V, V.n), (St, St.n)])
                lx = L.n
                plan.emit("vector", lambda e, tt=tt, mc=mc, rc=rc: e.tensor_scalar(
                    dscr[:], ffc[:, tt * D:(tt + 1) * D], mc, rc,
                    ALU.subtract, ALU.mult), waits=[(V, V.n)], inc=(V, 1))
                plan.emit("vector", lambda e, tt=tt, xb=xb: e.tensor_tensor(
                    xb[:], dscr[:], xb[:], ALU.add),
                    waits=[(L, lx)], inc=(V, 1))
                plan.dma("gpsimd", x4_o[(tt0 + tt) * 128:(tt0 + tt + 1) * 128, :],
                         xb[:], St, waits=[(V, V.n)])

            for tt in range(NTT):
                src = ffc[:, tt * D:(tt + 1) * D]
                plan.emit("scalar", lambda e, s=src: e.activation(sqs[:], s, AF.Square),
                          waits=[(V, max(v_ff, V.n))], inc=(Cs, 1))
                plan.emit("vector", lambda e, s=src, o=mcol[:, tt:tt + 1]:
                          e.tensor_reduce(o, s, AX.X, ALU.add),
                          waits=[(V, v_ff)], inc=(V, 1))
                plan.emit("vector", lambda e, o=scol[:, tt:tt + 1]:
                          e.tensor_reduce(o, sqs[:], AX.X, ALU.add),
                          waits=[(Cs, Cs.n)], inc=(V, 1))
            plan.emit("vector", lambda e: e.tensor_scalar(
                mcol[:, 0:NTT], mcol[:, 0:NTT], 1.0 / D, None, ALU.mult),
                waits=[(V, V.n)], inc=(V, 1))
            plan.emit("vector", lambda e: e.tensor_scalar(
                scol[:, 0:NTT], scol[:, 0:NTT], 1.0 / D, None, ALU.mult),
                waits=[(V, V.n)], inc=(V, 1))
            plan.emit("vector", lambda e: e.scalar_tensor_tensor(
                rcol[:, 0:NTT], mcol[:, 0:NTT], -1.0, mcol[:, 0:NTT], ALU.mult, ALU.mult),
                waits=[(V, V.n)], inc=(V, 1))
            plan.emit("vector", lambda e: e.tensor_tensor(
                scol[:, 0:NTT], scol[:, 0:NTT], rcol[:, 0:NTT], ALU.add),
                waits=[(V, V.n)], inc=(V, 1))
            plan.emit("vector", lambda e: e.tensor_scalar(
                scol[:, 0:NTT], scol[:, 0:NTT], LN_EPS, None, ALU.add),
                waits=[(V, V.n)], inc=(V, 1))
            vv = V.n
            plan.emit("scalar", lambda e: e.activation(
                rcol[:, 0:NTT], scol[:, 0:NTT], AF.Sqrt), waits=[(V, vv)], inc=(Cs, 1))
            plan.emit("vector", lambda e: e.reciprocal(
                rcol[:, 0:NTT], rcol[:, 0:NTT]), waits=[(Cs, Cs.n)], inc=(V, 1))
            for tt in range(NTT):
                fin(tt, mcol[:, tt:tt + 1], rcol[:, tt:tt + 1])
        plan.emit("gpsimd", lambda e: e.engine_nop(), waits=[(St, St.n)])
        if F is None:
            plan.run()
    return nc


def host_ffn_inputs(inp, x3_list):
    ident = np.eye(128, dtype=np.float32)
    layc = lambda v: np.ascontiguousarray(v.reshape(-1, 128).T)
    tr = lambda w: np.ascontiguousarray(w.T)
    maps = []
    for c in range(NCORE):
        maps.append({
            "x3_in": np.ascontiguousarray(x3_list[c]),
            "w1T_in": tr(inp["L1_ffn_w1_w"]),
            "w2T_in": tr(inp["L1_ffn_w2_w"]),
            "h1T_in": tr(inp["L1_ffn_h1_w"]),
            "h2T_in": tr(inp["L1_ffn_h2_w"]),
            "w3T_in": tr(inp["L1_ffn_w3_w"]),
            "s1_in": layc(inp["L1_ffn_w1_s"]),
            "s2_in": layc(inp["L1_ffn_w2_s"]),
            "h1s_in": layc(inp["L1_ffn_h1_s"]),
            "h2s_in": layc(inp["L1_ffn_h2_s"]),
            "w3s_in": inp["L1_ffn_w3_s"].reshape(1, D),
            "ident": ident,
        })
    return maps


def build_bae():
    nc = bass.Bass("TRN2", target_bir_lowering=False, debug=False)
    NF = FF // 128
    ND = D // 128
    I = lambda n, s, dt=F32: nc.dram_tensor(n, s, dt, kind="ExternalInput").ap()
    io = {
        "x_in": I("x_in", [TL, D]),
        "ybp_in": I("ybp_in", [DI, TL], BF16),
        "zs_in": I("zs_in", [DI, TL], BF16),
        "wout_in": I("wout_in", [DI, D]),
        "w1_in": I("w1_in", [D, FF]),
        "b1_in": I("b1_in", [128, NF]),
        "w2_in": I("w2_in", [FF, D]),
        "b2_in": I("b2_in", [1, D]),
        "gam_in": I("gam_in", [1, D]),
        "bet_in": I("bet_in", [1, D]),
        "ident": I("ident", [128, 128]),
        "wq_in": I("wq_in", [D, D]),
        "bq_in": I("bq_in", [128, ND]),
        "wd_in": I("wd_in", [D, 256]),
        "bd_in": I("bd_in", [128, 2]),
        "wu_in": I("wu_in", [256, DI]),
        "bu_in": I("bu_in", [128, DI // 128]),
        "wo_in": I("wo_in", [D, D]),
        "bo_in": I("bo_in", [1, D]),
        "cosr_in": I("cosr_in", [128, TL]),
        "sinr_in": I("sinr_in", [128, TL]),
        "selk_in": I("selk_in", [128, 128]),
        "selv_in": I("selv_in", [128, 128]),
        "red_in": I("red_in", [128, 8 * 16]),
        "am_in": I("am_in", [16, 8 * 128]),
        "w1T_in": I("w1T_in", [D, FF]),
        "w2T_in": I("w2T_in", [D, FF]),
        "h1T_in": I("h1T_in", [FF, FF]),
        "h2T_in": I("h2T_in", [FF, FF]),
        "w3T_in": I("w3T_in", [FF, D]),
        "s1_in": I("s1_in", [128, NF]),
        "s2_in": I("s2_in", [128, NF]),
        "h1s_in": I("h1s_in", [128, NF]),
        "h2s_in": I("h2s_in", [128, NF]),
        "w3s_in": I("w3s_in", [1, D]),
    }
    io["x2_in"] = io["x2_o"] = nc.dram_tensor("x2_h", [TL, D], F32, kind="Internal").ap()
    io["x3_in"] = io["x3_o"] = nc.dram_tensor("x3_h", [TL, D], F32, kind="Internal").ap()
    io["x4_o"] = nc.dram_tensor("x4_o", [TL, D], F32, kind="ExternalOutput").ap()
    ctx = contextlib.ExitStack()
    with ctx:
        plan = Plan(nc, ctx)
        build_back(F={"nc": nc, "plan": plan, "io": io, "pfx": "bk_"})
        plan.barrier()
        build_attn(F={"nc": nc, "plan": plan, "io": io, "pfx": "at_"})
        plan.barrier()
        build_ffn(F={"nc": nc, "plan": plan, "io": io, "pfx": "ff_"})
        plan.run()
    return nc


def host_bae_inputs(inp, front_results, scan_results):
    mb = host_back_inputs(inp, front_results, scan_results)
    dummy = [np.zeros((1, 1), np.float32)] * NCORE
    ma = host_attn_inputs(inp, dummy)
    mf = host_ffn_inputs(inp, dummy)
    maps = []
    for c in range(NCORE):
        m = {**mb[c], **ma[c], **mf[c]}
        m.pop("x2_in")
        m.pop("x3_in")
        maps.append(m)
    return maps


_NC_CACHE = {}


def _get(name, builder):
    if name not in _NC_CACHE:
        _NC_CACHE[name] = builder()
    return _NC_CACHE[name]


def kernel(**inputs):
    cores = list(range(NCORE))
    inp = inputs

    nc1 = _get("front", build_front)
    r1 = run_bass_kernel_spmd(nc1, host_front_inputs(inp), core_ids=cores).results

    nc2 = _get("scan", build_scan)
    r2 = run_bass_kernel_spmd(nc2, host_scan_inputs(inp, r1), core_ids=cores).results
    ybar = [r2[c]["ybar_o"] for c in range(NCORE)]

    nc3 = _get("bae", build_bae)
    r3 = run_bass_kernel_spmd(nc3, host_bae_inputs(inp, r1, ybar), core_ids=cores).results

    out = np.empty((B, T, D), np.float32)
    for c in range(NCORE):
        b, m = c // 2, c % 2
        out[b, m * TL:(m + 1) * TL] = r3[c]["x4_o"]
    return out


# version 05 (final)
# speedup vs baseline: 1.0510x; 1.0510x over previous
import contextlib
import numpy as np
import concourse.bass as bass
import concourse.mybir as mybir
from concourse.bass_utils import run_bass_kernel_spmd

F32 = mybir.dt.float32
BF16 = mybir.dt.bfloat16
AF = mybir.ActivationFunctionType
ALU = mybir.AluOpType
AX = mybir.AxisListType

B, T, D, DI, S, RNK, H, DH, FF = 4, 4096, 1024, 2048, 16, 64, 16, 64, 4096
TL = 2048
NCORE = 8
LN_EPS = 1e-5


class Sem:
    __slots__ = ("h", "n")

    def __init__(self, h):
        self.h = h
        self.n = 0


DEBUG_SELF_EDGES = False


class Plan:
    ENGINES = ("sync", "vector", "scalar", "tensor", "gpsimd")

    def __init__(self, nc, ctx):
        self.nc = nc
        self.ctx = ctx
        self.ops = {e: [] for e in self.ENGINES}
        self.esem = {}
        self._sems = {}
        self._pending = {}

    def sem(self, name):
        if name not in self._sems:
            self._sems[name] = Sem(self.ctx.enter_context(self.nc.semaphore(name)))
        return self._sems[name]

    def barrier(self):
        snap = [(s, s.n) for s in self._sems.values()]
        for e in self.ENGINES:
            self._pending[e] = snap

    def on(self, eng, fn):
        self.ops[eng].append(fn)

    def emit(self, eng, builder, waits=(), inc=None):
        pend = self._pending.pop(eng, None)
        w = [(s.h, v) for (s, v) in pend] if pend else []
        w += [(s.h, v) for (s, v) in waits]
        if DEBUG_SELF_EDGES:
            li = self.esem.get(eng)
            if li is not None:
                w.append((li[0].h, li[1]))
        if inc is not None:
            inc[0].n += inc[1]
            tgt = (inc[0].h, inc[1])
        else:
            tgt = None

        if DEBUG_SELF_EDGES and inc is not None:
            self.esem[eng] = (inc[0], inc[0].n)

        def fn(e, w=w, tgt=tgt, builder=builder):
            for (sh, v) in w:
                e.wait_ge(sh, v)
            inst = builder(e)
            if tgt is not None:
                assert inst is not None
                inst.then_inc(*tgt)

        self.on(eng, fn)

    def dma(self, eng, out, in_, sem, waits=()):
        self.emit(eng, lambda e, o=out, i=in_: e.dma_start(out=o, in_=i),
                  waits=waits, inc=(sem, 16))

    def barrier_waits(self, *sems):
        return [(s, s.n) for s in sems]

    def run(self):
        block = self.ctx.enter_context(self.nc.Block())
        for e in self.ENGINES:
            fns = self.ops[e]
            if fns:
                getattr(block, e)(lambda engine, _f=tuple(fns): [f(engine) for f in _f])


def bc0(ap2d, reps, inner=None):
    p = list(ap2d.ap[0])
    f = list(ap2d.ap[1])
    if inner is None:
        inner = f[1]
    return bass.AP(tensor=ap2d.tensor, offset=ap2d.offset,
                   ap=[p, [0, reps], [f[0], inner]])


def build_front():
    nc = bass.Bass("TRN2", target_bir_lowering=False, debug=False)
    xin = nc.dram_tensor("xin", [TL, D], F32, kind="ExternalInput").ap()
    xhalo = nc.dram_tensor("xhalo", [128, D], F32, kind="ExternalInput").ap()
    w_in = nc.dram_tensor("w_in", [D, 2 * DI], F32, kind="ExternalInput").ap()
    convw = nc.dram_tensor("convw", [128, (DI // 128) * 4], F32, kind="ExternalInput").ap()
    convb = nc.dram_tensor("convb", [128, DI // 128], F32, kind="ExternalInput").ap()
    wx = nc.dram_tensor("wx", [DI, 96], F32, kind="ExternalInput").ap()
    ident_in = nc.dram_tensor("ident", [128, 128], F32, kind="ExternalInput").ap()

    xconv_o = nc.dram_tensor("xconv_o", [DI, TL], BF16, kind="ExternalOutput").ap()
    zs_o = nc.dram_tensor("zs_o", [DI, TL], BF16, kind="ExternalOutput").ap()
    xdbl_o = nc.dram_tensor("xdbl_o", [96, TL], F32, kind="ExternalOutput").ap()

    NT = TL // 128
    ND = D // 128
    NC_ = DI // 128
    CH = 4
    CW = TL // CH
    UTW = TL + 128

    ctx = contextlib.ExitStack()
    with ctx:
        sb = lambda name, shape, dt: ctx.enter_context(nc.sbuf_tensor(name, shape, dt))
        ps = lambda name, shape, dt=F32: ctx.enter_context(nc.psum_tensor(name, shape, dt))

        identf = sb("identf", [128, 128], F32)
        identb = sb("identb", [128, 128], BF16)
        mcol = sb("mcol", [128, 17], F32)
        rcol = sb("rcol", [128, 17], F32)
        scol = sb("scol", [128, 17], F32)
        ub = sb("ub", [128, NT * D], BF16)
        uhb = sb("uhb", [128, D], BF16)
        uT = sb("uT", [128, ND * UTW], BF16)
        winb = sb("winb", [128, ND * DI], BF16)
        wxb = sb("wxb", [128, NC_ * 96], BF16)
        cwb = sb("cwb", [128, NC_ * 4], F32)
        cbb = sb("cbb", [128, NC_], F32)
        cbuf = sb("cbuf", [128, 8 + CW], F32)
        cva = sb("cva", [128, CW], F32)
        cvb = sb("cvb", [128, CW], F32)
        halo3 = sb("halo3", [128, NC_ * 4], F32)
        zst = sb("zst", [128, TL], BF16)
        xdbl_sb = sb("xdbl_sb", [96, TL], F32)
        g_xi = nc.sbuf_tensor("xi", [128, NT * D], F32, side="right")
        xi = g_xi.__enter__()
        g_xh = nc.sbuf_tensor("xh", [128, D], F32, side="right")
        xh = g_xh.__enter__()
        g_sqs = nc.sbuf_tensor("sqs", [128, 2 * D], F32, side="right")
        sqs = g_sqs.__enter__()

        ptb = [ctx.enter_context(nc.psum_tensor(f"ptb{i}", [128, 128], BF16)) for i in range(2)]
        pxz = [ps(f"pxz{i}", [128, CW]) for i in range(2)]
        pxh = ps("pxh", [128, 128])
        pxd = [ps(f"pxd{i}", [96, CW]) for i in range(2)]

        plan = Plan(nc, ctx)
        L = plan.sem("L")
        Lg = plan.sem("Lg")
        V = plan.sem("V")
        Cs = plan.sem("C")
        Ts = plan.sem("T")
        St = plan.sem("St")

        plan.dma("sync", xi[:].rearrange("p (n d) -> p n d", n=NT),
                 xin.rearrange("(n p) d -> p n d", p=128), L)
        plan.dma("sync", xh[:], xhalo, L)
        plan.dma("sync", identf[:], ident_in, L)
        plan.dma("gpsimd", winb[:].rearrange("p (k f) -> p k f", k=ND),
                 w_in[:, 0:DI].rearrange("(k p) f -> p k f", p=128), Lg)
        plan.dma("gpsimd", wxb[:].rearrange("p (k f) -> p k f", k=NC_),
                 wx.rearrange("(k p) f -> p k f", p=128), Lg)
        plan.dma("gpsimd", cwb[:], convw, Lg)
        plan.dma("gpsimd", cbb[:], convb, Lg)
        lds = L.n
        lgs = Lg.n
        for _eng in ("vector", "scalar", "tensor"):
            plan.emit(_eng, lambda e: None, waits=[(Lg, lgs)])

        plan.emit("vector", lambda e: e.tensor_copy(identb[:], identf[:]),
                  waits=[(L, lds)], inc=(V, 1))

        for tt in range(NT + 1):
            src = xi[:, tt * D:(tt + 1) * D] if tt < NT else xh[:]
            plan.emit("scalar",
                      lambda e, s=src, o=sqs[:, (tt % 2) * D:(tt % 2 + 1) * D]:
                      e.activation(o, s, AF.Square),
                      waits=[(L, lds), (V, V.n)], inc=(Cs, 1))
            plan.emit("vector",
                      lambda e, s=src, o=mcol[:, tt:tt + 1]:
                      e.tensor_reduce(o, s, AX.X, ALU.add),
                      waits=[(L, lds)], inc=(V, 1))
            plan.emit("vector",
                      lambda e, s=sqs[:, (tt % 2) * D:(tt % 2 + 1) * D], o=scol[:, tt:tt + 1]:
                      e.tensor_reduce(o, s, AX.X, ALU.add),
                      waits=[(Cs, Cs.n)], inc=(V, 1))
        plan.emit("vector", lambda e: e.tensor_scalar(mcol[:, 0:17], mcol[:, 0:17],
                                                      1.0 / D, None, ALU.mult),
                  waits=[(V, V.n)], inc=(V, 1))
        plan.emit("vector", lambda e: e.tensor_scalar(scol[:, 0:17], scol[:, 0:17],
                                                      1.0 / D, None, ALU.mult),
                  waits=[(V, V.n)], inc=(V, 1))
        plan.emit("vector", lambda e: e.scalar_tensor_tensor(
            rcol[:, 0:17], mcol[:, 0:17], -1.0, mcol[:, 0:17], ALU.mult, ALU.mult),
            waits=[(V, V.n)], inc=(V, 1))
        plan.emit("vector", lambda e: e.tensor_tensor(scol[:, 0:17], scol[:, 0:17],
                                                      rcol[:, 0:17], ALU.add),
                  waits=[(V, V.n)], inc=(V, 1))
        plan.emit("vector", lambda e: e.tensor_scalar(scol[:, 0:17], scol[:, 0:17],
                                                      LN_EPS, None, ALU.add),
                  waits=[(V, V.n)], inc=(V, 1))
        vstat = V.n
        plan.emit("scalar", lambda e: e.activation(rcol[:, 0:17], scol[:, 0:17],
                                                   AF.Sqrt),
                  waits=[(V, vstat)], inc=(Cs, 1))
        plan.emit("vector", lambda e: e.reciprocal(rcol[:, 0:17], rcol[:, 0:17]),
                  waits=[(Cs, Cs.n)], inc=(V, 1))
        for tt in range(NT + 1):
            src = xi[:, tt * D:(tt + 1) * D] if tt < NT else xh[:]
            dst = ub[:, tt * D:(tt + 1) * D] if tt < NT else uhb[:]
            plan.emit("vector",
                      lambda e, s=src, o=dst, t=tt: e.tensor_scalar(
                          o, s, mcol[:, t:t + 1], rcol[:, t:t + 1], ALU.subtract, ALU.mult),
                      inc=(V, 1))
        v_u = V.n

        n_tp = 0
        for dt in range(ND):
            for tt in range(NT + 1):
                src = (ub[:, tt * D + dt * 128: tt * D + dt * 128 + 128] if tt < NT
                       else uhb[:, dt * 128:dt * 128 + 128])
                dstcol = dt * UTW + (128 + tt * 128 if tt < NT else 0)
                pb = ptb[n_tp % 2]
                waits = [(V, v_u)]
                if n_tp >= 2:
                    waits.append((Cs, Cs.n))
                plan.emit("tensor", lambda e, s=src, p=pb: e.transpose(p[:], s, identb[:]),
                          waits=waits, inc=(Ts, 1))
                plan.emit("scalar", lambda e, p=pb, dc=dstcol: e.copy(uT[:, dc:dc + 128], p[:]),
                          waits=[(Ts, Ts.n)], inc=(Cs, 1))
                n_tp += 1
        v_uT_T, v_uT_C = Ts.n, Cs.n
        g_sqs.__exit__(None, None, None)
        g_xh.__exit__(None, None, None)
        g_xi.__exit__(None, None, None)
        xcv = sb("xcv", [128, NC_ * TL], BF16)

        for ct in range(NC_):
            for kt in range(ND):
                plan.emit("tensor", lambda e, ct=ct, kt=kt: e.matmul(
                    pxh[:], winb[:, kt * DI + ct * 128: kt * DI + ct * 128 + 128],
                    uT[:, kt * UTW: kt * UTW + 128],
                    start=(kt == 0), stop=(kt == ND - 1)),
                    waits=([(Cs, v_uT_C), (V, V.n)] if kt == 0 else []),
                    inc=((Ts, 1) if kt == ND - 1 else None))
            plan.emit("vector", lambda e, ct=ct: e.tensor_copy(
                halo3[:, ct * 4: ct * 4 + 3], pxh[:, 125:128]),
                waits=[(Ts, Ts.n)], inc=(V, 1))
        v_halo = V.n

        nxz = 0
        for ct in range(NC_):
            for chk in range(CH):
                pz = pxz[nxz % 2]
                for kt in range(ND):
                    waits = []
                    if kt == 0:
                        waits = [(V, V.n), (Cs, Cs.n)]
                    plan.emit("tensor", lambda e, ct=ct, kt=kt, chk=chk, pz=pz: e.matmul(
                        pz[:], winb[:, kt * DI + ct * 128: kt * DI + ct * 128 + 128],
                        uT[:, kt * UTW + 128 + chk * CW: kt * UTW + 128 + (chk + 1) * CW],
                        start=(kt == 0), stop=(kt == ND - 1)),
                        waits=waits, inc=((Ts, 1) if kt == ND - 1 else None))
                t_mm = Ts.n
                if chk == 0:
                    plan.emit("vector", lambda e, ct=ct: e.tensor_copy(
                        cbuf[:, 5:8], halo3[:, ct * 4:ct * 4 + 3]),
                        waits=[(V, v_halo)], inc=(V, 1))
                plan.emit("vector", lambda e, pz=pz: e.tensor_copy(cbuf[:, 8:8 + CW], pz[:]),
                          waits=[(Ts, t_mm), (Cs, Cs.n)], inc=(V, 1))
                plan.emit("vector", lambda e, ct=ct: e.tensor_scalar(
                    cva[:], cbuf[:, 5:5 + CW], cwb[:, ct * 4:ct * 4 + 1], None, ALU.mult),
                    inc=(V, 1))
                plan.emit("vector", lambda e, ct=ct: e.scalar_tensor_tensor(
                    cvb[:], cbuf[:, 6:6 + CW], cwb[:, ct * 4 + 1:ct * 4 + 2], cva[:],
                    ALU.mult, ALU.add), inc=(V, 1))
                plan.emit("vector", lambda e, ct=ct: e.scalar_tensor_tensor(
                    cva[:], cbuf[:, 7:7 + CW], cwb[:, ct * 4 + 2:ct * 4 + 3], cvb[:],
                    ALU.mult, ALU.add), inc=(V, 1))
                plan.emit("vector", lambda e, ct=ct: e.scalar_tensor_tensor(
                    cvb[:], cbuf[:, 8:8 + CW], cwb[:, ct * 4 + 3:ct * 4 + 4], cva[:],
                    ALU.mult, ALU.add), inc=(V, 1))
                plan.emit("vector", lambda e: e.tensor_copy(cbuf[:, 5:8],
                                                            cbuf[:, 5 + CW:8 + CW]),
                          inc=(V, 1))
                plan.emit("scalar", lambda e, ct=ct, chk=chk: e.activation(
                    xcv[:, ct * TL + chk * CW: ct * TL + (chk + 1) * CW], cvb[:],
                    AF.Silu, bias=cbb[:, ct:ct + 1]),
                    waits=[(V, V.n)], inc=(Cs, 1))
                nxz += 1
            plan.dma("gpsimd", xconv_o[ct * 128:(ct + 1) * 128, :],
                     xcv[:, ct * TL:(ct + 1) * TL], St, waits=[(Cs, Cs.n)])
        plan.dma("gpsimd", winb[:].rearrange("p (k f) -> p k f", k=ND),
                 w_in[:, DI:2 * DI].rearrange("(k p) f -> p k f", p=128), Lg,
                 waits=[(Ts, Ts.n)])
        l_z = Lg.n
        for ct in range(NC_):
            for chk in range(CH):
                pz = pxz[nxz % 2]
                for kt in range(ND):
                    waits = [] if kt else [(Cs, Cs.n), (V, V.n), (Lg, l_z)]
                    plan.emit("tensor", lambda e, ct=ct, kt=kt, chk=chk, pz=pz: e.matmul(
                        pz[:], winb[:, kt * DI + ct * 128: kt * DI + ct * 128 + 128],
                        uT[:, kt * UTW + 128 + chk * CW: kt * UTW + 128 + (chk + 1) * CW],
                        start=(kt == 0), stop=(kt == ND - 1)),
                        waits=waits, inc=((Ts, 1) if kt == ND - 1 else None))
                zw = [(Ts, Ts.n)] + ([(St, St.n)] if chk == 0 else [])
                plan.emit("scalar", lambda e, pz=pz, chk=chk: e.activation(
                    zst[:, chk * CW:(chk + 1) * CW], pz[:], AF.Silu),
                    waits=zw, inc=(Cs, 1))
                nxz += 1
            plan.dma("gpsimd", zs_o[ct * 128:(ct + 1) * 128, :],
                     zst[:], St, waits=[(Cs, Cs.n)])
        for chk in range(CH):
            pd = pxd[chk % 2]
            for kt in range(NC_):
                waits = []
                if kt == 0:
                    waits = [(Cs, Cs.n), (V, V.n)]
                plan.emit("tensor", lambda e, kt=kt, chk=chk, pd=pd: e.matmul(
                    pd[:], wxb[:, kt * 96:(kt + 1) * 96],
                    xcv[:, kt * TL + chk * CW: kt * TL + (chk + 1) * CW],
                    start=(kt == 0), stop=(kt == NC_ - 1)),
                    waits=waits, inc=((Ts, 1) if kt == NC_ - 1 else None))
            plan.emit("vector", lambda e, pd=pd, chk=chk: e.tensor_copy(
                xdbl_sb[:, chk * CW:(chk + 1) * CW], pd[:]),
                waits=[(Ts, Ts.n)], inc=(V, 1))
        plan.dma("gpsimd", xdbl_o, xdbl_sb[:], St, waits=[(V, V.n)])

        plan.emit("gpsimd", lambda e: e.engine_nop(), waits=[(St, St.n)])
        plan.run()
    return nc


def host_front_inputs(inp):
    x = np.ascontiguousarray(inp["L0_attn_x"])
    ident = np.eye(128, dtype=np.float32)
    convw = np.ascontiguousarray(
        inp["L0_attn_conv_w"].reshape(DI // 128, 128, 4).transpose(1, 0, 2).reshape(128, -1))
    convb_l = np.ascontiguousarray(
        inp["L0_attn_conv_b"].reshape(DI // 128, 128).T)
    maps = []
    for c in range(NCORE):
        b, m = c // 2, c % 2
        t0 = m * TL
        halo = np.zeros((128, D), np.float32)
        if m == 1:
            halo[125:128] = x[b, t0 - 3:t0]
        maps.append({
            "xin": np.ascontiguousarray(x[b, t0:t0 + TL]),
            "xhalo": halo,
            "w_in": inp["L0_attn_W_in"],
            "convw": convw,
            "convb": convb_l,
            "wx": inp["L0_attn_W_x"],
            "ident": ident,
        })
    return maps


def build_scan(nch_run=None):
    nc = bass.Bass("TRN2", target_bir_lowering=False, debug=False)
    CL = 1024
    NCT = CL // 128
    TC = 512
    NCH = T // TC
    if nch_run is None:
        nch_run = NCH

    xconv_c = nc.dram_tensor("xconv_c", [CL, T], BF16, kind="ExternalInput").ap()
    xdbl = nc.dram_tensor("xdbl", [64, T], F32, kind="ExternalInput").ap()
    xdbl_bc = nc.dram_tensor("xdbl_bc", [T // 512, 32 * 512], F32, kind="ExternalInput").ap()
    wdt = nc.dram_tensor("wdt", [64, CL], F32, kind="ExternalInput").ap()
    bdt = nc.dram_tensor("bdt", [128, NCT], F32, kind="ExternalInput").ap()
    alog = nc.dram_tensor("alog", [128, NCT * S], F32, kind="ExternalInput").ap()
    dpar = nc.dram_tensor("dpar", [128, NCT], F32, kind="ExternalInput").ap()
    ident_in = nc.dram_tensor("ident", [128, 128], F32, kind="ExternalInput").ap()
    ybar_o = nc.dram_tensor("ybar_o", [CL, T], BF16, kind="ExternalOutput").ap()

    ctx = contextlib.ExitStack()
    with ctx:
        sb = lambda name, shape, dt: ctx.enter_context(nc.sbuf_tensor(name, shape, dt))
        ps = lambda name, shape, dt=F32: ctx.enter_context(nc.psum_tensor(name, shape, dt))

        xcb = sb("xcb", [128, NCT * T], BF16)
        xdb = sb("xdb", [64, T], F32)
        bcrow = sb("bcrow", [1, 32 * TC], BF16)
        wdtb = sb("wdtb", [64, CL], F32)
        bdtb = sb("bdtb", [128, NCT], F32)
        acoef = sb("acoef", [128, NCT * S], F32)
        dparb = sb("dparb", [128, NCT], F32)
        identf = sb("identf", [128, 128], F32)
        identb = sb("identb", [128, 128], BF16)
        ones1b = sb("ones1b", [1, 128], BF16)
        dtc = sb("dtc", [128, NCT * TC], F32)
        vc = sb("vc", [128, NCT * TC], F32)
        brep = sb("brep", [128, S * TC], BF16)
        crep = sb("crep", [128, S * TC], BF16)
        bt = [sb(f"bt{i}", [128, TC], F32) for i in range(4)]
        dA = [sb(f"dA{i}", [128, TC], F32) for i in range(4)]
        hbuf = [sb(f"hbuf{i}", [128, TC], F32) for i in range(2)]
        hc = [sb(f"hc{i}", [128, TC], BF16) for i in range(2)]
        hcar = sb("hcar", [128, NCT * S], F32)
        ybst = [sb(f"ybst{i}", [128, TC], BF16) for i in range(2)]

        pdt = [ps(f"pdt{i}", [128, TC]) for i in range(2)]
        pbc = [ps(f"pbc{i}", [128, TC]) for i in range(2)]
        py = [ps(f"py{i}", [128, TC]) for i in range(2)]

        plan = Plan(nc, ctx)
        L = plan.sem("L")
        Lg = plan.sem("Lg")
        V = plan.sem("V")
        Cs = plan.sem("C")
        Ts = plan.sem("T")
        G = plan.sem("G")
        St = plan.sem("St")

        plan.dma("sync", xcb[:].rearrange("p (k t) -> p k t", k=NCT),
                 xconv_c.rearrange("(k p) t -> p k t", p=128), L)
        plan.dma("sync", xdb[:], xdbl, L)
        plan.dma("sync", identf[:], ident_in, L)
        plan.dma("sync", bdtb[:], bdt, L)
        plan.dma("sync", acoef[:], alog, L)
        plan.dma("sync", dparb[:], dpar, L)
        plan.dma("sync", wdtb[:], wdt, L)
        lds = L.n

        plan.emit("vector", lambda e: e.memset(ones1b[:], 1.0), inc=(V, 1))
        plan.emit("vector", lambda e: e.memset(hcar[:], 0.0), inc=(V, 1))
        plan.emit("vector", lambda e: e.tensor_copy(identb[:], identf[:]),
                  waits=[(L, lds)], inc=(V, 1))
        plan.emit("scalar", lambda e: e.activation(acoef[:], acoef[:], AF.Exp),
                  waits=[(L, lds)], inc=(Cs, 1))
        plan.emit("vector", lambda e: e.tensor_scalar(acoef[:], acoef[:], -1.0,
                                                      None, ALU.mult),
                  waits=[(Cs, Cs.n)], inc=(V, 1))

        it = 0
        ctg = 0
        vscan = {}
        c_dA = {}
        c_carry = {}
        g_b = {}
        v_hc = {}
        t_y = {}
        v_fin = {}
        st_dma = {}
        for ch in range(nch_run):
            c0 = ch * TC
            for ct in range(NCT):
                pd = pdt[ct % 2]
                plan.emit("tensor", lambda e, ct=ct, pd=pd, c0=c0: e.matmul(
                    pd[:], wdtb[:, ct * 128:(ct + 1) * 128], xdb[0:64, c0:c0 + TC],
                    start=True, stop=True),
                    waits=[(L, lds), (Cs, Cs.n)], inc=(Ts, 1))
                plan.emit("scalar", lambda e, ct=ct, pd=pd: e.activation(
                    dtc[:, ct * TC:(ct + 1) * TC], pd[:], AF.Sigmoid,
                    bias=bdtb[:, ct:ct + 1]),
                    waits=[(Ts, Ts.n), (V, V.n)], inc=(Cs, 1))
                plan.emit("vector", lambda e, ct=ct: e.tensor_scalar(
                    dtc[:, ct * TC:(ct + 1) * TC], dtc[:, ct * TC:(ct + 1) * TC],
                    0.099, 0.001, ALU.mult, ALU.add),
                    waits=[(Cs, Cs.n), (G, G.n)], inc=(V, 1))
                plan.emit("vector", lambda e, ct=ct, c0=c0: e.tensor_tensor(
                    vc[:, ct * TC:(ct + 1) * TC], dtc[:, ct * TC:(ct + 1) * TC],
                    xcb[:, ct * T + c0: ct * T + c0 + TC], ALU.mult),
                    inc=(V, 1))
            v_dtv = V.n
            plan.dma("gpsimd", bcrow[:], xdbl_bc[ch:ch + 1, :], Lg,
                     waits=[(Ts, Ts.n)])
            l_bc = Lg.n
            for s in range(2 * S):
                pb = pbc[s % 2]
                plan.emit("tensor", lambda e, s=s, pb=pb: e.matmul(
                    pb[:], ones1b[:], bcrow[0:1, s * TC:(s + 1) * TC],
                    start=True, stop=True),
                    waits=[(V, V.n), (Cs, Cs.n), (G, G.n), (Lg, l_bc)], inc=(Ts, 1))
                if s < S:
                    plan.emit("vector", lambda e, s=s, pb=pb: e.tensor_copy(
                        brep[:, s * TC:(s + 1) * TC], pb[:]),
                        waits=[(Ts, Ts.n)], inc=(V, 1))
                else:
                    plan.emit("scalar", lambda e, s=s, pb=pb: e.copy(
                        crep[:, (s - S) * TC:(s - S + 1) * TC], pb[:]),
                        waits=[(Ts, Ts.n), (V, V.n)], inc=(Cs, 1))
            v_bc = V.n
            c_bc = Cs.n
            it0 = ch * NCT * S

            def emit_dA(itl, it0=it0, v_dtv=v_dtv):
                ct_, s_ = divmod(itl, S)
                gi = it0 + itl
                wv_ = v_dtv
                if gi - 3 in vscan:
                    wv_ = max(wv_, vscan[gi - 3])
                plan.emit("scalar", lambda e, ct=ct_, s=s_, r=gi % 4: e.activation(
                    dA[r][:], dtc[:, ct * TC:(ct + 1) * TC], AF.Exp,
                    scale=acoef[:, ct * S + s:ct * S + s + 1]),
                    waits=[(V, wv_)], inc=(Cs, 1))
                c_dA[gi] = Cs.n

            emit_dA(0)
            for ct in range(NCT):
                pyt = py[ct % 2]
                for s in range(S):
                    r = it % 4
                    wv = max(v_dtv, v_bc)
                    if it - 4 in vscan:
                        wv = max(wv, vscan[it - 4])
                    plan.emit("gpsimd", lambda e, ct=ct, s=s, r=r: e.tensor_tensor(
                        bt[r][:], vc[:, ct * TC:(ct + 1) * TC],
                        brep[:, s * TC:(s + 1) * TC], ALU.mult),
                        waits=[(V, wv)], inc=(G, 1))
                    g_b[it] = G.n
                    wc = c_dA[it]
                    if it - 2 in c_carry:
                        wc = max(wc, c_carry[it - 2])
                    plan.emit("vector", lambda e, ct=ct, s=s, r=r, hb=hbuf[it % 2]:
                              e.tensor_tensor_scan(
                                  hb[:], dA[r][:], bt[r][:],
                                  hcar[:, ct * S + s:ct * S + s + 1],
                                  ALU.mult, ALU.add),
                              waits=[(Cs, wc), (G, g_b[it])], inc=(V, 1))
                    vscan[it] = V.n
                    wts = t_y.get(it - 2, 0)
                    plan.emit("vector", lambda e, s=s, r=it % 2, hb=hbuf[it % 2]:
                              e.tensor_tensor(
                                  hc[r][:], hb[:], crep[:, s * TC:(s + 1) * TC],
                                  ALU.mult),
                              waits=[(Ts, wts)], inc=(V, 1))
                    v_hc[it] = V.n
                    if ct * S + s + 1 < NCT * S:
                        emit_dA(ct * S + s + 1)
                    plan.emit("scalar", lambda e, ct=ct, s=s, hb=hbuf[it % 2]: e.copy(
                        hcar[:, ct * S + s:ct * S + s + 1], hb[:, TC - 1:TC]),
                        waits=[(V, v_hc[it])], inc=(Cs, 1))
                    c_carry[it] = Cs.n
                    wv = v_hc[it]
                    if s == 0 and ctg - 2 in v_fin:
                        wv = max(wv, v_fin[ctg - 2])
                    plan.emit("tensor", lambda e, s=s, r=it % 2, pyt=pyt: e.matmul(
                        pyt[:], identb[:], hc[r][:],
                        start=(s == 0), stop=(s == S - 1)),
                        waits=[(V, wv)], inc=(Ts, 1))
                    t_y[it] = Ts.n
                    it += 1
                wst = st_dma.get(ctg - 2, 0)
                plan.emit("vector", lambda e, ct=ct, c0=c0, pyt=pyt, rg=ctg % 2:
                          e.scalar_tensor_tensor(
                              ybst[rg][:], xcb[:, ct * T + c0: ct * T + c0 + TC],
                              dparb[:, ct:ct + 1], pyt[:], ALU.mult, ALU.add),
                          waits=[(Ts, t_y[it - 1]), (St, wst)], inc=(V, 1))
                v_fin[ctg] = V.n
                plan.dma("gpsimd", ybar_o[ct * 128:(ct + 1) * 128, c0:c0 + TC],
                         ybst[ctg % 2][:], St, waits=[(V, v_fin[ctg])])
                st_dma[ctg] = St.n
                ctg += 1
        plan.emit("gpsimd", lambda e: e.engine_nop(), waits=[(St, St.n)])
        plan.run()
    return nc


def host_scan_inputs(inp, front_results):
    ident = np.eye(128, dtype=np.float32)
    NCT = 8
    maps = []
    for c in range(NCORE):
        b, m = c // 2, c % 2
        ch0 = m * 1024
        xc = np.concatenate([front_results[2 * b]["xconv_o"],
                             front_results[2 * b + 1]["xconv_o"]], axis=1)
        xd = np.concatenate([front_results[2 * b]["xdbl_o"],
                             front_results[2 * b + 1]["xdbl_o"]], axis=1)
        lay = lambda v: np.ascontiguousarray(
            v.reshape(NCT, 128, -1).transpose(1, 0, 2).reshape(128, -1))
        maps.append({
            "xconv_c": np.ascontiguousarray(xc[ch0:ch0 + 1024]),
            "xdbl": np.ascontiguousarray(xd[0:64]),
            "xdbl_bc": np.ascontiguousarray(
                xd[64:96].T.reshape(T // 512, 512, 32).transpose(0, 2, 1).reshape(T // 512, -1)),
            "wdt": np.ascontiguousarray(inp["L0_attn_W_dt"][:, ch0:ch0 + 1024]),
            "bdt": lay(inp["L0_attn_b_dt"][ch0:ch0 + 1024]),
            "alog": lay(inp["L0_attn_A_log"][ch0:ch0 + 1024]),
            "dpar": lay(inp["L0_attn_D_param"][ch0:ch0 + 1024]),
            "ident": ident,
        })
    return maps


def build_back(upto="x2", F=None):
    nc = bass.Bass("TRN2", target_bir_lowering=False, debug=False) if F is None else F["nc"]
    NT = TL // 128
    ND = D // 128
    NC_ = DI // 128
    NF = FF // 128
    TCH = 512
    NCH = TL // TCH
    NTT = TCH // 128

    if F is None:
        x_in = nc.dram_tensor("x_in", [TL, D], F32, kind="ExternalInput").ap()
        ybp_in = nc.dram_tensor("ybp_in", [DI, TL], BF16, kind="ExternalInput").ap()
        zs_in = nc.dram_tensor("zs_in", [DI, TL], BF16, kind="ExternalInput").ap()
        wout_in = nc.dram_tensor("wout_in", [DI, D], F32, kind="ExternalInput").ap()
        w1_in = nc.dram_tensor("w1_in", [D, FF], F32, kind="ExternalInput").ap()
        b1_in = nc.dram_tensor("b1_in", [128, NF], F32, kind="ExternalInput").ap()
        w2_in = nc.dram_tensor("w2_in", [FF, D], F32, kind="ExternalInput").ap()
        b2_in = nc.dram_tensor("b2_in", [1, D], F32, kind="ExternalInput").ap()
        gam_in = nc.dram_tensor("gam_in", [1, D], F32, kind="ExternalInput").ap()
        bet_in = nc.dram_tensor("bet_in", [1, D], F32, kind="ExternalInput").ap()
        ident_in = nc.dram_tensor("ident", [128, 128], F32, kind="ExternalInput").ap()
        x2_o = nc.dram_tensor("x2_o", [TL, D], F32, kind="ExternalOutput").ap()
    else:
        io = F["io"]
        x_in, ybp_in, zs_in, wout_in = io["x_in"], io["ybp_in"], io["zs_in"], io["wout_in"]
        w1_in, b1_in, w2_in, b2_in = io["w1_in"], io["b1_in"], io["w2_in"], io["b2_in"]
        gam_in, bet_in, ident_in, x2_o = io["gam_in"], io["bet_in"], io["ident"], io["x2_o"]
    h_o = (nc.dram_tensor("h_o", [FF, 512], BF16, kind="ExternalOutput").ap()
           if upto == "h" else None)
    pfx = "" if F is None else F["pfx"]

    ctx = contextlib.ExitStack()
    with ctx:
        sb = lambda name, shape, dt: ctx.enter_context(nc.sbuf_tensor(pfx + name, shape, dt))
        ps = lambda name, shape, dt=F32: ctx.enter_context(nc.psum_tensor(pfx + name, shape, dt))

        xtd = sb("xtd", [128, NT * D], F32)
        identb = sb("identb", [128, 128], BF16)
        ones1 = sb("ones1", [1, 128], F32)
        b2rep = sb("b2rep", [128, D], F32)
        gamrep = sb("gamrep", [128, D], F32)
        betrep = sb("betrep", [128, D], F32)
        sqs = sb("sqs", [128, D], F32)
        mcol = sb("mcol", [128, NT], F32)
        rcol = sb("rcol", [128, NT], F32)
        scol = sb("scol", [128, NT], F32)
        dscr = sb("dscr", [128, D], F32)
        b1b = sb("b1b", [128, NF], F32)
        rowt = sb("rowt", [1, D], F32)
        g_yb = nc.sbuf_tensor(pfx + "ybar", [128, NC_ * TL], BF16, side="right")
        ybar = g_yb.__enter__()
        g_wo = nc.sbuf_tensor(pfx + "woutb", [128, NC_ * D], BF16, side="right")
        woutb = g_wo.__enter__()
        g_zb = nc.sbuf_tensor(pfx + "zbuf", [128, TL], BF16, side="right")
        zbuf = g_zb.__enter__()
        g_mx = nc.sbuf_tensor(pfx + "mixc", [128, 4 * D], F32, side="right")
        mixc = g_mx.__enter__()
        g_idf = nc.sbuf_tensor(pfx + "identf", [128, 128], F32, side="right")
        identf = g_idf.__enter__()

        pmm = [ps(f"pmm{i}", [128, 512]) for i in range(4)]
        pff = [ps(f"pff{i}", [128, 512]) for i in range(2)]
        ptb = [ctx.enter_context(nc.psum_tensor(pfx + f"ptb{i}", [128, 128], BF16)) for i in range(2)]

        plan = Plan(nc, ctx) if F is None else F["plan"]
        L = plan.sem("L")
        Lg = plan.sem("Lg")
        V = plan.sem("V")
        Cs = plan.sem("C")
        Ts = plan.sem("T")
        St = plan.sem("St")

        plan.dma("sync", xtd[:].rearrange("p (n d) -> p n d", n=NT),
                 x_in.rearrange("(n p) d -> p n d", p=128), L)
        plan.dma("sync", identf[:], ident_in, L)
        plan.dma("sync", ybar[:].rearrange("p (k t) -> p k t", k=NC_),
                 ybp_in.rearrange("(k p) t -> p k t", p=128), L)
        plan.dma("gpsimd", woutb[:].rearrange("p (k d) -> p k d", k=NC_),
                 wout_in.rearrange("(k p) d -> p k d", p=128), Lg)
        lg0 = Lg.n
        for _eng in ("vector", "scalar", "tensor"):
            plan.emit(_eng, lambda e: None, waits=[(Lg, lg0)])
        plan.dma("sync", b1b[:], b1_in, L)
        lds0 = L.n

        plan.emit("vector", lambda e: e.memset(ones1[:], 1.0), inc=(V, 1))
        plan.emit("vector", lambda e: e.tensor_copy(identb[:], identf[:]),
                  waits=[(L, lds0)], inc=(V, 1))
        for ri, (row_in, dst) in enumerate(((b2_in, b2rep), (gam_in, gamrep),
                                            (bet_in, betrep))):
            plan.dma("sync", rowt[:], row_in, L, waits=[(Ts, Ts.n)])
            lr = L.n
            for dc in range(2):
                plan.emit("tensor", lambda e, dc=dc: e.matmul(
                    pff[0][:], ones1[:], rowt[0:1, dc * 512:(dc + 1) * 512],
                    start=True, stop=True),
                    waits=[(L, lr), (V, V.n)], inc=(Ts, 1))
                plan.emit("vector", lambda e, dst=dst, dc=dc: e.tensor_copy(
                    dst[:, dc * 512:(dc + 1) * 512], pff[0][:]),
                    waits=[(Ts, Ts.n)], inc=(V, 1))

        for ct in range(NC_):
            plan.dma("sync", zbuf[:], zs_in[ct * 128:(ct + 1) * 128, :], L,
                     waits=[(V, V.n)])
            lz = L.n
            plan.emit("vector", lambda e, ct=ct: e.tensor_tensor(
                ybar[:, ct * TL:(ct + 1) * TL], ybar[:, ct * TL:(ct + 1) * TL],
                zbuf[:], ALU.mult),
                waits=[(L, lz)], inc=(V, 1))

        def ln_tiles(get_src, n_tiles, base_v, out_cb):
            for tt in range(n_tiles):
                src = get_src(tt)
                plan.emit("scalar", lambda e, s=src: e.activation(sqs[:], s, AF.Square),
                          waits=[(V, max(base_v, V.n))], inc=(Cs, 1))
                plan.emit("vector", lambda e, s=src, o=mcol[:, tt:tt + 1]:
                          e.tensor_reduce(o, s, AX.X, ALU.add),
                          waits=[(V, base_v)], inc=(V, 1))
                plan.emit("vector", lambda e, o=scol[:, tt:tt + 1]:
                          e.tensor_reduce(o, sqs[:], AX.X, ALU.add),
                          waits=[(Cs, Cs.n)], inc=(V, 1))
            nt = n_tiles
            plan.emit("vector", lambda e, nt=nt: e.tensor_scalar(
                mcol[:, 0:nt], mcol[:, 0:nt], 1.0 / D, None, ALU.mult),
                waits=[(V, V.n)], inc=(V, 1))
            plan.emit("vector", lambda e, nt=nt: e.tensor_scalar(
                scol[:, 0:nt], scol[:, 0:nt], 1.0 / D, None, ALU.mult),
                waits=[(V, V.n)], inc=(V, 1))
            plan.emit("vector", lambda e, nt=nt: e.scalar_tensor_tensor(
                rcol[:, 0:nt], mcol[:, 0:nt], -1.0, mcol[:, 0:nt], ALU.mult, ALU.mult),
                waits=[(V, V.n)], inc=(V, 1))
            plan.emit("vector", lambda e, nt=nt: e.tensor_tensor(
                scol[:, 0:nt], scol[:, 0:nt], rcol[:, 0:nt], ALU.add),
                waits=[(V, V.n)], inc=(V, 1))
            plan.emit("vector", lambda e, nt=nt: e.tensor_scalar(
                scol[:, 0:nt], scol[:, 0:nt], LN_EPS, None, ALU.add),
                waits=[(V, V.n)], inc=(V, 1))
            vv = V.n
            plan.emit("scalar", lambda e, nt=nt: e.activation(
                rcol[:, 0:nt], scol[:, 0:nt], AF.Sqrt), waits=[(V, vv)], inc=(Cs, 1))
            plan.emit("vector", lambda e, nt=nt: e.reciprocal(
                rcol[:, 0:nt], rcol[:, 0:nt]), waits=[(Cs, Cs.n)], inc=(V, 1))
            for tt in range(n_tiles):
                out_cb(tt, mcol[:, tt:tt + 1], rcol[:, tt:tt + 1])

        for ch in range(NCH):
            tt0 = ch * NTT
            for tl in range(NTT):
                for dc in range(2):
                    pm = pmm[tl % 4]
                    for kt in range(NC_):
                        w = [(V, V.n), (Cs, Cs.n)] if kt == 0 else []
                        plan.emit("tensor", lambda e, tl=tl, dc=dc, kt=kt, pm=pm, tt0=tt0: e.matmul(
                            pm[:], ybar[:, kt * TL + (tt0 + tl) * 128: kt * TL + (tt0 + tl) * 128 + 128],
                            woutb[:, kt * D + dc * 512: kt * D + (dc + 1) * 512],
                            start=(kt == 0), stop=(kt == NC_ - 1)),
                            waits=w, inc=((Ts, 1) if kt == NC_ - 1 else None))
                    plan.emit("vector", lambda e, tl=tl, dc=dc, pm=pm: e.tensor_copy(
                        mixc[:, tl * D + dc * 512: tl * D + (dc + 1) * 512], pm[:]),
                        waits=[(Ts, Ts.n), (Cs, Cs.n), (St, St.n)], inc=(V, 1))
            v_mix = V.n
            if upto == "mix":
                for tt in range(NTT):
                    plan.dma("gpsimd", x2_o[(tt0 + tt) * 128:(tt0 + tt + 1) * 128, :],
                             mixc[:, tt * D:(tt + 1) * D], St, waits=[(V, v_mix)])
                continue

            def upd_x1(tt, mc, rc, tt0=tt0):
                plan.emit("vector", lambda e, tt=tt, mc=mc, rc=rc: e.tensor_scalar(
                    dscr[:], mixc[:, tt * D:(tt + 1) * D], mc, rc,
                    ALU.subtract, ALU.mult), waits=[(V, V.n)], inc=(V, 1))
                plan.emit("vector", lambda e, tt=tt, tt0=tt0: e.tensor_tensor(
                    xtd[:, (tt0 + tt) * D:(tt0 + tt + 1) * D], dscr[:],
                    xtd[:, (tt0 + tt) * D:(tt0 + tt + 1) * D], ALU.add), inc=(V, 1))
            ln_tiles(lambda tt: mixc[:, tt * D:(tt + 1) * D], NTT, v_mix, upd_x1)

        if upto == "mix":
            plan.emit("gpsimd", lambda e: e.engine_nop(), waits=[(St, St.n)])
            plan.run()
            return nc
        if upto == "x1":
            for tt in range(NT):
                plan.dma("gpsimd", x2_o[tt * 128:(tt + 1) * 128, :],
                         xtd[:, tt * D:(tt + 1) * D], St, waits=[(V, V.n)])
            plan.emit("gpsimd", lambda e: e.engine_nop(), waits=[(St, St.n)])
            plan.run()
            return nc
        g_idf.__exit__(None, None, None)
        g_mx.__exit__(None, None, None)
        g_zb.__exit__(None, None, None)
        g_wo.__exit__(None, None, None)
        g_yb.__exit__(None, None, None)
        KG = 8
        u2c = sb("u2c", [128, NTT * D], BF16)
        u2T = sb("u2T", [128, ND * TCH], BF16)
        hbuf2 = sb("hbuf2", [128, NF * TCH], BF16)
        v2c = sb("v2c", [128, NTT * D], F32)
        w1s = [sb(f"w1s{i}", [128, ND * KG * 128], BF16) for i in range(2)]
        w2s = [sb(f"w2s{i}", [128, 4 * 512], BF16) for i in range(2)]

        ts_pa, v_pa, c_pa = Ts.n, V.n, Cs.n
        w1_last = {0: None, 1: None}
        w2_last = {0: None, 1: None}
        for ch in range(NCH):
            tt0 = ch * NTT
            v_x1 = V.n
            def mk_u2(tt, mc, rc, tt0=tt0):
                plan.emit("vector", lambda e, tt=tt, mc=mc, rc=rc: e.tensor_scalar(
                    u2c[:, tt * D:(tt + 1) * D], xtd[:, (tt0 + tt) * D:(tt0 + tt + 1) * D],
                    mc, rc, ALU.subtract, ALU.mult),
                    waits=[(Ts, Ts.n)], inc=(V, 1))
            ln_tiles(lambda tt, tt0=tt0: xtd[:, (tt0 + tt) * D:(tt0 + tt + 1) * D],
                     NTT, v_x1, mk_u2)
            v_u2 = V.n
            ntp = 0
            for tt in range(NTT):
                for dt in range(ND):
                    pb = ptb[ntp % 2]
                    w = [(V, v_u2)]
                    if ntp >= 2:
                        w.append((Cs, Cs.n))
                    plan.emit("tensor", lambda e, tt=tt, dt=dt, pb=pb: e.transpose(
                        pb[:], u2c[:, tt * D + dt * 128: tt * D + dt * 128 + 128],
                        identb[:]), waits=w, inc=(Ts, 1))
                    plan.emit("scalar", lambda e, tt=tt, dt=dt, pb=pb: e.copy(
                        u2T[:, dt * TCH + tt * 128: dt * TCH + (tt + 1) * 128], pb[:]),
                        waits=[(Ts, Ts.n)], inc=(Cs, 1))
                    ntp += 1
            c_u2T = Cs.n
            for fg in range(NF // KG):
                bi = fg % 2
                wbuf = w1s[bi]
                if w1_last[bi] is None:
                    wt = [(Ts, ts_pa), (V, v_pa), (Cs, c_pa)]
                else:
                    wt = [(Ts, w1_last[bi])]
                plan.dma("gpsimd", wbuf[:].rearrange("p (k f) -> p k f", k=ND),
                         w1_in[:, fg * KG * 128:(fg + 1) * KG * 128]
                         .rearrange("(k p) f -> p k f", p=128), Lg, waits=wt)
                lw = Lg.n
                for fi in range(KG):
                    ft = fg * KG + fi
                    pf = pff[ft % 2]
                    for kt in range(ND):
                        w = ([(Lg, lw), (Cs, Cs.n)] if kt == 0 else [])
                        plan.emit("tensor", lambda e, fi=fi, kt=kt, pf=pf, wbuf=wbuf: e.matmul(
                            pf[:], wbuf[:, kt * KG * 128 + fi * 128: kt * KG * 128 + fi * 128 + 128],
                            u2T[:, kt * TCH:(kt + 1) * TCH],
                            start=(kt == 0), stop=(kt == ND - 1)),
                            waits=w, inc=((Ts, 1) if kt == ND - 1 else None))
                    w1_last[bi] = Ts.n
                    plan.emit("scalar", lambda e, ft=ft, pf=pf: e.activation(
                        hbuf2[:, ft * TCH:(ft + 1) * TCH], pf[:], AF.Relu,
                        bias=b1b[:, ft:ft + 1]),
                        waits=[(Ts, Ts.n)], inc=(Cs, 1))
            c_h = Cs.n
            if upto == "h":
                if ch == 0:
                    for ft in range(NF):
                        plan.dma("gpsimd", h_o[ft * 128:(ft + 1) * 128, :],
                                 hbuf2[:, ft * TCH:(ft + 1) * TCH], St,
                                 waits=[(Cs, c_h)])
                continue
            for dc in range(2):
                for kg in range(NF // 4):
                    bi = kg % 2
                    wbuf = w2s[bi]
                    if w2_last[bi] is None:
                        wt = [(Ts, ts_pa), (V, v_pa), (Cs, c_pa)]
                    else:
                        wt = [(Ts, w2_last[bi])]
                    plan.dma("gpsimd", wbuf[:].rearrange("p (k d) -> p k d", k=4),
                             w2_in[kg * 512:(kg + 1) * 512, dc * 512:(dc + 1) * 512]
                             .rearrange("(k p) d -> p k d", p=128), Lg, waits=wt)
                    lw = Lg.n
                    for ki in range(4):
                        kt = kg * 4 + ki
                        for tl in range(NTT):
                            w = [(Lg, lw)] if tl == 0 else []
                            if kt == 0 and tl == 0:
                                w.append((V, V.n))
                                w.append((Cs, c_h))
                            plan.emit("tensor", lambda e, ki=ki, kt=kt, tl=tl, dc=dc,
                                      wbuf=wbuf: e.matmul(
                                pmm[tl][:, 0:512], hbuf2[:, kt * TCH + tl * 128: kt * TCH + tl * 128 + 128],
                                wbuf[:, ki * 512:(ki + 1) * 512],
                                start=(kt == 0), stop=(kt == NF - 1)),
                                waits=w,
                                inc=((Ts, 1) if (kt == NF - 1 or
                                                 (ki == 3 and tl == NTT - 1)) else None))
                    w2_last[bi] = Ts.n
                t_e = Ts.n
                for tl in range(NTT):
                    plan.emit("vector", lambda e, tl=tl, dc=dc: e.tensor_tensor(
                        v2c[:, tl * D + dc * 512: tl * D + (dc + 1) * 512], pmm[tl][:, 0:512],
                        b2rep[:, dc * 512:(dc + 1) * 512], ALU.add),
                        waits=[(Ts, t_e), (Cs, Cs.n), (St, St.n)], inc=(V, 1))
                    plan.emit("vector", lambda e, tl=tl, dc=dc: e.tensor_tensor(
                        v2c[:, tl * D + dc * 512: tl * D + (dc + 1) * 512],
                        v2c[:, tl * D + dc * 512: tl * D + (dc + 1) * 512],
                        u2c[:, tl * D + dc * 512: tl * D + (dc + 1) * 512], ALU.add),
                        inc=(V, 1))
            v_v2 = V.n
            if upto == "v2":
                for tt in range(NTT):
                    plan.dma("gpsimd", x2_o[(tt0 + tt) * 128:(tt0 + tt + 1) * 128, :],
                             v2c[:, tt * D:(tt + 1) * D], St, waits=[(V, v_v2)])
                continue
            def mk_w(tt, mc, rc):
                plan.emit("vector", lambda e, tt=tt, mc=mc, rc=rc: e.tensor_scalar(
                    v2c[:, tt * D:(tt + 1) * D], v2c[:, tt * D:(tt + 1) * D],
                    mc, rc, ALU.subtract, ALU.mult), inc=(V, 1))
                plan.emit("vector", lambda e, tt=tt: e.tensor_tensor(
                    v2c[:, tt * D:(tt + 1) * D], v2c[:, tt * D:(tt + 1) * D],
                    gamrep[:], ALU.mult), inc=(V, 1))
                plan.emit("vector", lambda e, tt=tt: e.tensor_tensor(
                    v2c[:, tt * D:(tt + 1) * D], v2c[:, tt * D:(tt + 1) * D],
                    betrep[:], ALU.add), inc=(V, 1))
            ln_tiles(lambda tt: v2c[:, tt * D:(tt + 1) * D], NTT, v_v2, mk_w)
            v_w = V.n
            def upd_x2(tt, mc, rc, tt0=tt0):
                plan.emit("vector", lambda e, tt=tt, mc=mc, rc=rc: e.tensor_scalar(
                    dscr[:], v2c[:, tt * D:(tt + 1) * D], mc, rc,
                    ALU.subtract, ALU.mult), waits=[(V, V.n)], inc=(V, 1))
                plan.emit("vector", lambda e, tt=tt, tt0=tt0: e.tensor_tensor(
                    xtd[:, (tt0 + tt) * D:(tt0 + tt + 1) * D], dscr[:],
                    xtd[:, (tt0 + tt) * D:(tt0 + tt + 1) * D], ALU.add), inc=(V, 1))
            ln_tiles(lambda tt: v2c[:, tt * D:(tt + 1) * D], NTT, v_w, upd_x2)
            for tt in range(NTT):
                plan.dma("gpsimd", x2_o[(tt0 + tt) * 128:(tt0 + tt + 1) * 128, :],
                         xtd[:, (tt0 + tt) * D:(tt0 + tt + 1) * D], St,
                         waits=[(V, V.n)])
        plan.emit("gpsimd", lambda e: e.engine_nop(), waits=[(St, St.n)])
        if F is None:
            plan.run()
    return nc


def host_back_inputs(inp, front_results, scan_results):
    ident = np.eye(128, dtype=np.float32)
    x = inp["L0_attn_x"]
    maps = []
    for c in range(NCORE):
        b, m = c // 2, c % 2
        t0 = m * TL
        ybp = np.concatenate([scan_results[2 * b][:, t0:t0 + TL],
                              scan_results[2 * b + 1][:, t0:t0 + TL]], axis=0)
        b1l = np.ascontiguousarray(
            inp["L0_ffn_b1"][0].reshape(FF // 128, 128).T)
        maps.append({
            "x_in": np.ascontiguousarray(x[b, t0:t0 + TL]),
            "ybp_in": np.ascontiguousarray(ybp),
            "zs_in": front_results[c]["zs_o"],
            "wout_in": inp["L0_attn_W_out"],
            "w1_in": inp["L0_ffn_w1"][0],
            "b1_in": b1l,
            "w2_in": inp["L0_ffn_w2"][0],
            "b2_in": inp["L0_ffn_b2"][0:1, :].reshape(1, D) if inp["L0_ffn_b2"].ndim == 2
                     else inp["L0_ffn_b2"][0].reshape(1, D),
            "gam_in": inp["L0_ffn_gamma"].reshape(1, D),
            "bet_in": inp["L0_ffn_beta"].reshape(1, D),
            "ident": ident,
        })
    return maps




def build_attn(upto='full', F=None):
    nc = bass.Bass("TRN2", target_bir_lowering=False, debug=False) if F is None else F["nc"]
    NT = TL // 128
    ND = D // 128
    TC = 256
    NCH = TL // TC
    NTT = TC // 128

    if F is None:
        x2_in = nc.dram_tensor("x2_in", [TL, D], F32, kind="ExternalInput").ap()
        wq_in = nc.dram_tensor("wq_in", [D, D], F32, kind="ExternalInput").ap()
        bq_in = nc.dram_tensor("bq_in", [128, ND], F32, kind="ExternalInput").ap()
        wd_in = nc.dram_tensor("wd_in", [D, 256], F32, kind="ExternalInput").ap()
        bd_in = nc.dram_tensor("bd_in", [128, 2], F32, kind="ExternalInput").ap()
        wu_in = nc.dram_tensor("wu_in", [256, DI], F32, kind="ExternalInput").ap()
        bu_in = nc.dram_tensor("bu_in", [128, DI // 128], F32, kind="ExternalInput").ap()
        wo_in = nc.dram_tensor("wo_in", [D, D], F32, kind="ExternalInput").ap()
        bo_in = nc.dram_tensor("bo_in", [1, D], F32, kind="ExternalInput").ap()
        cosr_in = nc.dram_tensor("cosr_in", [128, TL], F32, kind="ExternalInput").ap()
        sinr_in = nc.dram_tensor("sinr_in", [128, TL], F32, kind="ExternalInput").ap()
        selk_in = nc.dram_tensor("selk_in", [128, 128], F32, kind="ExternalInput").ap()
        selv_in = nc.dram_tensor("selv_in", [128, 128], F32, kind="ExternalInput").ap()
        red_in = nc.dram_tensor("red_in", [128, 8 * 16], F32, kind="ExternalInput").ap()
        am_in = nc.dram_tensor("am_in", [16, 8 * 128], F32, kind="ExternalInput").ap()
        ident_in = nc.dram_tensor("ident", [128, 128], F32, kind="ExternalInput").ap()
        x3_o = nc.dram_tensor("x3_o", [TL, D], F32, kind="ExternalOutput").ap()
    else:
        io = F["io"]
        x2_in, wq_in, bq_in, wd_in = io["x2_in"], io["wq_in"], io["bq_in"], io["wd_in"]
        bd_in, wu_in, bu_in, wo_in = io["bd_in"], io["wu_in"], io["bu_in"], io["wo_in"]
        bo_in, cosr_in, sinr_in = io["bo_in"], io["cosr_in"], io["sinr_in"]
        selk_in, selv_in, red_in, am_in = io["selk_in"], io["selv_in"], io["red_in"], io["am_in"]
        ident_in, x3_o = io["ident"], io["x3_o"]
    pfx = "" if F is None else F["pfx"]

    ctx = contextlib.ExitStack()
    with ctx:
        sb = lambda name, shape, dt: ctx.enter_context(nc.sbuf_tensor(pfx + name, shape, dt))
        ps = lambda name, shape, dt=F32: ctx.enter_context(nc.psum_tensor(pfx + name, shape, dt))

        identb = sb("identb", [128, 128], BF16)
        identf = sb("identf", [128, 128], F32)
        ones1 = sb("ones1", [1, 128], F32)
        sqs = sb("sqs", [128, D], F32)
        mcol = sb("mcol", [128, 16], F32)
        rcol = sb("rcol", [128, 16], F32)
        scol = sb("scol", [128, 16], F32)
        dscr = sb("dscr", [128, D], F32)
        xts = [sb(f"xts{i}", [128, D], F32) for i in range(2)]
        u3t = sb("u3t", [128, D], BF16)
        u3T = sb("u3T", [128, ND * TL], BF16)
        wqb = sb("wqb", [128, ND * D], BF16)
        wdb = sb("wdb", [128, ND * 256], BF16)
        wub = sb("wub", [128, 2 * DI], BF16)
        wob = sb("wob", [128, ND * D], BF16)
        bqb = sb("bqb", [128, ND], F32)
        bdb = sb("bdb", [128, 2], F32)
        bub = sb("bub", [128, DI // 128], F32)
        borep = sb("borep", [128, D], F32)
        cosr = sb("cosr", [128, TC], F32)
        sinr = sb("sinr", [128, TC], F32)
        selkb = sb("selkb", [128, 128], BF16)
        selvb = sb("selvb", [128, 128], BF16)
        redb = sb("redb", [128, 8 * 16], BF16)
        amb = sb("amb", [16, 8 * 128], BF16)
        rowt = sb("rowt", [1, D], F32)
        qb = sb("qb", [128, ND * TC], BF16)
        qsw = sb("qsw", [128, 16 * TC], BF16)
        kvb = sb("kvb", [128, 16 * TC], BF16)
        latb = sb("latb", [128, 2 * TC], BF16)
        krep = sb("krep", [128, 16 * TC], BF16)
        vrep = sb("vrep", [128, 16 * TC], BF16)
        pgb = sb("pgb", [128, ND * TC], BF16)
        attb = sb("attb", [16, 16 * TC], F32)
        rmaxb = sb("rmaxb", [16, TC], F32)
        rsumb = sb("rsumb", [16, TC], F32)
        probsb = sb("probsb", [16, 16 * TC], BF16)
        arep = [sb(f"arep{i}", [128, TC], BF16) for i in range(2)]
        mbuf = [sb(f"mbuf{i}", [128, TC], BF16) for i in range(2)]
        aoutT = sb("aoutT", [128, ND * TC], BF16)
        aoc = sb("aoc", [128, NTT * D], F32)
        t1 = sb("t1", [128, 16 * TC], BF16)
        t2 = sb("t2", [128, 16 * TC], BF16)

        pq = [ps(f"pq{i}", [128, 512]) for i in range(2)]
        pat = [ctx.enter_context(nc.psum_tensor(pfx + "pat0", [16, TC], F32))] * 2
        prep = [ps(f"prep{i}", [128, TC]) for i in range(2)]
        pou = [ps("pou0", [128, TC])] * 2
        ptb = [ctx.enter_context(nc.psum_tensor(pfx + f"ptb{i}", [128, 128], BF16))
               for i in range(2)]

        plan = Plan(nc, ctx) if F is None else F["plan"]
        L = plan.sem("L")
        Lg = plan.sem("Lg")
        V = plan.sem("V")
        Cs = plan.sem("C")
        Ts = plan.sem("T")
        G = plan.sem("G")
        St = plan.sem("St")

        plan.dma("gpsimd", wqb[:].rearrange("p (k d) -> p k d", k=ND),
                 wq_in.rearrange("(k p) d -> p k d", p=128), Lg)
        plan.dma("gpsimd", wdb[:].rearrange("p (k d) -> p k d", k=ND),
                 wd_in.rearrange("(k p) d -> p k d", p=128), Lg)
        plan.dma("gpsimd", wub[:].rearrange("p (k d) -> p k d", k=2),
                 wu_in.rearrange("(k p) d -> p k d", p=128), Lg)
        plan.dma("gpsimd", wob[:].rearrange("p (k d) -> p k d", k=ND),
                 wo_in.rearrange("(k p) d -> p k d", p=128), Lg)
        plan.dma("sync", bqb[:], bq_in, L)
        plan.dma("sync", bdb[:], bd_in, L)
        plan.dma("sync", bub[:], bu_in, L)

        plan.dma("gpsimd", selkb[:], selk_in, Lg)
        plan.dma("gpsimd", selvb[:], selv_in, Lg)
        plan.dma("gpsimd", redb[:], red_in, Lg)
        plan.dma("gpsimd", amb[:], am_in, Lg)
        plan.dma("sync", rowt[:], bo_in, L)
        plan.dma("sync", identf[:], ident_in, L)
        lds = L.n
        lga = Lg.n
        for _eng in ("vector", "scalar", "tensor"):
            plan.emit(_eng, lambda e: None, waits=[(Lg, lga)])

        plan.emit("vector", lambda e: e.memset(ones1[:], 1.0), inc=(V, 1))
        plan.emit("vector", lambda e: e.tensor_copy(identb[:], identf[:]),
                  waits=[(L, lds)], inc=(V, 1))
        for dc in range(2):
            plan.emit("tensor", lambda e, dc=dc: e.matmul(
                pq[0][:], ones1[:], rowt[0:1, dc * 512:(dc + 1) * 512],
                start=True, stop=True),
                waits=[(L, lds), (V, V.n)], inc=(Ts, 1))
            plan.emit("vector", lambda e, dc=dc: e.tensor_copy(
                borep[:, dc * 512:(dc + 1) * 512], pq[0][:]),
                waits=[(Ts, Ts.n)], inc=(V, 1))

        def ln_tiles(get_src, n_tiles, base_v, out_cb):
            for tt in range(n_tiles):
                src = get_src(tt)
                plan.emit("scalar", lambda e, s=src: e.activation(sqs[:], s, AF.Square),
                          waits=[(V, max(base_v, V.n))], inc=(Cs, 1))
                plan.emit("vector", lambda e, s=src, o=mcol[:, tt:tt + 1]:
                          e.tensor_reduce(o, s, AX.X, ALU.add),
                          waits=[(V, base_v)], inc=(V, 1))
                plan.emit("vector", lambda e, o=scol[:, tt:tt + 1]:
                          e.tensor_reduce(o, sqs[:], AX.X, ALU.add),
                          waits=[(Cs, Cs.n)], inc=(V, 1))
            nt = n_tiles
            plan.emit("vector", lambda e, nt=nt: e.tensor_scalar(
                mcol[:, 0:nt], mcol[:, 0:nt], 1.0 / D, None, ALU.mult),
                waits=[(V, V.n)], inc=(V, 1))
            plan.emit("vector", lambda e, nt=nt: e.tensor_scalar(
                scol[:, 0:nt], scol[:, 0:nt], 1.0 / D, None, ALU.mult),
                waits=[(V, V.n)], inc=(V, 1))
            plan.emit("vector", lambda e, nt=nt: e.scalar_tensor_tensor(
                rcol[:, 0:nt], mcol[:, 0:nt], -1.0, mcol[:, 0:nt], ALU.mult, ALU.mult),
                waits=[(V, V.n)], inc=(V, 1))
            plan.emit("vector", lambda e, nt=nt: e.tensor_tensor(
                scol[:, 0:nt], scol[:, 0:nt], rcol[:, 0:nt], ALU.add),
                waits=[(V, V.n)], inc=(V, 1))
            plan.emit("vector", lambda e, nt=nt: e.tensor_scalar(
                scol[:, 0:nt], scol[:, 0:nt], LN_EPS, None, ALU.add),
                waits=[(V, V.n)], inc=(V, 1))
            vv = V.n
            plan.emit("scalar", lambda e, nt=nt: e.activation(
                rcol[:, 0:nt], scol[:, 0:nt], AF.Sqrt), waits=[(V, vv)], inc=(Cs, 1))
            plan.emit("vector", lambda e, nt=nt: e.reciprocal(
                rcol[:, 0:nt], rcol[:, 0:nt]), waits=[(Cs, Cs.n)], inc=(V, 1))
            for tt in range(n_tiles):
                out_cb(tt, mcol[:, tt:tt + 1], rcol[:, tt:tt + 1])

        tp_use = {}
        for tt in range(NT):
            xb = xts[tt % 2]
            wt = [(Ts, tp_use[tt - 2])] if (tt - 2) in tp_use else []
            plan.dma("sync", xb[:], x2_in[tt * 128:(tt + 1) * 128, :], L, waits=wt)
            lx = L.n

            plan.emit("scalar", lambda e, xb=xb: e.activation(sqs[:], xb[:], AF.Square),
                      waits=[(L, lx), (V, V.n)], inc=(Cs, 1))
            plan.emit("vector", lambda e, xb=xb: e.tensor_reduce(
                mcol[:, 0:1], xb[:], AX.X, ALU.add), waits=[(L, lx)], inc=(V, 1))
            plan.emit("vector", lambda e: e.tensor_reduce(
                scol[:, 0:1], sqs[:], AX.X, ALU.add), waits=[(Cs, Cs.n)], inc=(V, 1))
            plan.emit("vector", lambda e: e.tensor_scalar(
                mcol[:, 0:1], mcol[:, 0:1], 1.0 / D, None, ALU.mult),
                waits=[(V, V.n)], inc=(V, 1))
            plan.emit("vector", lambda e: e.tensor_scalar(
                scol[:, 0:1], scol[:, 0:1], 1.0 / D, None, ALU.mult),
                waits=[(V, V.n)], inc=(V, 1))
            plan.emit("vector", lambda e: e.scalar_tensor_tensor(
                rcol[:, 0:1], mcol[:, 0:1], -1.0, mcol[:, 0:1], ALU.mult, ALU.mult),
                waits=[(V, V.n)], inc=(V, 1))
            plan.emit("vector", lambda e: e.tensor_tensor(
                scol[:, 0:1], scol[:, 0:1], rcol[:, 0:1], ALU.add),
                waits=[(V, V.n)], inc=(V, 1))
            plan.emit("vector", lambda e: e.tensor_scalar(
                scol[:, 0:1], scol[:, 0:1], LN_EPS, None, ALU.add),
                waits=[(V, V.n)], inc=(V, 1))
            vv = V.n
            plan.emit("scalar", lambda e: e.activation(
                rcol[:, 0:1], scol[:, 0:1], AF.Sqrt), waits=[(V, vv)], inc=(Cs, 1))
            plan.emit("vector", lambda e: e.reciprocal(
                rcol[:, 0:1], rcol[:, 0:1]), waits=[(Cs, Cs.n)], inc=(V, 1))
            plan.emit("vector", lambda e, xb=xb: e.tensor_scalar(
                u3t[:], xb[:], mcol[:, 0:1], rcol[:, 0:1], ALU.subtract, ALU.mult),
                waits=[(Ts, Ts.n)], inc=(V, 1))
            v_u3 = V.n
            for dt in range(ND):
                pbu = (tt * ND + dt) % 2
                w = [(V, v_u3)]
                if tt * ND + dt >= 2:
                    w.append((Cs, Cs.n))
                plan.emit("tensor", lambda e, dt=dt, pbu=pbu: e.transpose(
                    ptb[pbu][:], u3t[:, dt * 128:(dt + 1) * 128], identb[:]),
                    waits=w, inc=(Ts, 1))
                plan.emit("scalar", lambda e, tt=tt, dt=dt, pbu=pbu: e.copy(
                    u3T[:, dt * TL + tt * 128: dt * TL + (tt + 1) * 128], ptb[pbu][:]),
                    waits=[(Ts, Ts.n)], inc=(Cs, 1))
            tp_use[tt] = Ts.n
        c_u3T = Cs.n
        if upto == "u3":
            for dt in range(ND):
                plan.dma("gpsimd", x3_o[dt * 256:dt * 256 + 128, 0:D],
                         u3T[:, dt * TL:dt * TL + D], St, waits=[(Cs, c_u3T)])
            plan.emit("gpsimd", lambda e: e.engine_nop(), waits=[(St, St.n)])
            plan.run()
            return nc

        nch_run = 1 if upto.startswith("one_") else (2 if upto.startswith("two_") else NCH)
        upto = upto.removeprefix("one_").removeprefix("two_")
        for ch in range(nch_run):
            c0 = ch * TC
            plan.dma("sync", cosr[:], cosr_in[:, c0:c0 + TC], L, waits=[(V, V.n)])
            plan.dma("sync", sinr[:], sinr_in[:, c0:c0 + TC], L, waits=[(V, V.n)])
            l_rope = L.n
            for qt in range(ND):
                pqt = pq[qt % 2]
                for kt in range(ND):
                    w = [(Cs, Cs.n), (V, V.n), (G, G.n)] if kt == 0 else []
                    plan.emit("tensor", lambda e, qt=qt, kt=kt, c0=c0, pqt=pqt: e.matmul(
                        pqt[:, 0:TC], wqb[:, kt * D + qt * 128: kt * D + qt * 128 + 128],
                        u3T[:, kt * TL + c0: kt * TL + c0 + TC],
                        start=(kt == 0), stop=(kt == ND - 1)),
                        waits=w, inc=((Ts, 1) if kt == ND - 1 else None))
                plan.emit("scalar", lambda e, qt=qt, pqt=pqt: e.activation(
                    qb[:, qt * TC:(qt + 1) * TC], pqt[:, 0:TC], AF.Identity,
                    bias=bqb[:, qt:qt + 1]),
                    waits=[(Ts, Ts.n)], inc=(Cs, 1))
            c_q = Cs.n
            if upto == "q":
                if ch == 0:
                    plan.dma("gpsimd", x3_o[0:128, 0:D], qb[:, 0:D], St,
                             waits=[(Cs, c_q)])
                continue
            for lt in range(2):
                plt = pq[lt % 2]
                for kt in range(ND):
                    w = [(Cs, Cs.n)] if kt == 0 else []
                    plan.emit("tensor", lambda e, lt=lt, kt=kt, c0=c0, plt=plt: e.matmul(
                        plt[:, 0:TC], wdb[:, kt * 256 + lt * 128: kt * 256 + lt * 128 + 128],
                        u3T[:, kt * TL + c0: kt * TL + c0 + TC],
                        start=(kt == 0), stop=(kt == ND - 1)),
                        waits=w, inc=((Ts, 1) if kt == ND - 1 else None))
                plan.emit("scalar", lambda e, lt=lt, plt=plt: e.activation(
                    latb[:, lt * TC:(lt + 1) * TC], plt[:, 0:TC], AF.Identity,
                    bias=bdb[:, lt:lt + 1]),
                    waits=[(Ts, Ts.n)], inc=(Cs, 1))
            c_lat = Cs.n
            for ht in range(16):
                pkv = pq[ht % 2]
                for kt in range(2):
                    w = [(Cs, Cs.n)] if kt == 0 else []
                    plan.emit("tensor", lambda e, ht=ht, kt=kt, pkv=pkv: e.matmul(
                        pkv[:, 0:TC], wub[:, kt * DI + ht * 128: kt * DI + ht * 128 + 128],
                        latb[:, kt * TC:(kt + 1) * TC],
                        start=(kt == 0), stop=(kt == 1)),
                        waits=w, inc=((Ts, 1) if kt == 1 else None))
                plan.emit("scalar", lambda e, ht=ht, pkv=pkv: e.activation(
                    kvb[:, ht * TC:(ht + 1) * TC], pkv[:, 0:TC], AF.Identity,
                    bias=bub[:, ht:ht + 1]),
                    waits=[(Ts, Ts.n)], inc=(Cs, 1))
            c_kv = Cs.n
            if upto == "qkv":
                if ch == 0:
                    plan.dma("gpsimd", x3_o[0:128, 0:D], qb[:, 0:D], St,
                             waits=[(Cs, c_kv)])
                    plan.dma("gpsimd", x3_o[128:256, 0:D],
                             kvb[:, 0:D], St, waits=[(Cs, c_kv)])
                continue
            for blk in range(4):
                r0 = blk * 32
                s0 = r0 + 32 if blk % 2 == 0 else r0 - 32
                plan.emit("gpsimd", lambda e, r0=r0, s0=s0: e.dma_start(
                    out=qsw[r0:r0 + 32, 0:ND * TC], in_=qb[s0:s0 + 32, :]),
                    waits=[(Cs, c_kv), (V, V.n)], inc=(G, 16))
            g_qsw = G.n
            plan.emit("vector", lambda e: e.tensor_tensor(
                t1[:, 0:ND * TC].rearrange("p (n t) -> p n t", n=ND),
                qb[:].rearrange("p (n t) -> p n t", n=ND),
                bc0(cosr[:], ND), ALU.mult),
                waits=[(G, g_qsw), (L, l_rope)], inc=(V, 1))
            plan.emit("vector", lambda e: e.tensor_tensor(
                t2[:, 0:ND * TC].rearrange("p (n t) -> p n t", n=ND),
                qsw[:, 0:ND * TC].rearrange("p (n t) -> p n t", n=ND),
                bc0(sinr[:], ND), ALU.mult), inc=(V, 1))
            plan.emit("vector", lambda e: e.tensor_tensor(
                qb[:], t1[:, 0:ND * TC], t2[:, 0:ND * TC], ALU.add), inc=(V, 1))
            v_qrope = V.n
            for ht in range(16):
                plan.emit("gpsimd", lambda e, ht=ht: e.dma_start(
                    out=qsw[0:32, ht * TC:(ht + 1) * TC],
                    in_=kvb[32:64, ht * TC:(ht + 1) * TC]),
                    waits=[(Cs, c_kv), (V, v_qrope)], inc=(G, 16))
                plan.emit("gpsimd", lambda e, ht=ht: e.dma_start(
                    out=qsw[32:64, ht * TC:(ht + 1) * TC],
                    in_=kvb[0:32, ht * TC:(ht + 1) * TC]),
                    inc=(G, 16))
            g_ksw = G.n
            plan.emit("vector", lambda e: e.tensor_tensor(
                t1[0:64, :].rearrange("p (n t) -> p n t", n=16),
                kvb[0:64, :].rearrange("p (n t) -> p n t", n=16),
                bc0(cosr[0:64, :], 16), ALU.mult),
                waits=[(G, g_ksw)], inc=(V, 1))
            plan.emit("vector", lambda e: e.tensor_tensor(
                t2[0:64, :].rearrange("p (n t) -> p n t", n=16),
                qsw[0:64, :].rearrange("p (n t) -> p n t", n=16),
                bc0(sinr[0:64, :], 16), ALU.mult), inc=(V, 1))
            plan.emit("vector", lambda e: e.tensor_tensor(
                kvb[0:64, :].rearrange("p (n t) -> p n t", n=16),
                t1[0:64, :].rearrange("p (n t) -> p n t", n=16),
                t2[0:64, :].rearrange("p (n t) -> p n t", n=16), ALU.add),
                inc=(V, 1))
            v_rope = V.n
            if upto == "rope":
                if ch == 0:
                    plan.dma("gpsimd", x3_o[0:128, 0:D], qb[:, 0:D], St,
                             waits=[(V, v_rope)])
                continue
            for g in range(16):
                pr = prep[g % 2]
                plan.emit("tensor", lambda e, g=g, pr=pr: e.matmul(
                    pr[:], selkb[:], kvb[:, g * TC:(g + 1) * TC], start=True, stop=True),
                    waits=[(V, v_rope), (Cs, Cs.n)], inc=(Ts, 1))
                plan.emit("scalar", lambda e, g=g, pr=pr: e.copy(
                    krep[:, g * TC:(g + 1) * TC], pr[:]),
                    waits=[(Ts, Ts.n)], inc=(Cs, 1))
                plan.emit("tensor", lambda e, g=g, pr=pr: e.matmul(
                    pr[:], selvb[:], kvb[:, g * TC:(g + 1) * TC], start=True, stop=True),
                    waits=[(Cs, Cs.n)], inc=(Ts, 1))
                plan.emit("scalar", lambda e, g=g, pr=pr: e.copy(
                    vrep[:, g * TC:(g + 1) * TC], pr[:]),
                    waits=[(Ts, Ts.n)], inc=(Cs, 1))
            c_rep = Cs.n
            for g in range(16):
                plan.emit("vector", lambda e, g=g: e.tensor_tensor(
                    pgb[:].rearrange("p (n t) -> p n t", n=ND),
                    qb[:].rearrange("p (n t) -> p n t", n=ND),
                    bc0(krep[:, g * TC:(g + 1) * TC], ND), ALU.mult),
                    waits=[(Cs, c_rep), (Ts, Ts.n)], inc=(V, 1))
                v_pg = V.n
                pa = pat[g % 2]
                for qt in range(ND):
                    w = [(V, v_pg)] if qt == 0 else []
                    plan.emit("tensor", lambda e, g=g, qt=qt, pa=pa: e.matmul(
                        pa[:], redb[:, qt * 16:(qt + 1) * 16],
                        pgb[:, qt * TC:(qt + 1) * TC],
                        start=(qt == 0), stop=(qt == ND - 1)),
                        waits=w, inc=((Ts, 1) if qt == ND - 1 else None))
                plan.emit("vector", lambda e, g=g, pa=pa: e.tensor_copy(
                    attb[:, g * TC:(g + 1) * TC], pa[:]),
                    waits=[(Ts, Ts.n)], inc=(V, 1))
            v_att = V.n
            if upto == "att":
                if ch == 0:
                    plan.dma("gpsimd", x3_o[0:16, 0:D], attb[:, 0:D], St,
                             waits=[(V, v_att)])
                continue
            plan.emit("vector", lambda e: e.tensor_copy(rmaxb[:], attb[:, 0:TC]),
                      waits=[(V, v_att)], inc=(V, 1))
            for g in range(1, 16):
                plan.emit("vector", lambda e, g=g: e.tensor_tensor(
                    rmaxb[:], rmaxb[:], attb[:, g * TC:(g + 1) * TC], ALU.max),
                    inc=(V, 1))
            for g in range(16):
                plan.emit("vector", lambda e, g=g: e.tensor_tensor(
                    attb[:, g * TC:(g + 1) * TC], attb[:, g * TC:(g + 1) * TC],
                    rmaxb[:], ALU.subtract), inc=(V, 1))
            v_sub = V.n
            plan.emit("scalar", lambda e: e.activation(attb[:], attb[:], AF.Exp,
                                                       scale=0.125),
                      waits=[(V, v_sub)], inc=(Cs, 1))
            c_exp = Cs.n
            plan.emit("vector", lambda e: e.tensor_copy(rsumb[:], attb[:, 0:TC]),
                      waits=[(Cs, c_exp)], inc=(V, 1))
            for g in range(1, 16):
                plan.emit("vector", lambda e, g=g: e.tensor_tensor(
                    rsumb[:], rsumb[:], attb[:, g * TC:(g + 1) * TC], ALU.add),
                    inc=(V, 1))
            plan.emit("vector", lambda e: e.reciprocal(rsumb[:], rsumb[:]), inc=(V, 1))
            for g in range(16):
                plan.emit("vector", lambda e, g=g: e.tensor_tensor(
                    probsb[:, g * TC:(g + 1) * TC], attb[:, g * TC:(g + 1) * TC],
                    rsumb[:], ALU.mult), inc=(V, 1))
            v_probs = V.n
            if upto == "probs":
                if ch == 0:
                    plan.dma("gpsimd", x3_o[0:16, 0:D],
                             probsb[:, 0:D], St, waits=[(V, v_probs)])
                continue
            for qt in range(ND):
                po = pou[qt % 2]
                for g in range(16):
                    ra = arep[g % 2]
                    rm = mbuf[g % 2]
                    pr = prep[g % 2]
                    plan.emit("tensor", lambda e, g=g, qt=qt, pr=pr: e.matmul(
                        pr[:], amb[:, qt * 128:(qt + 1) * 128],
                        probsb[:, g * TC:(g + 1) * TC], start=True, stop=True),
                        waits=[(V, max(v_probs, V.n)), (Cs, Cs.n)], inc=(Ts, 1))
                    plan.emit("scalar", lambda e, pr=pr, ra=ra: e.copy(ra[:], pr[:]),
                              waits=[(Ts, Ts.n)], inc=(Cs, 1))
                    plan.emit("vector", lambda e, g=g, ra=ra, rm=rm: e.tensor_tensor(
                        rm[:], ra[:], vrep[:, g * TC:(g + 1) * TC], ALU.mult),
                        waits=[(Cs, Cs.n), (Ts, Ts.n)], inc=(V, 1))
                    plan.emit("tensor", lambda e, rm=rm, po=po, g=g: e.matmul(
                        po[:], identb[:], rm[:], start=(g == 0), stop=(g == 15)),
                        waits=[(V, V.n)], inc=(Ts, 1))
                plan.emit("scalar", lambda e, qt=qt, po=po: e.copy(
                    aoutT[:, qt * TC:(qt + 1) * TC], po[:]),
                    waits=[(Ts, Ts.n)], inc=(Cs, 1))
            c_aout = Cs.n
            if upto == "aout":
                if ch == 0:
                    plan.dma("gpsimd", x3_o[0:128, 0:D], aoutT[:, 0:D], St,
                             waits=[(Cs, c_aout)])
                continue
            for tl in range(NTT):
                for dc in range(2):
                    pm = pq[(tl * 2 + dc) % 2]
                    for kt in range(ND):
                        w = [(Cs, c_aout), (V, V.n)] if kt == 0 else []
                        plan.emit("tensor", lambda e, tl=tl, dc=dc, kt=kt, pm=pm: e.matmul(
                            pm[:], aoutT[:, kt * TC + tl * 128: kt * TC + tl * 128 + 128],
                            wob[:, kt * D + dc * 512: kt * D + (dc + 1) * 512],
                            start=(kt == 0), stop=(kt == ND - 1)),
                            waits=w, inc=((Ts, 1) if kt == ND - 1 else None))
                    plan.emit("vector", lambda e, tl=tl, dc=dc, pm=pm: e.tensor_tensor(
                        aoc[:, tl * D + dc * 512: tl * D + (dc + 1) * 512], pm[:],
                        borep[:, dc * 512:(dc + 1) * 512], ALU.add),
                        waits=[(Ts, Ts.n), (Cs, Cs.n)], inc=(V, 1))
            v_ao = V.n
            tt0 = ch * NTT
            lxc = {}
            for tt in range(NTT):
                xb = xts[tt % 2]
                plan.dma("sync", xb[:], x2_in[(tt0 + tt) * 128:(tt0 + tt + 1) * 128, :],
                         L, waits=[(V, V.n), (St, St.n), (Cs, Cs.n)])
                lxc[tt] = L.n

            def upd_x3(tt, mc, rc, tt0=tt0, lxc=lxc):
                xb = xts[tt % 2]
                plan.emit("vector", lambda e, tt=tt, mc=mc, rc=rc: e.tensor_scalar(
                    dscr[:], aoc[:, tt * D:(tt + 1) * D], mc, rc,
                    ALU.subtract, ALU.mult), waits=[(V, V.n)], inc=(V, 1))
                plan.emit("vector", lambda e, tt=tt, xb=xb: e.tensor_tensor(
                    xb[:], dscr[:], xb[:], ALU.add),
                    waits=[(L, lxc[tt])], inc=(V, 1))
                plan.dma("gpsimd", x3_o[(tt0 + tt) * 128:(tt0 + tt + 1) * 128, :],
                         xb[:], St, waits=[(V, V.n)])
            ln_tiles(lambda tt: aoc[:, tt * D:(tt + 1) * D], NTT, v_ao, upd_x3)
        plan.emit("gpsimd", lambda e: e.engine_nop(), waits=[(St, St.n)])
        if F is None:
            plan.run()
    return nc


def host_attn_inputs(inp, x2_list):
    ident = np.eye(128, dtype=np.float32)
    NDg = D // 128
    lay = lambda v, n: np.ascontiguousarray(v.reshape(n, 128).T)
    inv = 1.0 / 10000.0 ** (np.arange(0, DH, 2, dtype=np.float32) / DH)
    selk = np.zeros((128, 128), np.float32)
    selv = np.zeros((128, 128), np.float32)
    for mm in range(128):
        selk[mm % 64, mm] = 1.0
        selv[64 + mm % 64, mm] = 1.0
    red = np.zeros((128, 8 * 16), np.float32)
    for qt in range(8):
        for j in range(128):
            red[j, qt * 16 + 2 * qt + j // 64] = 1.0
    am = np.zeros((16, 8 * 128), np.float32)
    for qt in range(8):
        for j in range(128):
            am[2 * qt + j // 64, qt * 128 + j] = 1.0
    maps = []
    for c in range(NCORE):
        b, m = c // 2, c % 2
        t0 = m * TL
        t = np.arange(t0, t0 + TL, dtype=np.float32)
        fr = t[None, :] * inv[:, None]
        cos = np.cos(fr).astype(np.float32)
        sin = np.sin(fr).astype(np.float32)
        cosr = np.ascontiguousarray(np.tile(cos, (4, 1)))
        sinr = np.ascontiguousarray(np.concatenate([-sin, sin, -sin, sin], 0))
        maps.append({
            "x2_in": np.ascontiguousarray(x2_list[c]),
            "wq_in": inp["L1_attn_Wq"],
            "bq_in": lay(inp["L1_attn_bq"], NDg),
            "wd_in": inp["L1_attn_Wdown"],
            "bd_in": lay(inp["L1_attn_bdown"], 2),
            "wu_in": inp["L1_attn_Wup"],
            "bu_in": lay(inp["L1_attn_bup"], DI // 128),
            "wo_in": inp["L1_attn_Wout"],
            "bo_in": inp["L1_attn_bout"].reshape(1, D),
            "cosr_in": cosr, "sinr_in": sinr,
            "selk_in": selk, "selv_in": selv, "red_in": red, "am_in": am,
            "ident": ident,
        })
    return maps


def build_ffn(upto="full", BARRIER=False, F=None):
    nc = bass.Bass("TRN2", target_bir_lowering=False, debug=False) if F is None else F["nc"]
    NT = TL // 128
    ND = D // 128
    NF = FF // 128
    TCH = 512
    NCH = TL // TCH
    NTT = TCH // 128

    if F is None:
        x3_in = nc.dram_tensor("x3_in", [TL, D], F32, kind="ExternalInput").ap()
        w1T_in = nc.dram_tensor("w1T_in", [D, FF], F32, kind="ExternalInput").ap()
        w2T_in = nc.dram_tensor("w2T_in", [D, FF], F32, kind="ExternalInput").ap()
        h1T_in = nc.dram_tensor("h1T_in", [FF, FF], F32, kind="ExternalInput").ap()
        h2T_in = nc.dram_tensor("h2T_in", [FF, FF], F32, kind="ExternalInput").ap()
        w3T_in = nc.dram_tensor("w3T_in", [FF, D], F32, kind="ExternalInput").ap()
        s1_in = nc.dram_tensor("s1_in", [128, NF], F32, kind="ExternalInput").ap()
        s2_in = nc.dram_tensor("s2_in", [128, NF], F32, kind="ExternalInput").ap()
        h1s_in = nc.dram_tensor("h1s_in", [128, NF], F32, kind="ExternalInput").ap()
        h2s_in = nc.dram_tensor("h2s_in", [128, NF], F32, kind="ExternalInput").ap()
        w3s_in = nc.dram_tensor("w3s_in", [1, D], F32, kind="ExternalInput").ap()
        ident_in = nc.dram_tensor("ident", [128, 128], F32, kind="ExternalInput").ap()
        x4_o = nc.dram_tensor("x4_o", [TL, D], F32, kind="ExternalOutput").ap()
    else:
        io = F["io"]
        x3_in, w1T_in, w2T_in, h1T_in = io["x3_in"], io["w1T_in"], io["w2T_in"], io["h1T_in"]
        h2T_in, w3T_in, s1_in, s2_in = io["h2T_in"], io["w3T_in"], io["s1_in"], io["s2_in"]
        h1s_in, h2s_in, w3s_in = io["h1s_in"], io["h2s_in"], io["w3s_in"]
        ident_in, x4_o = io["ident"], io["x4_o"]
    pfx = "" if F is None else F["pfx"]

    ctx = contextlib.ExitStack()
    with ctx:
        sb = lambda name, shape, dt: ctx.enter_context(nc.sbuf_tensor(pfx + name, shape, dt))
        ps = lambda name, shape, dt=F32: ctx.enter_context(nc.psum_tensor(pfx + name, shape, dt))

        identb = sb("identb", [128, 128], BF16)
        identf = sb("identf", [128, 128], F32)
        ones1 = sb("ones1", [1, 128], F32)
        sqs = sb("sqs", [128, D], F32)
        mcol = sb("mcol", [128, 4], F32)
        rcol = sb("rcol", [128, 4], F32)
        scol = sb("scol", [128, 4], F32)
        dscr = sb("dscr", [128, D], F32)
        xts = [sb(f"xts{i}", [128, D], F32) for i in range(2)]
        u4t = sb("u4t", [128, D], BF16)
        u4T = sb("u4T", [128, ND * TCH], BF16)
        abuf = sb("abuf", [128, NF * TCH], BF16)
        hbuf = sb("hbuf", [128, NF * TCH], BF16)
        wga = [sb(f"wga{i}", [128, 8 * 1024], BF16) for i in range(2)]
        sa = sb("sa", [128, TCH], F32)
        sbb = sb("sbb", [128, TCH], F32)
        s1c = sb("s1c", [128, NF], F32)
        s2c = sb("s2c", [128, NF], F32)
        h1c = sb("h1c", [128, NF], F32)
        h2c = sb("h2c", [128, NF], F32)
        w3rep = sb("w3rep", [128, D], F32)
        rowt = sb("rowt", [1, D], F32)
        ffc = sb("ffc", [128, NTT * D], F32)

        pff = [ps(f"pff{i}", [128, 512]) for i in range(2)]
        pmm = [ps(f"pmm{i}", [128, 512]) for i in range(4)]
        ptb = [ctx.enter_context(nc.psum_tensor(pfx + f"ptb{i}", [128, 128], BF16))
               for i in range(2)]

        plan = Plan(nc, ctx) if F is None else F["plan"]
        L = plan.sem("L")
        Lg = plan.sem("Lg")
        V = plan.sem("V")
        Cs = plan.sem("C")
        Ts = plan.sem("T")
        St = plan.sem("St")

        plan.dma("sync", s1c[:], s1_in, L)
        plan.dma("sync", s2c[:], s2_in, L)
        plan.dma("sync", h1c[:], h1s_in, L)
        plan.dma("sync", h2c[:], h2s_in, L)
        plan.dma("sync", rowt[:], w3s_in, L)
        plan.dma("sync", identf[:], ident_in, L)
        lds = L.n

        plan.emit("vector", lambda e: e.memset(ones1[:], 1.0), inc=(V, 1))
        plan.emit("vector", lambda e: e.tensor_copy(identb[:], identf[:]),
                  waits=[(L, lds)], inc=(V, 1))
        for dc in range(2):
            plan.emit("tensor", lambda e, dc=dc: e.matmul(
                pff[0][:], ones1[:], rowt[0:1, dc * 512:(dc + 1) * 512],
                start=True, stop=True),
                waits=[(L, lds), (V, V.n)], inc=(Ts, 1))
            plan.emit("vector", lambda e, dc=dc: e.tensor_copy(
                w3rep[:, dc * 512:(dc + 1) * 512], pff[0][:]),
                waits=[(Ts, Ts.n)], inc=(V, 1))

        wlast = {0: None, 1: None}
        nch_run = 1 if upto.startswith("one_") else (2 if upto.startswith("two_") else NCH)
        upto = upto.removeprefix("one_").removeprefix("two_")

        def stream(dram_ap, cols, bi, extra=()):
            if wlast[bi] is None:
                wt = list(extra)
            else:
                wt = [(Ts, wlast[bi])] + list(extra)
            kk = dram_ap.shape[0] // 128
            plan.dma("gpsimd", wga[bi][:, 0:cols].rearrange("p (k f) -> p k f", k=kk),
                     dram_ap.rearrange("(k p) f -> p k f", p=128), Lg, waits=wt)
            return Lg.n

        for ch in range(nch_run):
            tt0 = ch * NTT
            if ch > 0 and BARRIER:
                for eng in ("vector", "scalar", "tensor", "gpsimd", "sync"):
                    plan.emit(eng, lambda e: None,
                              waits=[(L, L.n), (V, V.n), (Cs, Cs.n), (Ts, Ts.n), (St, St.n)])
            tpu = {}
            for tt in range(NTT):
                xb = xts[tt % 2]
                wt = [(Ts, tpu[tt - 2])] if (tt - 2) in tpu else [(Ts, Ts.n)]
                wt = wt + [(St, St.n), (V, V.n), (Cs, Cs.n)]
                plan.dma("sync", xb[:], x3_in[(tt0 + tt) * 128:(tt0 + tt + 1) * 128, :],
                         L, waits=wt)
                lx = L.n
                plan.emit("scalar", lambda e, xb=xb: e.activation(sqs[:], xb[:], AF.Square),
                          waits=[(L, lx), (V, V.n)], inc=(Cs, 1))
                plan.emit("vector", lambda e, xb=xb: e.tensor_reduce(
                    mcol[:, 0:1], xb[:], AX.X, ALU.add), waits=[(L, lx)], inc=(V, 1))
                plan.emit("vector", lambda e: e.tensor_reduce(
                    scol[:, 0:1], sqs[:], AX.X, ALU.add), waits=[(Cs, Cs.n)], inc=(V, 1))
                plan.emit("vector", lambda e: e.tensor_scalar(
                    mcol[:, 0:1], mcol[:, 0:1], 1.0 / D, None, ALU.mult), inc=(V, 1))
                plan.emit("vector", lambda e: e.tensor_scalar(
                    scol[:, 0:1], scol[:, 0:1], 1.0 / D, None, ALU.mult), inc=(V, 1))
                plan.emit("vector", lambda e: e.scalar_tensor_tensor(
                    rcol[:, 0:1], mcol[:, 0:1], -1.0, mcol[:, 0:1], ALU.mult, ALU.mult),
                    inc=(V, 1))
                plan.emit("vector", lambda e: e.tensor_tensor(
                    scol[:, 0:1], scol[:, 0:1], rcol[:, 0:1], ALU.add), inc=(V, 1))
                plan.emit("vector", lambda e: e.tensor_scalar(
                    scol[:, 0:1], scol[:, 0:1], LN_EPS, None, ALU.add), inc=(V, 1))
                vv = V.n
                plan.emit("scalar", lambda e: e.activation(
                    rcol[:, 0:1], scol[:, 0:1], AF.Sqrt), waits=[(V, vv)], inc=(Cs, 1))
                plan.emit("vector", lambda e: e.reciprocal(
                    rcol[:, 0:1], rcol[:, 0:1]), waits=[(Cs, Cs.n)], inc=(V, 1))
                plan.emit("vector", lambda e, xb=xb: e.tensor_scalar(
                    u4t[:], xb[:], mcol[:, 0:1], rcol[:, 0:1], ALU.subtract, ALU.mult),
                    waits=[(Ts, Ts.n)], inc=(V, 1))
                v_u4 = V.n
                for dt in range(ND):
                    pbu = (tt * ND + dt) % 2
                    w = [(V, v_u4)]
                    if tt * ND + dt >= 2:
                        w.append((Cs, Cs.n))
                    plan.emit("tensor", lambda e, dt=dt, pbu=pbu: e.transpose(
                        ptb[pbu][:], u4t[:, dt * 128:(dt + 1) * 128], identb[:]),
                        waits=w, inc=(Ts, 1))
                    plan.emit("scalar", lambda e, tt=tt, dt=dt, pbu=pbu: e.copy(
                        u4T[:, dt * TCH + tt * 128: dt * TCH + (tt + 1) * 128],
                        ptb[pbu][:]), waits=[(Ts, Ts.n)], inc=(Cs, 1))
                tpu[tt] = Ts.n
            c_u4T = Cs.n
            for fg in range(NF // 8):
                lw1 = stream(w1T_in[:, fg * 1024:(fg + 1) * 1024], 8 * 1024, 0,
                             extra=([(V, V.n), (Cs, Cs.n)] if wlast[0] is None else []))
                lw2 = stream(w2T_in[:, fg * 1024:(fg + 1) * 1024], 8 * 1024, 1,
                             extra=([(V, V.n), (Cs, Cs.n)] if wlast[1] is None else []))
                for fi in range(8):
                    ft = fg * 8 + fi
                    for kt in range(ND):
                        w = [(Lg, lw1), (Cs, Cs.n), (V, V.n)] if kt == 0 else []
                        plan.emit("tensor", lambda e, fi=fi, kt=kt: e.matmul(
                            pff[0][:], wga[0][:, kt * 1024 + fi * 128: kt * 1024 + fi * 128 + 128],
                            u4T[:, kt * TCH:(kt + 1) * TCH],
                            start=(kt == 0), stop=(kt == ND - 1)),
                            waits=w, inc=((Ts, 1) if kt == ND - 1 else None))
                    for kt in range(ND):
                        w = [(Lg, lw2), (Cs, Cs.n)] if kt == 0 else []
                        plan.emit("tensor", lambda e, fi=fi, kt=kt: e.matmul(
                            pff[1][:], wga[1][:, kt * 1024 + fi * 128: kt * 1024 + fi * 128 + 128],
                            u4T[:, kt * TCH:(kt + 1) * TCH],
                            start=(kt == 0), stop=(kt == ND - 1)),
                            waits=w, inc=((Ts, 1) if kt == ND - 1 else None))
                    wlast[0] = wlast[1] = Ts.n
                    plan.emit("scalar", lambda e, ft=ft: e.activation(
                        sa[:], pff[0][:], AF.Silu, scale=s1c[:, ft:ft + 1]),
                        waits=[(Ts, Ts.n), (V, V.n)], inc=(Cs, 1))
                    plan.emit("scalar", lambda e, ft=ft: e.activation(
                        sbb[:], pff[1][:], AF.Identity, scale=s2c[:, ft:ft + 1]),
                        inc=(Cs, 1))
                    plan.emit("vector", lambda e, ft=ft: e.tensor_tensor(
                        abuf[:, ft * TCH:(ft + 1) * TCH], sa[:], sbb[:], ALU.mult),
                        waits=[(Cs, Cs.n), (Ts, Ts.n)], inc=(V, 1))
            v_a = V.n
            if upto == "a":
                if ch == nch_run - 1:
                    plan.dma("gpsimd", x4_o[0:128, 0:D], abuf[:, 0:D], St,
                             waits=[(V, v_a)])
                continue
            stages = ((abuf, hbuf, h1c, h1T_in), (hbuf, abuf, h2c, h2T_in))
            if upto == "h1":
                stages = stages[:1]
            for stage, (src_t, dst_t, scl, win) in enumerate(stages):
                for fg in range(NF // 2):
                    lw = stream(win[:, fg * 256:(fg + 1) * 256], 32 * 256, fg % 2,
                                extra=[(V, V.n)] if stage == 0 and fg < 2 else [])
                    for fi in range(2):
                        ft = fg * 2 + fi
                        pf = pff[fi % 2]
                        for kt in range(NF):
                            w = [(Lg, lw), (Cs, Cs.n)] if kt == 0 else []
                            plan.emit("tensor", lambda e, fi=fi, kt=kt, pf=pf,
                                      bi=fg % 2, src_t=src_t: e.matmul(
                                pf[:], wga[bi][:, kt * 256 + fi * 128: kt * 256 + fi * 128 + 128],
                                src_t[:, kt * TCH:(kt + 1) * TCH],
                                start=(kt == 0), stop=(kt == NF - 1)),
                                waits=w, inc=((Ts, 1) if (kt == NF - 1) else None))
                        plan.emit("scalar", lambda e, ft=ft, pf=pf, scl=scl,
                                  dst_t=dst_t: e.activation(
                            dst_t[:, ft * TCH:(ft + 1) * TCH], pf[:], AF.Gelu,
                            scale=scl[:, ft:ft + 1]),
                            waits=[(Ts, Ts.n)], inc=(Cs, 1))
                    wlast[fg % 2] = Ts.n
            c_h2 = Cs.n
            if upto in ("h1", "h2"):
                if ch == nch_run - 1:
                    srcb = hbuf if upto == "h1" else abuf
                    plan.dma("gpsimd", x4_o[0:128, 0:D], srcb[:, 0:D], St,
                             waits=[(Cs, c_h2)])
                    if upto == "h2":
                        plan.dma("gpsimd", x4_o[128:256, 0:D], hbuf[:, 0:D], St,
                                 waits=[(Cs, c_h2)])
                continue
            for dc in range(2):
                for kg in range(NF // 8):
                    bi = kg % 2
                    wt = [(Ts, wlast[bi])]
                    kk = 8
                    plan.dma("gpsimd",
                             wga[bi][:, 0:8 * 512].rearrange("p (k f) -> p k f", k=8),
                             w3T_in[kg * 1024:(kg + 1) * 1024, dc * 512:(dc + 1) * 512]
                             .rearrange("(k p) f -> p k f", p=128), Lg, waits=wt)
                    lw = Lg.n
                    for ki in range(8):
                        kt = kg * 8 + ki
                        for tl in range(NTT):
                            w = [(Lg, lw)] if tl == 0 else []
                            if kt == 0 and tl == 0:
                                w.append((V, V.n))
                                w.append((Cs, c_h2))
                            plan.emit("tensor", lambda e, ki=ki, kt=kt, tl=tl, bi=bi:
                                      e.matmul(
                                pmm[tl][:], abuf[:, kt * TCH + tl * 128: kt * TCH + tl * 128 + 128],
                                wga[bi][:, ki * 512:(ki + 1) * 512],
                                start=(kt == 0), stop=(kt == NF - 1)),
                                waits=w,
                                inc=((Ts, 1) if (kt == NF - 1 or
                                                 (ki == 7 and tl == NTT - 1)) else None))
                    wlast[bi] = Ts.n
                t_e = Ts.n
                for tl in range(NTT):
                    plan.emit("vector", lambda e, tl=tl, dc=dc: e.tensor_tensor(
                        ffc[:, tl * D + dc * 512: tl * D + (dc + 1) * 512], pmm[tl][:],
                        w3rep[:, dc * 512:(dc + 1) * 512], ALU.mult),
                        waits=[(Ts, t_e), (Cs, Cs.n), (St, St.n)], inc=(V, 1))
            v_ff = V.n
            if upto == "stats":
                if ch == nch_run - 1:
                    plan.dma("gpsimd", x4_o[0:128, 0:D], ffc[:, 0:D], St,
                             waits=[(V, v_ff)])
                    plan.dma("gpsimd", x4_o[128:256, 0:D], ffc[:, D:2 * D], St,
                             waits=[(V, v_ff)])
                continue
            if upto == "ff":
                if ch == nch_run - 1:
                    plan.dma("gpsimd", x4_o[0:128, 0:D], ffc[:, 0:D], St,
                             waits=[(V, v_ff)])
                continue
            def fin(tt, mc, rc, tt0=tt0):
                xb = xts[tt % 2]
                plan.dma("sync", xb[:], x3_in[(tt0 + tt) * 128:(tt0 + tt + 1) * 128, :],
                         L, waits=[(V, V.n), (St, St.n)])
                lx = L.n
                plan.emit("vector", lambda e, tt=tt, mc=mc, rc=rc: e.tensor_scalar(
                    dscr[:], ffc[:, tt * D:(tt + 1) * D], mc, rc,
                    ALU.subtract, ALU.mult), waits=[(V, V.n)], inc=(V, 1))
                plan.emit("vector", lambda e, tt=tt, xb=xb: e.tensor_tensor(
                    xb[:], dscr[:], xb[:], ALU.add),
                    waits=[(L, lx)], inc=(V, 1))
                plan.dma("gpsimd", x4_o[(tt0 + tt) * 128:(tt0 + tt + 1) * 128, :],
                         xb[:], St, waits=[(V, V.n)])

            for tt in range(NTT):
                src = ffc[:, tt * D:(tt + 1) * D]
                plan.emit("scalar", lambda e, s=src: e.activation(sqs[:], s, AF.Square),
                          waits=[(V, max(v_ff, V.n))], inc=(Cs, 1))
                plan.emit("vector", lambda e, s=src, o=mcol[:, tt:tt + 1]:
                          e.tensor_reduce(o, s, AX.X, ALU.add),
                          waits=[(V, v_ff)], inc=(V, 1))
                plan.emit("vector", lambda e, o=scol[:, tt:tt + 1]:
                          e.tensor_reduce(o, sqs[:], AX.X, ALU.add),
                          waits=[(Cs, Cs.n)], inc=(V, 1))
            plan.emit("vector", lambda e: e.tensor_scalar(
                mcol[:, 0:NTT], mcol[:, 0:NTT], 1.0 / D, None, ALU.mult),
                waits=[(V, V.n)], inc=(V, 1))
            plan.emit("vector", lambda e: e.tensor_scalar(
                scol[:, 0:NTT], scol[:, 0:NTT], 1.0 / D, None, ALU.mult),
                waits=[(V, V.n)], inc=(V, 1))
            plan.emit("vector", lambda e: e.scalar_tensor_tensor(
                rcol[:, 0:NTT], mcol[:, 0:NTT], -1.0, mcol[:, 0:NTT], ALU.mult, ALU.mult),
                waits=[(V, V.n)], inc=(V, 1))
            plan.emit("vector", lambda e: e.tensor_tensor(
                scol[:, 0:NTT], scol[:, 0:NTT], rcol[:, 0:NTT], ALU.add),
                waits=[(V, V.n)], inc=(V, 1))
            plan.emit("vector", lambda e: e.tensor_scalar(
                scol[:, 0:NTT], scol[:, 0:NTT], LN_EPS, None, ALU.add),
                waits=[(V, V.n)], inc=(V, 1))
            vv = V.n
            plan.emit("scalar", lambda e: e.activation(
                rcol[:, 0:NTT], scol[:, 0:NTT], AF.Sqrt), waits=[(V, vv)], inc=(Cs, 1))
            plan.emit("vector", lambda e: e.reciprocal(
                rcol[:, 0:NTT], rcol[:, 0:NTT]), waits=[(Cs, Cs.n)], inc=(V, 1))
            for tt in range(NTT):
                fin(tt, mcol[:, tt:tt + 1], rcol[:, tt:tt + 1])
        plan.emit("gpsimd", lambda e: e.engine_nop(), waits=[(St, St.n)])
        if F is None:
            plan.run()
    return nc


def host_ffn_inputs(inp, x3_list):
    ident = np.eye(128, dtype=np.float32)
    layc = lambda v: np.ascontiguousarray(v.reshape(-1, 128).T)
    tr = lambda w: np.ascontiguousarray(w.T)
    maps = []
    for c in range(NCORE):
        maps.append({
            "x3_in": np.ascontiguousarray(x3_list[c]),
            "w1T_in": tr(inp["L1_ffn_w1_w"]),
            "w2T_in": tr(inp["L1_ffn_w2_w"]),
            "h1T_in": tr(inp["L1_ffn_h1_w"]),
            "h2T_in": tr(inp["L1_ffn_h2_w"]),
            "w3T_in": tr(inp["L1_ffn_w3_w"]),
            "s1_in": layc(inp["L1_ffn_w1_s"]),
            "s2_in": layc(inp["L1_ffn_w2_s"]),
            "h1s_in": layc(inp["L1_ffn_h1_s"]),
            "h2s_in": layc(inp["L1_ffn_h2_s"]),
            "w3s_in": inp["L1_ffn_w3_s"].reshape(1, D),
            "ident": ident,
        })
    return maps


def build_bae():
    nc = bass.Bass("TRN2", target_bir_lowering=False, debug=False)
    NF = FF // 128
    ND = D // 128
    I = lambda n, s, dt=F32: nc.dram_tensor(n, s, dt, kind="ExternalInput").ap()
    io = {
        "x_in": I("x_in", [TL, D]),
        "ybp_in": I("ybp_in", [DI, TL], BF16),
        "zs_in": I("zs_in", [DI, TL], BF16),
        "wout_in": I("wout_in", [DI, D]),
        "w1_in": I("w1_in", [D, FF]),
        "b1_in": I("b1_in", [128, NF]),
        "w2_in": I("w2_in", [FF, D]),
        "b2_in": I("b2_in", [1, D]),
        "gam_in": I("gam_in", [1, D]),
        "bet_in": I("bet_in", [1, D]),
        "ident": I("ident", [128, 128]),
        "wq_in": I("wq_in", [D, D]),
        "bq_in": I("bq_in", [128, ND]),
        "wd_in": I("wd_in", [D, 256]),
        "bd_in": I("bd_in", [128, 2]),
        "wu_in": I("wu_in", [256, DI]),
        "bu_in": I("bu_in", [128, DI // 128]),
        "wo_in": I("wo_in", [D, D]),
        "bo_in": I("bo_in", [1, D]),
        "cosr_in": I("cosr_in", [128, TL]),
        "sinr_in": I("sinr_in", [128, TL]),
        "selk_in": I("selk_in", [128, 128]),
        "selv_in": I("selv_in", [128, 128]),
        "red_in": I("red_in", [128, 8 * 16]),
        "am_in": I("am_in", [16, 8 * 128]),
        "w1T_in": I("w1T_in", [D, FF]),
        "w2T_in": I("w2T_in", [D, FF]),
        "h1T_in": I("h1T_in", [FF, FF]),
        "h2T_in": I("h2T_in", [FF, FF]),
        "w3T_in": I("w3T_in", [FF, D]),
        "s1_in": I("s1_in", [128, NF]),
        "s2_in": I("s2_in", [128, NF]),
        "h1s_in": I("h1s_in", [128, NF]),
        "h2s_in": I("h2s_in", [128, NF]),
        "w3s_in": I("w3s_in", [1, D]),
    }
    io["x2_in"] = io["x2_o"] = nc.dram_tensor("x2_h", [TL, D], F32, kind="Internal").ap()
    io["x3_in"] = io["x3_o"] = nc.dram_tensor("x3_h", [TL, D], F32, kind="Internal").ap()
    io["x4_o"] = nc.dram_tensor("x4_o", [TL, D], F32, kind="ExternalOutput").ap()
    ctx = contextlib.ExitStack()
    with ctx:
        plan = Plan(nc, ctx)
        build_back(F={"nc": nc, "plan": plan, "io": io, "pfx": "bk_"})
        plan.barrier()
        build_attn(F={"nc": nc, "plan": plan, "io": io, "pfx": "at_"})
        plan.barrier()
        build_ffn(F={"nc": nc, "plan": plan, "io": io, "pfx": "ff_"})
        plan.run()
    return nc


def host_bae_inputs(inp, front_results, scan_results):
    mb = host_back_inputs(inp, front_results, scan_results)
    dummy = [np.zeros((1, 1), np.float32)] * NCORE
    ma = host_attn_inputs(inp, dummy)
    mf = host_ffn_inputs(inp, dummy)
    maps = []
    for c in range(NCORE):
        m = {**mb[c], **ma[c], **mf[c]}
        m.pop("x2_in")
        m.pop("x3_in")
        maps.append(m)
    return maps


_NC_CACHE = {}


def _get(name, builder):
    if name not in _NC_CACHE:
        _NC_CACHE[name] = builder()
    return _NC_CACHE[name]


def kernel(**inputs):
    cores = list(range(NCORE))
    inp = inputs

    nc1 = _get("front", build_front)
    r1 = run_bass_kernel_spmd(nc1, host_front_inputs(inp), core_ids=cores).results

    nc2 = _get("scan", build_scan)
    r2 = run_bass_kernel_spmd(nc2, host_scan_inputs(inp, r1), core_ids=cores).results
    ybar = [r2[c]["ybar_o"] for c in range(NCORE)]

    nc3 = _get("bae", build_bae)
    r3 = run_bass_kernel_spmd(nc3, host_bae_inputs(inp, r1, ybar), core_ids=cores).results

    out = np.empty((B, T, D), np.float32)
    for c in range(NCORE):
        b, m = c // 2, c % 2
        out[b, m * TL:(m + 1) * TL] = r3[c]["x4_o"]
    return out
